# version 01
# baseline (speedup 1.0000x reference)
import numpy as np
import ml_dtypes
from contextlib import ExitStack

import concourse.bass as bass
import concourse.mybir as mybir
from concourse.bass_utils import run_bass_kernel_spmd

F32 = mybir.dt.float32
BF16 = mybir.dt.bfloat16
AF = mybir.ActivationFunctionType
ALU = mybir.AluOpType
AX = mybir.AxisListType

NCORES = 8
T = 4096
SEQ = 2048
D = 1024
DT = 8
CI = 2048
CIT = 16
NE = 8
TCH = 512
NCHK = T // TCH

SUBS = [0, 1, 2, 3, 4, 5]
M3LVL = 3
M3N = 16
MODE = "fused"

BF = ml_dtypes.bfloat16

_UQ = [0]


def U(nm):
    _UQ[0] += 1
    return f"{nm}_u{_UQ[0]}"


C_ONES, C_MHALF, C_EPS5, C_MONE = 0, 1, 2, 3
C_1702 = 20
C_IOTA, C_IOTA1K = 4, 12
C_A = 32
C_D = 288
C_CONVW = 304
C_CONVB = 368
C_IPB = 384
C_DTB = 416
C_OPB = 432
C_NW = 440
C_NB = 448
C_B1M = 456
C_L1B1 = 712
C_L1B2 = 744
C_XPB = 752
C_AL1 = 768
C_BJ1 = 800
C_AL2 = 832
C_BJ2 = 840

B_ID = 0
B_TRI = 128
B_PROJ = 256
B_SEL2 = 384
B_SEL8 = 640
B_ONESC = 1664
B_ONER = 1666


def _moe_perm():
    perm = np.empty(4096, np.int64)
    for k in range(16):
        p = np.arange(128 * k, 128 * (k + 1))
        perm[256 * k:256 * k + 128] = 2 * p
        perm[256 * k + 128:256 * k + 256] = 2 * p + 1
    return perm


def _hbuild_consts(w):
    cst = np.zeros((128, 1024), np.float32)
    cst[:, C_ONES] = 1.0
    cst[:, C_MHALF] = -0.5
    cst[:, C_EPS5] = 1e-5
    cst[:, C_MONE] = -1.0
    cst[:, C_1702] = 1.702
    cst[:, C_IOTA:C_IOTA + 8] = np.arange(8, dtype=np.float32)[None, :]
    cst[:, C_IOTA1K:C_IOTA1K + 8] = np.arange(8, dtype=np.float32)[None, :] + 1000.0
    cst[:, C_A:C_A + 256] = np.asarray(w["L0_attn_A_log"]).reshape(16, 128, 16).transpose(1, 0, 2).reshape(128, 256)
    cst[:, C_D:C_D + 16] = np.asarray(w["L0_attn_D_param"]).reshape(16, 128).T
    cst[:, C_CONVW:C_CONVW + 64] = np.asarray(w["L0_attn_conv_w"]).reshape(2048, 4).reshape(16, 128, 4).transpose(1, 0, 2).reshape(128, 64)
    cst[:, C_CONVB:C_CONVB + 16] = np.asarray(w["L0_attn_conv_b"]).reshape(16, 128).T
    cst[:, C_IPB:C_IPB + 32] = np.asarray(w["L0_attn_in_proj_b"]).reshape(32, 128).T
    cst[:, C_DTB:C_DTB + 16] = np.asarray(w["L0_attn_dt_proj_b"]).reshape(16, 128).T
    cst[:, C_OPB:C_OPB + 8] = np.asarray(w["L0_attn_out_proj_b"]).reshape(8, 128).T
    cst[:, C_NW:C_NW + 8] = np.asarray(w["L0_attn_norm_w"]).reshape(8, 128).T
    cst[:, C_NB:C_NB + 8] = np.asarray(w["L0_attn_norm_b"]).reshape(8, 128).T
    b1p = np.asarray(w["L0_ffn_dense_1_b"])[:, _moe_perm()]
    cst[:, C_B1M:C_B1M + 256] = b1p.reshape(8, 32, 128).transpose(2, 0, 1).reshape(128, 256)
    cst[:, C_L1B1:C_L1B1 + 32] = np.asarray(w["L1_ffn_b1"]).reshape(32, 128).T
    cst[:, C_L1B2:C_L1B2 + 8] = np.asarray(w["L1_ffn_b2"]).reshape(8, 128).T
    cst[0:32, C_XPB] = np.asarray(w["L0_attn_x_proj_b"])

    cstb = np.zeros((128, 2048), BF)
    cstb[:, B_ID:B_ID + 128] = np.eye(128, dtype=BF)
    cstb[:, B_TRI:B_TRI + 128] = (np.arange(128)[None, :] >= np.arange(128)[:, None]).astype(BF)
    for r in range(2):
        cstb[r, B_SEL2 + r * 128:B_SEL2 + (r + 1) * 128] = BF(1.0)
    for r in range(8):
        cstb[r, B_SEL8 + r * 128:B_SEL8 + (r + 1) * 128] = BF(1.0)
    cstb[:, B_ONESC] = BF(1.0)
    cstb[0, B_ONER:B_ONER + 128] = BF(1.0)
    return cst, cstb


def _host_prep(inp):
    w = inp
    perm = _moe_perm()
    d = {}
    d["cst"], d["cstb"] = _hbuild_consts(w)
    x = np.ascontiguousarray(np.asarray(inp["L0_attn_x"], np.float32))
    xs = []
    for c in range(NCORES):
        xc = x[2 * c:2 * c + 2].reshape(T, D).T
        xs.append(np.ascontiguousarray(xc).reshape(8, 128, T))
    d["_x_cores"] = xs
    w1t = np.ascontiguousarray(np.asarray(w["L0_ffn_dense_1_w"]).transpose(0, 2, 1))
    w1t = w1t[:, :, perm]
    d["moe_w1t"] = np.ascontiguousarray(w1t.reshape(8, 8, 128, 4096))
    d["moe_w2t"] = np.ascontiguousarray(np.asarray(w["L0_ffn_dense_2_w"]).transpose(0, 2, 1)).reshape(8, 16, 128, 1024)
    d["moe_b2"] = np.ascontiguousarray(np.asarray(w["L0_ffn_dense_2_b"]))
    d["gate_w"] = np.ascontiguousarray(np.asarray(w["L0_ffn_gate_w"])).reshape(8, 128, 8)
    d["ipjt"] = np.ascontiguousarray(np.asarray(w["L0_attn_in_proj_w"]).T).reshape(8, 128, 4096)
    d["xpjt"] = np.ascontiguousarray(np.asarray(w["L0_attn_x_proj_w"]).T).reshape(16, 128, 32)
    d["dtjt"] = np.ascontiguousarray(np.asarray(w["L0_attn_dt_proj_w"]).T).reshape(16, 128, 2048)
    d["opjt"] = np.ascontiguousarray(np.asarray(w["L0_attn_out_proj_w"]).T).reshape(16, 128, 1024)
    d["wq"] = np.ascontiguousarray(np.asarray(w["L1_attn_Wq"])).reshape(8, 128, 1024)
    d["wk"] = np.ascontiguousarray(np.asarray(w["L1_attn_Wk"])).reshape(8, 128, 1024)
    d["wv"] = np.ascontiguousarray(np.asarray(w["L1_attn_Wv"])).reshape(8, 128, 1024)
    d["wo"] = np.ascontiguousarray(np.asarray(w["L1_attn_Wo"])).reshape(8, 128, 1024)
    pb = np.zeros((128, 128), np.float32)
    pb[0:64, 0:64] = np.asarray(w["L1_attn_proj"])
    pb[64:128, 64:128] = np.asarray(w["L1_attn_proj"])
    d["projbd"] = pb
    d["w1t_l1"] = np.ascontiguousarray(np.asarray(w["L1_ffn_w1"]).T).reshape(8, 128, 4096)
    d["w2t_l1"] = np.ascontiguousarray(np.asarray(w["L1_ffn_w2"]).T).reshape(32, 128, 1024)
    sel32 = np.zeros((32, 4096), BF)
    for r in range(32):
        sel32[r, r * 128:(r + 1) * 128] = BF(1.0)
    d["sel32"] = sel32
    d["idf32"] = np.eye(128, dtype=np.float32)
    return d


class Sem:
    def __init__(self, h):
        self.h = h
        self.n = 0

    def inc(self, instr, k=1):
        self.n += k
        instr.then_inc(self.h, k)
        return instr


class Stream:

    def __init__(self, ctx, name, k, kind="hw"):
        self.sems = [ctx.S(f"{name}_{i}", kind) for i in range(k)]
        self.base = [s.n for s in self.sems]
        self.k = k
        self.kind = kind
        self.n = 0

    def put(self, q, out, in_):
        assert (type(q).__name__ == "BassGpSimd") == (self.kind == "gp"), self.kind
        j = self.n % self.k
        s = self.sems[j]
        if self.n >= self.k:
            q.wait_ge(s.h, self.base[j] + (self.n // self.k) * 16)
        s.inc(q.dma_start(out, in_), 16)
        self.n += 1
        return self.n - 1

    def wait_item(self, eng, i):
        j = i % self.k
        eng.wait_ge(self.sems[j].h, self.base[j] + (i // self.k + 1) * 16)

    def wait_upto(self, eng, i):
        for j in range(self.k):
            cnt = (i - j) // self.k + 1
            if cnt > 0:
                eng.wait_ge(self.sems[j].h, self.base[j] + cnt * 16)

    def wait_all(self, eng):
        if self.n > 0:
            self.wait_upto(eng, self.n - 1)


class Ctx:
    def __init__(self, nc, st):
        self.nc = nc
        self.st = st
        self._pools = {}
        self._phase_maps = {}

    def new_phase(self):
        self._phase_maps = {}

    def S(self, name, kind="cc"):
        pm = self._phase_maps.setdefault(kind, {})
        if name not in pm:
            pool = self._pools.setdefault(kind, [])
            i = len(pm)
            if i >= len(pool):
                pool.append(Sem(self.st.enter_context(self.nc.semaphore(f"sp_{kind}{i}"))))
            pm[name] = pool[i]
        return pm[name]

    def alloc_global(self):
        st, nc = self.st, self.nc
        self.cst = st.enter_context(nc.sbuf_tensor("sb_cst", [128, 1024], F32))
        self.cstb = st.enter_context(nc.sbuf_tensor("sb_cstb", [128, 2048], BF16))
        self.ta = st.enter_context(nc.sbuf_tensor("g_ta", [128, 512], F32))
        self.tb = st.enter_context(nc.sbuf_tensor("g_tb", [128, 512], F32))
        self.mT = st.enter_context(nc.sbuf_tensor("g_mT", [8, 4096], BF16))
        self.idf = st.enter_context(nc.sbuf_tensor("g_idf", [128, 128], F32))
        self.mba = st.enter_context(nc.sbuf_tensor("g_mba", [128, 512], BF16))
        self.mbb = st.enter_context(nc.sbuf_tensor("g_mbb", [128, 512], BF16))
        self.b2bf = st.enter_context(nc.sbuf_tensor("g_b2bf", [8, 1024], BF16))
        self.gwt = st.enter_context(nc.sbuf_tensor("g_gw", [128, 64], F32))
        self.srow = st.enter_context(nc.sbuf_tensor("g_srow", [8, 512], F32))
        self.p0r = st.enter_context(nc.sbuf_tensor("g_p0r", [1, 512], BF16))
        self.p1r = st.enter_context(nc.sbuf_tensor("g_p1r", [1, 512], BF16))
        self.lr = st.enter_context(nc.sbuf_tensor("g_lr", [1, 1536], F32))
        self.g8a = st.enter_context(nc.sbuf_tensor("g_g8a", [128, 64], F32))
        self.g8b = st.enter_context(nc.sbuf_tensor("g_g8b", [128, 8], BF16))



def col(t, c0, n=1):
    return t[:, c0:c0 + n]


def vsync(ctx, eng, instr, name="vsp"):
    s = ctx.S(name)
    s.inc(instr)
    eng.wait_ge(s.h, s.n)



def emit_const_load(ctx, dr):
    ctx.new_phase()
    nc = ctx.nc
    cs = Stream(ctx, "c_ld", 2, kind="gp")
    with nc.Block() as blk:
        def f_g(gp):
            cs.put(gp, ctx.cst[:], dr["cst"][:])
            cs.put(gp, ctx.cstb[:], dr["cstb"][:])
            cs.put(gp, ctx.idf[:], dr["idf32"][:])
            cs.put(gp, ctx.gwt[:, 0:64].rearrange("p (d e) -> p d e", d=8, e=8),
                   dr["gate_w"][:, :, :].rearrange("d p e -> p d e"))
            if "moe_b2" in dr:
                cs.put(gp, ctx.b2bf[:], dr["moe_b2"][:, :])
            if "projbd" in dr:
                cs.put(gp, ctx.cstb[:, B_PROJ:B_PROJ + 128], dr["projbd"][:])
        blk.gpsimd(f_g)

        def f_s(se):
            cs.wait_all(se)
            se.activation(col(ctx.cst, C_A, 256), col(ctx.cst, C_A, 256), AF.Exp)
        blk.scalar(f_s)
    with nc.Block() as blk2:
        def f_v(ve):
            ve.tensor_scalar_mul(col(ctx.cst, C_A, 256), col(ctx.cst, C_A, 256), -1.0)
            if "moe_b2" in dr:
                ve.tensor_scalar_mul(ctx.b2bf[:], ctx.b2bf[:], 1.702)
        blk2.vector(f_v)



def emit_moe_prep(ctx, dr, scr):
    ctx.new_phase()
    nc = ctx.nc
    pc = Stream(ctx, "mp_c", 2, kind="gp")
    pst = Stream(ctx, "mp_s", 2)
    jobs = []
    for e in range(NE):
        for dt in range(DT):
            jobs.append((dr["moe_w1t"][e, dt, :, :], scr["w1bf"][e, dt, :, :], None))
    for e in range(NE):
        for g4 in range(4):
            jobs.append((dr["moe_w2t"][e, g4 * 4:(g4 + 1) * 4, :, :].rearrange("j p c -> p j c"),
                         scr["w2bf"][e, g4 * 4:(g4 + 1) * 4, :, :].rearrange("j p c -> p j c"), 4))
    with ExitStack() as ps:
        sb = [ps.enter_context(nc.sbuf_tensor(U(f"mp_b{i}"), [128, 4096], BF16)) for i in range(2)]
        with nc.Block() as blk:
            def f_g(gp):
                for i, (src_ap, _, j) in enumerate(jobs):
                    if i >= 2:
                        pst.wait_item(gp, i - 2)
                    dst = sb[i % 2][:] if j is None else sb[i % 2][:].rearrange("p (j c) -> p j c", j=j, c=1024)
                    pc.put(gp, dst, src_ap)
            blk.gpsimd(f_g)

            def f_s(sy):
                for i, (_, dst_ap, j) in enumerate(jobs):
                    pc.wait_item(sy, i)
                    srcb = sb[i % 2][:] if j is None else sb[i % 2][:].rearrange("p (j c) -> p j c", j=j, c=1024)
                    pst.put(sy, dst_ap, srcb)
                pst.wait_all(sy)
            blk.sync(f_s)



def emit_ln(ctx, x_src, xn_dst, kind):
    ctx.new_phase()
    nc = ctx.nc
    moe = kind == "moe"
    affine = kind == "affine"
    src_bf = affine
    lnx = Stream(ctx, "ln_x", 4)
    lns = Stream(ctx, "ln_s", 4, kind="gp")
    q1 = ctx.S("ln_q1"); m1 = ctx.S("ln_m1"); v1 = ctx.S("ln_v1")
    q2 = ctx.S("ln_q2"); v2 = ctx.S("ln_v2"); m2 = ctx.S("ln_m2")
    v3 = ctx.S("ln_v3"); m3 = ctx.S("ln_m3"); v4 = ctx.S("ln_v4")
    m4 = ctx.S("ln_m4"); v5 = ctx.S("ln_v5"); q3 = ctx.S("ln_q3")
    v6 = ctx.S("ln_v6"); m5 = ctx.S("ln_m5"); v7 = ctx.S("ln_v7")
    L0 = {s: s.n for s in (q1, m1, v1, q2, v2, m2, v3, m3, v4, m4, v5, q3, v6, m5, v7)}

    with ExitStack() as ps:
        xdt = BF16 if src_bf else F32
        xa = [ps.enter_context(nc.sbuf_tensor(U(f"ln_xa{i}"), [128, 4096], xdt)) for i in range(2)]
        xq = ps.enter_context(nc.sbuf_tensor(U("ln_xq"), [128, 4096], F32))
        nab = [ps.enter_context(nc.sbuf_tensor(U(f"ln_n{i}"), [128, 4096], BF16)) for i in range(2)]
        xnf = ps.enter_context(nc.sbuf_tensor(U("ln_xnf"), [128, 4096], F32)) if moe else None
        psS = ps.enter_context(nc.psum_tensor(U("ln_ps"), [1, 1024], F32))
        psB = ps.enter_context(nc.psum_tensor(U("ln_pb"), [128, 1024], F32))
        psG = ps.enter_context(nc.psum_tensor(U("ln_pg"), [8, 512], F32)) if moe else None
        psT = ps.enter_context(nc.psum_tensor(U("ln_pt"), [128, 8], F32)) if moe else None
        psM = ps.enter_context(nc.psum_tensor(U("ln_pm"), [8, 128], BF16)) if moe else None

        with nc.Block() as blk:
            def f_sync(sy):
                for ch in range(NCHK):
                    if ch >= 2:
                        sy.wait_ge(v3.h, L0[v3] + ch - 1)
                    for h in range(2):
                        lnx.put(sy, xa[ch % 2][:, h * 2048:(h + 1) * 2048].rearrange(
                                    "p (j c) -> p j c", j=4, c=512),
                                x_src[h * 4:(h + 1) * 4, :, ch * 512:(ch + 1) * 512].rearrange(
                                    "j p c -> p j c"))
            blk.sync(f_sync)

            def f_scalar(se):
                for ch in range(NCHK):
                    lnx.wait_upto(se, ch * 2 + 1)
                    if ch >= 1:
                        se.wait_ge(m1.h, L0[m1] + ch)
                    ins = None
                    for dt in range(DT):
                        ins = se.activation(xq[:, dt * 512:(dt + 1) * 512],
                                            xa[ch % 2][:, dt * 512:(dt + 1) * 512], AF.Square)
                    q1.inc(ins)
                    se.wait_ge(v1.h, L0[v1] + ch + 1)
                    q2.inc(se.activation(ctx.lr[:, 512:1024], ctx.lr[:, 1024:1536], AF.Sqrt))
                    if moe:
                        for u in range(4):
                            se.wait_ge(v5.h, L0[v5] + ch * 4 + u + 1)
                            q3.inc(se.activation(ctx.g8a[:, 35:36], ctx.g8a[:, 34:35], AF.Exp))
            blk.scalar(f_scalar)

            def f_tensor(te):
                onesf = col(ctx.cst, C_ONES)
                ones_in = ctx.cstb[:, B_ONESC:B_ONESC + 1] if src_bf else onesf
                for ch in range(NCHK):
                    te.wait_ge(q1.h, L0[q1] + ch + 1)
                    if ch >= 1:
                        te.wait_ge(v1.h, L0[v1] + ch)
                    for dt in range(DT):
                        te.matmul(psS[:, 0:512], ones_in, xa[ch % 2][:, dt * 512:(dt + 1) * 512],
                                  start=(dt == 0), stop=(dt == DT - 1))
                    ins = None
                    for dt in range(DT):
                        ins = te.matmul(psS[:, 512:1024], onesf, xq[:, dt * 512:(dt + 1) * 512],
                                        start=(dt == 0), stop=(dt == DT - 1))
                    m1.inc(ins)
                    te.wait_ge(v2.h, L0[v2] + ch + 1)
                    if ch >= 1:
                        te.wait_ge(v3.h, L0[v3] + ch)
                    onerow = ctx.cstb[0:1, B_ONER:B_ONER + 128]
                    te.matmul(psB[:, 0:512], onerow, ctx.p0r[0:1, :], start=True, stop=True)
                    ins = te.matmul(psB[:, 512:1024], onerow, ctx.p1r[0:1, :], start=True, stop=True)
                    m2.inc(ins)
                    if moe:
                        te.wait_ge(v3.h, L0[v3] + ch + 1)
                        if ch >= 1:
                            te.wait_ge(v4.h, L0[v4] + ch)
                        ins = None
                        for dt in range(DT):
                            ins = te.matmul(psG[:], ctx.gwt[:, dt * 8:(dt + 1) * 8],
                                            xnf[:, dt * 512:(dt + 1) * 512],
                                            start=(dt == 0), stop=(dt == DT - 1))
                        m3.inc(ins)
                        te.wait_ge(v4.h, L0[v4] + ch + 1)
                        for u in range(4):
                            if ch * 4 + u >= 1:
                                te.wait_ge(v5.h, L0[v5] + ch * 4 + u)
                            m4.inc(te.transpose(psT[:], ctx.srow[0:8, u * 128:(u + 1) * 128],
                                                ctx.idf[0:8, 0:8]))
                            te.wait_ge(v6.h, L0[v6] + ch * 4 + u + 1)
                            if ch * 4 + u >= 1:
                                te.wait_ge(v7.h, L0[v7] + ch * 4 + u)
                            m5.inc(te.transpose(psM[:], ctx.g8b[:], ctx.cstb[:, B_ID:B_ID + 128]))
            blk.tensor(f_tensor)

            def f_vector(ve):
                iota = col(ctx.cst, C_IOTA, 8)
                iota1k = col(ctx.cst, C_IOTA1K, 8)
                lr = ctx.lr
                for ch in range(NCHK):
                    ve.wait_ge(m1.h, L0[m1] + ch + 1)
                    if ch >= 1:
                        ve.wait_ge(q2.h, L0[q2] + ch)
                    ve.tensor_scalar_mul(lr[:, 0:512], psS[:, 0:512], 1.0 / D)
                    ve.tensor_scalar_mul(lr[:, 512:1024], psS[:, 512:1024], 1.0 / D)
                    ve.tensor_tensor(lr[:, 1024:1536], lr[:, 0:512], lr[:, 0:512], ALU.mult)
                    ve.tensor_tensor(lr[:, 1024:1536], lr[:, 512:1024], lr[:, 1024:1536], ALU.subtract)
                    v1.inc(ve.tensor_scalar_add(lr[:, 1024:1536], lr[:, 1024:1536], 1e-5))
                    ve.wait_ge(q2.h, L0[q2] + ch + 1)
                    vsync(ctx, ve, ve.reciprocal(lr[:, 1024:1536], lr[:, 512:1024]))
                    ve.tensor_copy(ctx.p0r[0:1, :], lr[:, 1024:1536])
                    ve.tensor_tensor(lr[:, 0:512], lr[:, 0:512], lr[:, 1024:1536], ALU.mult)
                    v2.inc(ve.tensor_scalar_mul(ctx.p1r[0:1, :], lr[:, 0:512], -1.0))
                    ve.wait_ge(m2.h, L0[m2] + ch + 1)
                    if ch >= 2:
                        lns.wait_upto(ve, (ch - 1) * 2 - 1)
                    if moe and ch >= 1:
                        ve.wait_ge(m3.h, L0[m3] + ch)
                    nbuf = nab[ch % 2]
                    ins = None
                    for dt in range(DT):
                        sl = slice(dt * 512, (dt + 1) * 512)
                        ve.tensor_tensor(ctx.ta[:], xa[ch % 2][:, sl], psB[:, 0:512], ALU.mult)
                        if moe:
                            ve.tensor_tensor(xnf[:, sl], ctx.ta[:], psB[:, 512:1024], ALU.add)
                            ins = ve.tensor_copy(nbuf[:, sl], xnf[:, sl])
                        elif affine:
                            ve.tensor_tensor(ctx.tb[:], ctx.ta[:], psB[:, 512:1024], ALU.add)
                            ins = ve.tensor_scalar(nbuf[:, sl], ctx.tb[:],
                                                   col(ctx.cst, C_NW + dt), col(ctx.cst, C_NB + dt),
                                                   ALU.mult, ALU.add)
                        else:
                            ins = ve.tensor_tensor(nbuf[:, sl], ctx.ta[:], psB[:, 512:1024], ALU.add)
                    v3.inc(ins)
                    if moe:
                        ve.wait_ge(m3.h, L0[m3] + ch + 1)
                        v4.inc(ve.tensor_copy(ctx.srow[:], psG[:]))
                        if ch == 0 and getattr(ctx, "dbg", None) is not None:
                            ve.tensor_copy(ctx.dbg[0:8, 0:512], ctx.srow[:])
                        for u in range(4):
                            ve.wait_ge(m4.h, L0[m4] + ch * 4 + u + 1)
                            g8 = ctx.g8a
                            if ch == 0 and u == 0 and getattr(ctx, "dbg", None) is not None:
                                ve.tensor_copy(ctx.dbg[:, 512:520], psT[:])
                            def V(ins):
                                vsync(ctx, ve, ins)
                            V(ve.max(g8[:, 8:16], psT[:]))
                            v1c, v2c = g8[:, 8:9], g8[:, 9:10]
                            V(ve.tensor_scalar(g8[:, 16:24], psT[:], v1c, None, ALU.subtract))
                            V(ve.tensor_single_scalar(g8[:, 16:24], g8[:, 16:24], 0.0, ALU.is_equal))
                            V(ve.scalar_tensor_tensor(g8[:, 24:32], g8[:, 16:24], -1000.0, iota1k,
                                                      ALU.mult, ALU.add))
                            V(ve.tensor_reduce(g8[:, 32:33], g8[:, 24:32], AX.X, ALU.min))
                            V(ve.tensor_scalar(g8[:, 16:24], iota, g8[:, 32:33], None, ALU.subtract))
                            V(ve.tensor_single_scalar(g8[:, 16:24], g8[:, 16:24], 0.0, ALU.is_equal))
                            V(ve.tensor_scalar(g8[:, 40:48], psT[:], v2c, None, ALU.subtract))
                            V(ve.tensor_single_scalar(g8[:, 40:48], g8[:, 40:48], 0.0, ALU.is_equal))
                            V(ve.tensor_tensor(g8[:, 48:56], g8[:, 40:48], g8[:, 16:24], ALU.mult))
                            V(ve.tensor_tensor(g8[:, 40:48], g8[:, 40:48], g8[:, 48:56], ALU.subtract))
                            V(ve.scalar_tensor_tensor(g8[:, 48:56], g8[:, 40:48], -1000.0, iota1k,
                                                      ALU.mult, ALU.add))
                            V(ve.tensor_reduce(g8[:, 33:34], g8[:, 48:56], AX.X, ALU.min))
                            V(ve.tensor_scalar(g8[:, 40:48], iota, g8[:, 33:34], None, ALU.subtract))
                            V(ve.tensor_single_scalar(g8[:, 40:48], g8[:, 40:48], 0.0, ALU.is_equal))
                            v5.inc(ve.tensor_tensor(g8[:, 34:35], v2c, v1c, ALU.subtract))
                            ve.wait_ge(q3.h, L0[q3] + ch * 4 + u + 1)
                            V(ve.tensor_scalar_add(g8[:, 36:37], g8[:, 35:36], 1.0))
                            V(ve.reciprocal(g8[:, 37:38], g8[:, 36:37]))
                            V(ve.tensor_tensor(g8[:, 38:39], g8[:, 35:36], g8[:, 37:38], ALU.mult))
                            V(ve.tensor_scalar_mul(g8[:, 37:38], g8[:, 37:38], 1.0 / 1.702))
                            V(ve.tensor_scalar_mul(g8[:, 38:39], g8[:, 38:39], 1.0 / 1.702))
                            V(ve.tensor_scalar(g8[:, 48:56], g8[:, 16:24], g8[:, 37:38], None, ALU.mult))
                            if ch == 0 and u == 0 and getattr(ctx, "dbg", None) is not None:
                                ve.tensor_copy(ctx.dbg[:, 528:559], g8[:, 8:39])
                                ve.tensor_copy(ctx.dbg[:, 560:576], g8[:, 40:56])
                            v6.inc(ve.scalar_tensor_tensor(ctx.g8b[:], g8[:, 40:48], g8[:, 38:39],
                                                           g8[:, 48:56], ALU.mult, ALU.add))
                            ve.wait_ge(m5.h, L0[m5] + ch * 4 + u + 1)
                            v7.inc(ve.tensor_copy(ctx.mT[0:8, ch * 512 + u * 128:ch * 512 + (u + 1) * 128],
                                                  psM[:]))
            blk.vector(f_vector)

            def f_gp(gp):
                for ch in range(NCHK):
                    gp.wait_ge(v3.h, L0[v3] + ch + 1)
                    nbuf = nab[ch % 2]
                    for h in range(2):
                        lns.put(gp, xn_dst[h * 4:(h + 1) * 4, :, ch * 512:(ch + 1) * 512].rearrange(
                                    "j p c -> p j c"),
                                nbuf[:, h * 2048:(h + 1) * 2048].rearrange("p (j c) -> p j c", j=4, c=512))
                lns.wait_all(gp)
            blk.gpsimd(f_gp)

    assert v3.n == L0[v3] + NCHK and lnx.n % (2 * NCHK) == 0
    if moe:
        assert v7.n == L0[v7] + NCHK * 4 and q3.n == L0[q3] + NCHK * 4



def emit_moe(ctx, dr, scr, xn_scr, x_src, x_dst):
    ctx.new_phase()
    nc = ctx.nc
    w1 = scr["w1bf"]
    w2 = scr["w2bf"]
    mx = Stream(ctx, "mo_x", 2)
    mw1 = Stream(ctx, "mo_w1", 4)
    mw2 = Stream(ctx, "mo_w2", 2)
    mxr = Stream(ctx, "mo_xr", 2, kind="gp")
    mst = Stream(ctx, "mo_st", 2, kind="gp")
    mmb = ctx.S("mo_mb"); seb = ctx.S("mo_se")
    md1 = ctx.S("mo_m1"); vg = ctx.S("mo_vg"); sg = ctx.S("mo_sg"); vq = ctx.S("mo_vq")
    md2 = ctx.S("mo_m2"); m2u = ctx.S("mo_m2u"); vev = ctx.S("mo_ev")
    L = {s: s.n for s in (mmb, seb, md1, vg, sg, vq, md2, m2u, vev)}
    NU = 8
    PPC = NE * NU * 2
    UPC = NE * NU
    U2PC = 2 * NE * 8

    with ExitStack() as ps:
        gpr = ps.enter_context(nc.sbuf_tensor(U("mo_g"), [128, 128 * 512], BF16))
        xt = [ps.enter_context(nc.sbuf_tensor(U(f"mo_xt{i}"), [128, 512], F32)) for i in range(2)]
        ot = [ps.enter_context(nc.sbuf_tensor(U(f"mo_ot{i}"), [128, 512], F32)) for i in range(2)]
        sgb = ps.enter_context(nc.sbuf_tensor(U("mo_sgb"), [128, 512], BF16))
        vvb = ps.enter_context(nc.sbuf_tensor(U("mo_vvb"), [128, 512], BF16))
        nbuf = ps.enter_context(nc.sbuf_tensor(U("mo_n"), [128, 4096], BF16))
        wab = [ps.enter_context(nc.sbuf_tensor(U(f"mo_w{i}"), [128, 4096], BF16)) for i in range(2)]
        w2ab = [ps.enter_context(nc.sbuf_tensor(U(f"mo_w2{i}"), [128, 2048], BF16)) for i in range(2)]
        psg = [ps.enter_context(nc.psum_tensor(U(f"mo_pg{i}"), [128, 512], F32)) for i in range(2)]
        psv = [ps.enter_context(nc.psum_tensor(U(f"mo_pv{i}"), [128, 512], F32)) for i in range(2)]
        pso = [ps.enter_context(nc.psum_tensor(U(f"mo_po{i}"), [128, 512], F32)) for i in range(4)]

        with nc.Block() as blk:
            def f_sync(sy):
                for ch in range(NCHK):
                    if ch >= 1:
                        sy.wait_ge(md1.h, L[md1] + ch * PPC)
                    for h in range(2):
                        mx.put(sy, nbuf[:, h * 2048:(h + 1) * 2048].rearrange(
                                   "p (j c) -> p j c", j=4, c=512),
                               xn_scr[h * 4:(h + 1) * 4, :, ch * 512:(ch + 1) * 512].rearrange(
                                   "j p c -> p j c"))
                    for e in range(NE):
                        for q in range(NU):
                            k = ch * UPC + e * NU + q
                            if k >= 2:
                                sy.wait_ge(md1.h, L[md1] + (k - 1) * 2)
                            for h in range(2):
                                mw1.put(sy, wab[k % 2][:, h * 2048:(h + 1) * 2048].rearrange(
                                            "p (j c) -> p j c", j=4, c=512),
                                        w1[e, h * 4:(h + 1) * 4, :, q * 512:(q + 1) * 512].rearrange(
                                            "j p c -> p j c"))
                    for half in range(2):
                        for e in range(NE):
                            for ug in range(8):
                                k2 = ch * U2PC + half * 64 + e * 8 + ug
                                if k2 >= 2:
                                    sy.wait_ge(m2u.h, L[m2u] + k2 - 1)
                                mw2.put(sy, w2ab[k2 % 2][:].rearrange("p (j c) -> p j c", j=2, c=1024),
                                        w2[e, ug * 2:(ug + 1) * 2, :, :].rearrange("j p c -> p j c"))
            blk.sync(f_sync)

            def f_gp(gp):
                for i in range(NCHK * DT):
                    ch, dt = divmod(i, DT)
                    if i >= 2:
                        gp.wait_ge(vev.h, L[vev] + i - 1)
                    mxr.put(gp, xt[i % 2][:], x_src[dt, :, ch * 512:(ch + 1) * 512])
                    if i >= 1:
                        gp.wait_ge(vev.h, L[vev] + i)
                        mst.put(gp, x_dst[(i - 1) % DT, :, ((i - 1) // DT) * 512:((i - 1) // DT + 1) * 512],
                                ot[(i - 1) % 2][:])
                i = NCHK * DT
                gp.wait_ge(vev.h, L[vev] + i)
                mst.put(gp, x_dst[(i - 1) % DT, :, ((i - 1) // DT) * 512:((i - 1) // DT + 1) * 512],
                        ot[(i - 1) % 2][:])
                mst.wait_all(gp)
            blk.gpsimd(f_gp)

            def f_tensor(te):
                for ch in range(NCHK):
                    mx.wait_upto(te, ch * 2 + 1)
                    if ch >= 1:
                        te.wait_ge(vev.h, L[vev] + ch * 8)
                    for e in range(NE):
                        ie = ch * NE + e
                        if ie >= 1:
                            te.wait_ge(seb.h, L[seb] + ie)
                        mmb.inc(te.matmul(pso[0][:], ctx.cstb[0:8, B_SEL8 + e * 128:B_SEL8 + (e + 1) * 128],
                                          ctx.mT[0:8, ch * 512:(ch + 1) * 512], start=True, stop=True))
                        for q in range(NU):
                            k = ch * UPC + e * NU + q
                            mw1.wait_item(te, 2 * k)
                            mw1.wait_item(te, 2 * k + 1)
                            wbuf = wab[k % 2]
                            for pr in range(2):
                                ip = k * 2 + pr
                                if ip >= 2:
                                    te.wait_ge(vg.h, L[vg] + ip - 1)
                                ins = None
                                for dt in range(DT):
                                    te.matmul(psg[ip % 2][:],
                                              wbuf[:, dt * 512 + (2 * pr) * 128: dt * 512 + (2 * pr + 1) * 128],
                                              nbuf[:, dt * 512:(dt + 1) * 512],
                                              start=(dt == 0), stop=(dt == DT - 1))
                                for dt in range(DT):
                                    ins = te.matmul(psv[ip % 2][:],
                                                    wbuf[:, dt * 512 + (2 * pr + 1) * 128: dt * 512 + (2 * pr + 2) * 128],
                                                    nbuf[:, dt * 512:(dt + 1) * 512],
                                                    start=(dt == 0), stop=(dt == DT - 1))
                                md1.inc(ins)
                    te.wait_ge(vq.h, L[vq] + (ch + 1) * PPC)
                    for half in range(2):
                        if half == 1:
                            te.wait_ge(vev.h, L[vev] + ch * 8 + 4)
                        for e in range(NE):
                            for ug in range(8):
                                k2 = ch * U2PC + half * 64 + e * 8 + ug
                                mw2.wait_item(te, k2)
                                w2buf = w2ab[k2 % 2]
                                ins = None
                                for ft in range(2):
                                    kt = e * 16 + ug * 2 + ft
                                    for dth in range(4):
                                        dt = half * 4 + dth
                                        ins = te.matmul(pso[dth][:],
                                                        w2buf[:, ft * 1024 + dt * 128: ft * 1024 + (dt + 1) * 128],
                                                        gpr[:, kt * 512:(kt + 1) * 512],
                                                        start=(e == 0 and ug == 0 and ft == 0),
                                                        stop=False)
                                m2u.inc(ins)
                        for dth in range(4):
                            dt = half * 4 + dth
                            ins = te.matmul(pso[dth][:], ctx.b2bf[0:8, dt * 128:(dt + 1) * 128],
                                            ctx.mT[0:8, ch * 512:(ch + 1) * 512],
                                            start=False, stop=True)
                            md2.inc(ins)
            blk.tensor(f_tensor)

            def f_scalar(se):
                for ch in range(NCHK):
                    for e in range(NE):
                        ie = ch * NE + e
                        se.wait_ge(mmb.h, L[mmb] + ie + 1)
                        if ie >= 2:
                            se.wait_ge(vq.h, L[vq] + (ie - 1) * 16)
                        mb = ctx.mba if e % 2 == 0 else ctx.mbb
                        seb.inc(se.copy(mb[:], pso[0][:]))
                        for qp in range(16):
                            ip = ie * 16 + qp
                            se.wait_ge(vg.h, L[vg] + ip + 1)
                            sg.inc(se.activation(sgb[:], ctx.tb[:], AF.Silu, scale=col(ctx.cst, C_1702)))
            blk.scalar(f_scalar)

            def f_vector(ve):
                for ch in range(NCHK):
                    if ch >= 1:
                        ve.wait_ge(m2u.h, L[m2u] + ch * U2PC)
                    for e in range(NE):
                        ie = ch * NE + e
                        mb = ctx.mba if e % 2 == 0 else ctx.mbb
                        for qp in range(16):
                            ip = ie * 16 + qp
                            q, pr = divmod(qp, 2)
                            gt = 4 * q + 2 * pr
                            kt = e * 16 + q * 2 + pr
                            ve.wait_ge(md1.h, L[md1] + ip + 1)
                            b1g = col(ctx.cst, C_B1M + e * 32 + gt)
                            b1v = col(ctx.cst, C_B1M + e * 32 + gt + 1)
                            ve.tensor_scalar(ctx.ta[:], psv[ip % 2][:], b1v, 9.0, ALU.add, ALU.min)
                            ve.tensor_scalar(vvb[:], ctx.ta[:], -9.0, 1.0, ALU.max, ALU.add)
                            vg.inc(ve.tensor_scalar(ctx.tb[:], psg[ip % 2][:], b1g, 9.0, ALU.add, ALU.min))
                            ve.wait_ge(sg.h, L[sg] + ip + 1)
                            if ip == ie * 16:
                                ve.wait_ge(seb.h, L[seb] + ie + 1)
                            ve.tensor_tensor(sgb[:], sgb[:], mb[:], ALU.mult)
                            vq.inc(ve.tensor_tensor(gpr[:, kt * 512:(kt + 1) * 512], sgb[:], vvb[:], ALU.mult))
                    for half in range(2):
                        for dth in range(4):
                            i = ch * 8 + half * 4 + dth
                            ve.wait_ge(md2.h, L[md2] + ch * 8 + half * 4 + dth + 1)
                            mxr.wait_item(ve, i)
                            if i >= 2:
                                mst.wait_item(ve, i - 2)
                            vev.inc(ve.scalar_tensor_tensor(ot[i % 2][:], pso[dth][:], 0.5,
                                                            xt[i % 2][:], ALU.mult, ALU.add))
            blk.vector(f_vector)

    assert md1.n == L[md1] + NCHK * PPC and vq.n == L[vq] + NCHK * PPC
    assert md2.n == L[md2] + NCHK * 8 and vev.n == L[vev] + NCHK * 8
    assert m2u.n == L[m2u] + NCHK * U2PC



def _dram_in(nc, name, arr):
    return nc.dram_tensor(name, list(arr.shape), F32 if arr.dtype == np.float32 else BF16,
                          kind="ExternalInput")


def _needed(subs):
    need = {"cst", "cstb", "idf32", "gate_w", "sel32"}
    for s in subs:
        if s in (0, 2):
            need |= {"moe_w1t", "moe_w2t", "moe_b2"}
        if s == 1:
            need |= {"ipjt", "xpjt", "dtjt", "opjt"}
        if s in (3, 5):
            need |= {"w1t_l1", "w2t_l1"}
        if s == 4:
            need |= {"wq", "wk", "wv", "wo", "projbd"}
    return need


def _build_program(dr_arrays, subs):
    nc = bass.Bass("TRN2", target_bir_lowering=False, detect_race_conditions=False)
    dr = {}
    for name in sorted(_needed(subs)):
        dr[name] = _dram_in(nc, name, dr_arrays[name])
    xin = nc.dram_tensor("XIN", [8, 128, T], F32, kind="ExternalInput")
    xout = nc.dram_tensor("XOUT", [8, 128, T], F32, kind="ExternalOutput")
    xres = nc.dram_tensor("xres", [8, 128, T], F32, kind="Internal")
    xn_scr = nc.dram_tensor("xn_scr", [8, 128, T], BF16, kind="Internal")
    xn1_scr = nc.dram_tensor("xn1_scr", [8, 128, T], BF16, kind="Internal")
    scr = {}
    if 1 in subs:
        for nm in ("xc_scr", "z_scr", "u_scr", "dl_scr"):
            scr[nm] = nc.dram_tensor(nm, [16, 128, T], BF16, kind="Internal")
    if 3 in subs or 5 in subs:
        scr["s1_scr"] = nc.dram_tensor("s1_scr", [8, 128, 4096], BF16, kind="Internal")
        scr["s2_scr"] = nc.dram_tensor("s2_scr", [32, 128, 1024], BF16, kind="Internal")
    if 0 in subs or 2 in subs:
        scr["w1bf"] = nc.dram_tensor("w1bf", [8, 8, 128, 4096], BF16, kind="Internal")
        scr["w2bf"] = nc.dram_tensor("w2bf", [8, 16, 128, 1024], BF16, kind="Internal")

    with ExitStack() as st:
        ctx = Ctx(nc, st)
        ctx.alloc_global()
        emit_const_load(ctx, dr)
        if 0 in subs or 2 in subs:
            emit_moe_prep(ctx, dr, scr)
        if 3 in subs or 5 in subs:
            emit_l1prep(ctx, dr, scr)
        cur = xin
        for idx, s in enumerate(subs):
            last = idx == len(subs) - 1
            dst = xout if last else xres
            if s in (0, 2):
                emit_ln(ctx, cur, xn_scr, "moe")
                emit_moe(ctx, dr, scr, xn_scr, cur, dst)
            elif s == 1:
                emit_ln(ctx, cur, xn1_scr, "plain")
                emit_ln(ctx, xn1_scr, xn_scr, "affine")
                emit_mamba(ctx, dr, xn_scr, xn1_scr, cur, dst, scr)
            elif s in (3, 5):
                emit_ln(ctx, cur, xn_scr, "plain")
                emit_l1ffn(ctx, dr, scr, xn_scr, cur, dst)
            elif s == 4:
                emit_ln(ctx, cur, xn_scr, "plain")
                emit_l1attn(ctx, dr, xn_scr, cur, dst)
            cur = dst
    return nc


def _in_map_for(dr_arrays, subs, xarr):
    m = {"XIN": xarr}
    for name in sorted(_needed(subs)):
        m[name] = dr_arrays[name]
    return m


def kernel(**inputs):
    d = _host_prep(inputs)
    xs = d["_x_cores"]
    if MODE == "fused":
        groups = [list(SUBS)]
    else:
        groups = [[s] for s in SUBS]
    cur = xs
    for g in groups:
        nc = _build_program(d, g)
        in_maps = [_in_map_for(d, g, cur[c]) for c in range(NCORES)]
        res = run_bass_kernel_spmd(nc, in_maps, core_ids=list(range(NCORES)))
        cur = [np.asarray(res.results[c]["XOUT"]) for c in range(NCORES)]
    out = np.empty((16, SEQ, D), np.float32)
    for c in range(NCORES):
        xc = np.asarray(cur[c], np.float32).reshape(D, T).T.reshape(2, SEQ, D)
        out[2 * c:2 * c + 2] = xc
    return out


def emit_l1prep(ctx, dr, scr):
    ctx.new_phase()
    nc = ctx.nc
    cst = ctx.cst
    lw = Stream(ctx, "lp_w", 2)
    ls = Stream(ctx, "lp_s", 2, kind="gp")
    pmm = ctx.S("lp_mm"); pse = ctx.S("lp_se"); pve = ctx.S("lp_ve")
    L = {pmm: pmm.n, pse: pse.n, pve: pve.n}
    onesb = ctx.cstb[:, B_ONESC:B_ONESC + 1]
    with ExitStack() as ps:
        wf = [ps.enter_context(nc.sbuf_tensor(U(f"lp_w{i}"), [128, 4096], F32)) for i in range(2)]
        sg = [ps.enter_context(nc.sbuf_tensor(U(f"lp_g{i}"), [128, 4096], BF16)) for i in range(2)]
        ab8 = ps.enter_context(nc.sbuf_tensor(U("lp_a"), [128, 32768], BF16))
        psA = ps.enter_context(nc.psum_tensor(U("lp_pA"), [128, 32], F32))
        psB2 = ps.enter_context(nc.psum_tensor(U("lp_pB"), [128, 8], F32))
        with nc.Block() as blk:
            def f_sync(sy):
                for dt in range(DT):
                    if dt >= 2:
                        sy.wait_ge(pse.h, L[pse] + dt - 1)
                    lw.put(sy, wf[dt % 2][:], dr["w1t_l1"][dt, :, :])
                for ht in range(32):
                    i = 8 + ht
                    sy.wait_ge(pse.h, L[pse] + i - 1)
                    lw.put(sy, wf[i % 2][:, 0:1024], dr["w2t_l1"][ht, :, :])
            blk.sync(f_sync)

            def f_scalar(se):
                for dt in range(DT):
                    lw.wait_item(se, dt)
                    if dt >= 2:
                        ls.wait_item(se, dt - 2)
                    se.activation(sg[dt % 2][:], wf[dt % 2][:], AF.Sign)
                    pse.inc(se.activation(ab8[:, dt * 4096:(dt + 1) * 4096], wf[dt % 2][:], AF.Abs))
                for ht in range(32):
                    i = 8 + ht
                    lw.wait_item(se, i)
                    ls.wait_item(se, i - 2)
                    se.activation(sg[i % 2][:, 0:1024], wf[i % 2][:, 0:1024], AF.Sign)
                    if ht == 0:
                        se.wait_ge(pmm.h, L[pmm] + 8)
                    pse.inc(se.activation(ab8[:, ht * 1024:(ht + 1) * 1024], wf[i % 2][:, 0:1024], AF.Abs))
            blk.scalar(f_scalar)

            def f_gp(gp):
                for dt in range(DT):
                    gp.wait_ge(pse.h, L[pse] + dt + 1)
                    ls.put(gp, scr["s1_scr"][dt, :, :], sg[dt % 2][:])
                for ht in range(32):
                    i = 8 + ht
                    gp.wait_ge(pse.h, L[pse] + i + 1)
                    ls.put(gp, scr["s2_scr"][ht, :, :], sg[i % 2][:, 0:1024])
                ls.wait_all(gp)
            blk.gpsimd(f_gp)

            def f_tensor(te):
                te.wait_ge(pse.h, L[pse] + 8)
                ins = None
                for ht in range(32):
                    for dt in range(DT):
                        ins = te.matmul(psA[:, ht:ht + 1],
                                        ab8[:, dt * 4096 + ht * 128: dt * 4096 + (ht + 1) * 128],
                                        onesb, start=(dt == 0), stop=(dt == DT - 1))
                pmm.inc(ins, 8)
                te.wait_ge(pse.h, L[pse] + 40)
                ins = None
                for dto in range(DT):
                    for ht in range(32):
                        ins = te.matmul(psB2[:, dto:dto + 1],
                                        ab8[:, ht * 1024 + dto * 128: ht * 1024 + (dto + 1) * 128],
                                        onesb, start=(ht == 0), stop=(ht == 31))
                pmm.inc(ins, 32)
            blk.tensor(f_tensor)

            def f_vector(ve):
                ve.wait_ge(pmm.h, L[pmm] + 8)
                vsync(ctx, ve, ve.tensor_scalar(col(cst, C_AL1, 32), psA[:], 1.0 / 1024, None, ALU.mult))
                vsync(ctx, ve, ve.reciprocal(col(cst, C_BJ1, 32), col(cst, C_AL1, 32)))
                vsync(ctx, ve, ve.tensor_tensor(col(cst, C_BJ1, 32), col(cst, C_L1B1, 32),
                                                col(cst, C_BJ1, 32), ALU.mult))
                ve.wait_ge(pmm.h, L[pmm] + 40)
                vsync(ctx, ve, ve.tensor_scalar(col(cst, C_AL2, 8), psB2[:], 0.5 / 4096, None, ALU.mult))
                vsync(ctx, ve, ve.tensor_scalar(col(cst, C_BJ2, 8), col(cst, C_AL2, 8), 2.0, None, ALU.mult))
                vsync(ctx, ve, ve.reciprocal(col(cst, C_BJ2, 8), col(cst, C_BJ2, 8)))
                pve.inc(ve.tensor_tensor(col(cst, C_BJ2, 8), col(cst, C_L1B2, 8),
                                         col(cst, C_BJ2, 8), ALU.mult))
            blk.vector(f_vector)


def emit_l1ffn(ctx, dr, scr, xn_scr, x_src, x_dst):
    ctx.new_phase()
    nc = ctx.nc
    cst = ctx.cst
    ls1 = Stream(ctx, "lf_s1", 2)
    lx = Stream(ctx, "lf_x", 2)
    lw2 = Stream(ctx, "lf_w2", 2)
    lxr = Stream(ctx, "lf_xr", 2, kind="gp")
    lst = Stream(ctx, "lf_st", 2, kind="gp")
    d1m = ctx.S("lf_d1")
    d1v = ctx.S("lf_v1")
    d1s = ctx.S("lf_q1")
    d2m = ctx.S("lf_d2")
    d2u = ctx.S("lf_d2u")
    d2v = ctx.S("lf_v2")
    L = {s: s.n for s in (d1m, d1v, d1s, d2m, d2u, d2v)}
    with ExitStack() as ps:
        s1r = ps.enter_context(nc.sbuf_tensor(U("lf_S1"), [128, 32768], BF16))
        nbuf = ps.enter_context(nc.sbuf_tensor(U("lf_n"), [128, 4096], BF16))
        g1 = ps.enter_context(nc.sbuf_tensor(U("lf_g1"), [128, 16384], BF16))
        tg = [ps.enter_context(nc.sbuf_tensor(U(f"lf_tg{i}"), [128, 512], F32)) for i in range(2)]
        w2l = [ps.enter_context(nc.sbuf_tensor(U(f"lf_w2{i}"), [128, 1024], BF16)) for i in range(2)]
        xt = [ps.enter_context(nc.sbuf_tensor(U(f"lf_xt{i}"), [128, 512], F32)) for i in range(2)]
        ot = [ps.enter_context(nc.sbuf_tensor(U(f"lf_ot{i}"), [128, 512], F32)) for i in range(2)]
        tv = ps.enter_context(nc.sbuf_tensor(U("lf_tv"), [128, 512], F32))
        p1 = [ps.enter_context(nc.psum_tensor(U(f"lf_p1{i}"), [128, 512], F32)) for i in range(2)]
        p2 = [ps.enter_context(nc.psum_tensor(U(f"lf_p2{i}"), [128, 512], F32)) for i in range(4)]
        with nc.Block() as blk:
            def f_sync(sy):
                for dt in range(DT):
                    ls1.put(sy, s1r[:, dt * 4096:(dt + 1) * 4096], scr["s1_scr"][dt, :, :])
                for ch in range(NCHK):
                    if ch >= 1:
                        sy.wait_ge(d1m.h, L[d1m] + ch * 32)
                    for h in range(2):
                        lx.put(sy, nbuf[:, h * 2048:(h + 1) * 2048].rearrange(
                                   "p (j c) -> p j c", j=4, c=512),
                               xn_scr[h * 4:(h + 1) * 4, :, ch * 512:(ch + 1) * 512].rearrange(
                                   "j p c -> p j c"))
                    for half in range(2):
                        for ht in range(32):
                            k2 = (ch * 2 + half) * 32 + ht
                            if k2 >= 2:
                                sy.wait_ge(d2u.h, L[d2u] + k2 - 1)
                            lw2.put(sy, w2l[k2 % 2][:], scr["s2_scr"][ht, :, :])
            blk.sync(f_sync)

            def f_gp(gp):
                for i in range(NCHK * DT):
                    ch, dt = divmod(i, DT)
                    if i >= 2:
                        gp.wait_ge(d2v.h, L[d2v] + i - 1)
                    lxr.put(gp, xt[i % 2][:], x_src[dt, :, ch * 512:(ch + 1) * 512])
                    if i >= 1:
                        gp.wait_ge(d2v.h, L[d2v] + i)
                        lst.put(gp, x_dst[(i - 1) % DT, :, ((i - 1) // DT) * 512:((i - 1) // DT + 1) * 512],
                                ot[(i - 1) % 2][:])
                i = NCHK * DT
                gp.wait_ge(d2v.h, L[d2v] + i)
                lst.put(gp, x_dst[(i - 1) % DT, :, ((i - 1) // DT) * 512:((i - 1) // DT + 1) * 512],
                        ot[(i - 1) % 2][:])
                lst.wait_all(gp)
            blk.gpsimd(f_gp)

            def f_tensor(te):
                ls1.wait_all(te)
                for ch in range(NCHK):
                    lx.wait_upto(te, ch * 2 + 1)
                    for ht in range(32):
                        ik = ch * 32 + ht
                        if ik >= 2:
                            te.wait_ge(d1v.h, L[d1v] + ik - 1)
                        ins = None
                        for dt in range(DT):
                            ins = te.matmul(p1[ik % 2][:],
                                            s1r[:, dt * 4096 + ht * 128: dt * 4096 + (ht + 1) * 128],
                                            nbuf[:, dt * 512:(dt + 1) * 512],
                                            start=(dt == 0), stop=(dt == DT - 1))
                        d1m.inc(ins)
                    te.wait_ge(d1s.h, L[d1s] + (ch + 1) * 32)
                    for half in range(2):
                        if ch * 2 + half >= 1:
                            te.wait_ge(d2v.h, L[d2v] + (ch * 2 + half) * 4)
                        for ht in range(32):
                            k2 = (ch * 2 + half) * 32 + ht
                            lw2.wait_item(te, k2)
                            ins = None
                            for dth in range(4):
                                dt = half * 4 + dth
                                ins = te.matmul(p2[dth][:],
                                                w2l[k2 % 2][:, dt * 128:(dt + 1) * 128],
                                                g1[:, ht * 512:(ht + 1) * 512],
                                                start=(ht == 0), stop=(ht == 31))
                            d2u.inc(ins)
                        for dth in range(4):
                            d2m.inc(te.matmul(p2[dth][:], w2l[(ch * 2 + half) * 32 % 2][:, 0:128],
                                              g1[:, 0:512], start=False, stop=True)) if False else None
            blk.tensor(f_tensor)

            def f_vector(ve):
                for ch in range(NCHK):
                    for ht in range(32):
                        ik = ch * 32 + ht
                        ve.wait_ge(d1m.h, L[d1m] + ik + 1)
                        if ik >= 2:
                            ve.wait_ge(d1s.h, L[d1s] + ik - 1)
                        d1v.inc(ve.tensor_scalar(tg[ik % 2][:], p1[ik % 2][:],
                                                 col(cst, C_BJ1 + ht), col(cst, C_AL1 + ht),
                                                 ALU.add, ALU.mult))
                    for half in range(2):
                        for dth in range(4):
                            i = ch * 8 + half * 4 + dth
                            dt = half * 4 + dth
                            ve.wait_ge(d2u.h, L[d2u] + (ch * 2 + half + 1) * 32)
                            lxr.wait_item(ve, i)
                            if i >= 2:
                                lst.wait_item(ve, i - 2)
                            ve.tensor_scalar(tv[:], p2[dth][:], col(cst, C_BJ2 + dt),
                                             col(cst, C_AL2 + dt), ALU.add, ALU.mult)
                            d2v.inc(ve.tensor_tensor(ot[i % 2][:], tv[:], xt[i % 2][:], ALU.add))
            blk.vector(f_vector)

            def f_scalar(se):
                for ch in range(NCHK):
                    for ht in range(32):
                        ik = ch * 32 + ht
                        se.wait_ge(d1v.h, L[d1v] + ik + 1)
                        d1s.inc(se.activation(g1[:, ht * 512:(ht + 1) * 512], tg[ik % 2][:], AF.Gelu))
            blk.scalar(f_scalar)

    assert d1m.n == L[d1m] + NCHK * 32 and d2v.n == L[d2v] + NCHK * 8


def emit_l1attn(ctx, dr, xn_scr, x_src, x_dst):
    ctx.new_phase()
    nc = ctx.nc
    cst = ctx.cst
    lwq = Stream(ctx, "la_w", 2, kind="gp")
    lx = Stream(ctx, "la_x", 2)
    lxr = Stream(ctx, "la_xr", 2, kind="gp")
    lst = Stream(ctx, "la_st", 2, kind="gp")
    qm = ctx.S("la_qm")
    qv = ctx.S("la_qv")
    fm = ctx.S("la_fm")
    fv = ctx.S("la_fv")
    fq = ctx.S("la_fq")
    fz = ctx.S("la_fz")
    vm = ctx.S("la_vm")
    vv = ctx.S("la_vv")
    am_ = ctx.S("la_am")
    av = ctx.S("la_av")
    nm = ctx.S("la_nm")
    nv = ctx.S("la_nv")
    km = ctx.S("la_km")
    kv_ = ctx.S("la_kv")
    um = ctx.S("la_um")
    uv = ctx.S("la_uv")
    tm = ctx.S("la_tm")
    tv_ = ctx.S("la_tv")
    wm = ctx.S("la_wm")
    wv_ = ctx.S("la_wv")
    L = {s: s.n for s in (qm, qv, fm, fv, fq, fz, vm, vv, am_, av, nm, nv, km, kv_, um, uv, tm, tv_, wm, wv_)}
    with ExitStack() as ps:
        wsb = {}
        for nmw in ("wq", "wk", "wv", "wo"):
            wsb[nmw] = ps.enter_context(nc.sbuf_tensor(U(f"la_{nmw}"), [128, 8192], BF16))
        nbuf = ps.enter_context(nc.sbuf_tensor(U("la_n"), [128, 4096], BF16))
        qps = ps.enter_context(nc.sbuf_tensor(U("la_q"), [128, 4096], BF16))
        kps = ps.enter_context(nc.sbuf_tensor(U("la_k"), [128, 4096], BF16))
        vps = ps.enter_context(nc.sbuf_tensor(U("la_v"), [128, 4096], BF16))
        aps = ps.enter_context(nc.sbuf_tensor(U("la_a"), [128, 4096], BF16))
        rawb = ps.enter_context(nc.sbuf_tensor(U("la_r"), [128, 512], BF16))
        t5 = ps.enter_context(nc.sbuf_tensor(U("la_t5"), [128, 512], F32))
        t6 = ps.enter_context(nc.sbuf_tensor(U("la_t6"), [128, 512], F32))
        t7 = ps.enter_context(nc.sbuf_tensor(U("la_t7"), [128, 512], F32))
        amb = ps.enter_context(nc.sbuf_tensor(U("la_amb"), [128, 128], BF16))
        knb = ps.enter_context(nc.sbuf_tensor(U("la_kn"), [128, 64], BF16))
        kvf = ps.enter_context(nc.sbuf_tensor(U("la_kvf"), [128, 520], F32))
        kvb = ps.enter_context(nc.sbuf_tensor(U("la_kvb"), [128, 520], BF16))
        dnc = ps.enter_context(nc.sbuf_tensor(U("la_dn"), [128, 2], F32))
        atT = ps.enter_context(nc.sbuf_tensor(U("la_atT"), [128, 1024], BF16))
        xt = [ps.enter_context(nc.sbuf_tensor(U(f"la_xt{i}"), [128, 128], F32)) for i in range(2)]
        ot = [ps.enter_context(nc.sbuf_tensor(U(f"la_ot{i}"), [128, 128], F32)) for i in range(2)]
        PQ = ps.enter_context(nc.psum_tensor(U("la_PQ"), [128, 512], F32))
        PF = ps.enter_context(nc.psum_tensor(U("la_PF"), [128, 512], F32))
        PV = ps.enter_context(nc.psum_tensor(U("la_PV"), [128, 512], F32))
        PA = ps.enter_context(nc.psum_tensor(U("la_PA"), [128, 128], F32))
        PND = ps.enter_context(nc.psum_tensor(U("la_PND"), [128, 66], F32))
        PKV = ps.enter_context(nc.psum_tensor(U("la_PKV"), [128, 65], F32))
        PTT = ps.enter_context(nc.psum_tensor(U("la_PTT"), [128, 192], BF16))
        PW = ps.enter_context(nc.psum_tensor(U("la_PW"), [128, 128], F32))

        with nc.Block() as blk:
            def f_gp(gp):
                for i, nmw in enumerate(("wq", "wk", "wv", "wo")):
                    for h in range(2):
                        lwq.put(gp, wsb[nmw][:, h * 4096:(h + 1) * 4096].rearrange(
                                    "p (j c) -> p j c", j=4, c=1024),
                                dr[nmw][h * 4:(h + 1) * 4, :, :].rearrange("j p c -> p j c"))
                NIT = 2 * 4 * 4 * 8
                for i in range(NIT):
                    if i >= 2:
                        gp.wait_ge(wv_.h, L[wv_] + i - 1)
                    b = i // 128
                    r = i % 128
                    ch, r2 = divmod(r, 32)
                    ul, dto = divmod(r2, 8)
                    tcol = b * 2048 + ch * 512 + ul * 128
                    lxr.put(gp, xt[i % 2][:], x_src[dto, :, tcol:tcol + 128])
                    if i >= 1:
                        gp.wait_ge(wv_.h, L[wv_] + i)
                        j = i - 1
                        bj = j // 128
                        rj = j % 128
                        chj, r2j = divmod(rj, 32)
                        ulj, dtoj = divmod(r2j, 8)
                        tcj = bj * 2048 + chj * 512 + ulj * 128
                        lst.put(gp, x_dst[dtoj, :, tcj:tcj + 128], ot[j % 2][:])
                i = NIT
                gp.wait_ge(wv_.h, L[wv_] + i)
                j = i - 1
                bj = j // 128
                rj = j % 128
                chj, r2j = divmod(rj, 32)
                ulj, dtoj = divmod(r2j, 8)
                tcj = bj * 2048 + chj * 512 + ulj * 128
                lst.put(gp, x_dst[dtoj, :, tcj:tcj + 128], ot[j % 2][:])
                lst.wait_all(gp)
            blk.gpsimd(f_gp)

            def f_sync(sy):
                for b in range(2):
                    for ch in range(4):
                        ic = b * 4 + ch
                        if ic >= 1:
                            sy.wait_ge(vm.h, L[vm] + ic * 8)
                        for h in range(2):
                            lx.put(sy, nbuf[:, h * 2048:(h + 1) * 2048].rearrange(
                                       "p (j c) -> p j c", j=4, c=512),
                                   xn_scr[h * 4:(h + 1) * 4, :, b * 2048 + ch * 512:
                                          b * 2048 + (ch + 1) * 512].rearrange("j p c -> p j c"))
            blk.sync(f_sync)

            def f_tensor(te):
                lwq.wait_all(te)
                for b in range(2):
                    for ch in range(4):
                        ic = b * 4 + ch
                        lx.wait_upto(te, ic * 2 + 1)
                        for pr in range(2):
                            wname = "wq" if pr == 0 else "wk"
                            for dqt in range(DT):
                                iq = (ic * 2 + pr) * 8 + dqt
                                if iq >= 1:
                                    te.wait_ge(qv.h, L[qv] + iq)
                                ins = None
                                for dt in range(DT):
                                    ins = te.matmul(PQ[:], wsb[wname][:, dt * 1024 + dqt * 128:
                                                                      dt * 1024 + (dqt + 1) * 128],
                                                    nbuf[:, dt * 512:(dt + 1) * 512],
                                                    start=(dt == 0), stop=(dt == DT - 1))
                                qm.inc(ins)
                                te.wait_ge(qv.h, L[qv] + iq + 1)
                                if iq >= 1:
                                    te.wait_ge(fv.h, L[fv] + iq)
                                    te.wait_ge(fq.h, L[fq] + iq)
                                fm.inc(te.matmul(PF[:], ctx.cstb[:, B_PROJ:B_PROJ + 128],
                                                 rawb[:], start=True, stop=True))
                        for ul in range(4):
                            for dvh in range(2):
                                iv = ic * 8 + ul * 2 + dvh
                                if iv >= 1:
                                    te.wait_ge(vv.h, L[vv] + iv)
                                ins = None
                                for dt in range(DT):
                                    ins = te.matmul(PV[:],
                                                    nbuf[:, dt * 512 + ul * 128: dt * 512 + (ul + 1) * 128],
                                                    wsb["wv"][:, dt * 1024 + dvh * 512:
                                                              dt * 1024 + (dvh + 1) * 512],
                                                    start=(dt == 0), stop=(dt == DT - 1))
                                vm.inc(ins)
                        te.wait_ge(fz.h, L[fz] + (ic + 1) * 16)
                        for h in range(16):
                            hrow = (h % 2) * 64
                            hcol = (h // 2) * 512
                            for ul in range(4):
                                ia = (ic * 16 + h) * 4 + ul
                                u_abs = ch * 4 + ul
                                if ia >= 1:
                                    te.wait_ge(av.h, L[av] + ia)
                                am_.inc(te.matmul(PA[:],
                                                  kps[hrow:hrow + 64, hcol + ul * 128: hcol + (ul + 1) * 128],
                                                  qps[hrow:hrow + 64, hcol + ul * 128: hcol + (ul + 1) * 128],
                                                  start=True, stop=True))
                                te.wait_ge(av.h, L[av] + ia + 1)
                                if ia >= 1:
                                    te.wait_ge(nv.h, L[nv] + ia)
                                te.matmul(PND[:, 0:64], amb[:],
                                          vps[:, ul * 1024 + h * 64: ul * 1024 + (h + 1) * 64],
                                          start=True, stop=False)
                                ins = te.matmul(PND[:, 64:65], amb[:], ctx.cstb[:, B_ONESC:B_ONESC + 1],
                                                start=False, stop=(u_abs == 0))
                                if u_abs > 0:
                                    te.wait_ge(uv.h, L[uv] + ia)
                                    ins = te.matmul(PND[:, 0:65],
                                                    qps[hrow:hrow + 64, hcol + ul * 128: hcol + (ul + 1) * 128],
                                                    kvb[hrow:hrow + 64, (h // 2) * 65:(h // 2 + 1) * 65],
                                                    start=False, stop=True)
                                nm.inc(ins)
                                if ia >= 1:
                                    te.wait_ge(kv_.h, L[kv_] + ia)
                                km.inc(te.transpose(PTT[:, 0:64],
                                                    kps[hrow:hrow + 64, hcol + ul * 128: hcol + (ul + 1) * 128],
                                                    ctx.cstb[hrow:hrow + 64, B_ID + hrow:B_ID + hrow + 64]))
                                te.wait_ge(kv_.h, L[kv_] + ia + 1)
                                if ia >= 1:
                                    te.wait_ge(uv.h, L[uv] + ia)
                                te.matmul(PKV[hrow:hrow + 64, 0:64], knb[:],
                                          vps[:, ul * 1024 + h * 64: ul * 1024 + (h + 1) * 64],
                                          start=True, stop=False)
                                um.inc(te.matmul(PKV[hrow:hrow + 64, 64:65], knb[:],
                                                 ctx.cstb[:, B_ONESC:B_ONESC + 1],
                                                 start=False, stop=True))
                        te.wait_ge(nv.h, L[nv] + (ic * 16 + 16) * 4)
                        for ul in range(4):
                            for dvt in range(DT):
                                it = (ic * 4 + ul) * 8 + dvt
                                if it >= 1:
                                    te.wait_ge(tv_.h, L[tv_] + it)
                                tm.inc(te.transpose(PTT[:, 64:192],
                                                    aps[:, ul * 1024 + dvt * 128: ul * 1024 + (dvt + 1) * 128],
                                                    ctx.cstb[:, B_ID:B_ID + 128]))
                            te.wait_ge(tv_.h, L[tv_] + (ic * 4 + ul) * 8 + 8)
                            for dto in range(DT):
                                iw = (ic * 4 + ul) * 8 + dto
                                if iw >= 1:
                                    te.wait_ge(wv_.h, L[wv_] + iw)
                                ins = None
                                for dvt in range(DT):
                                    ins = te.matmul(PW[:], wsb["wo"][:, dvt * 1024 + dto * 128:
                                                                     dvt * 1024 + (dto + 1) * 128],
                                                    atT[:, dvt * 128:(dvt + 1) * 128],
                                                    start=(dvt == 0), stop=(dvt == DT - 1))
                                wm.inc(ins)
            blk.tensor(f_tensor)

            def f_vector(ve):
                for b in range(2):
                    for ch in range(4):
                        ic = b * 4 + ch
                        for pr in range(2):
                            dst = qps if pr == 0 else kps
                            for dqt in range(DT):
                                iq = (ic * 2 + pr) * 8 + dqt
                                ve.wait_ge(qm.h, L[qm] + iq + 1)
                                if iq >= 1:
                                    ve.wait_ge(fm.h, L[fm] + iq)
                                qv.inc(ve.tensor_copy(rawb[:], PQ[:]))
                                ve.wait_ge(fm.h, L[fm] + iq + 1)
                                fv.inc(ve.tensor_single_scalar(t5[:], PF[:], 0.0, ALU.min))
                                ve.wait_ge(fq.h, L[fq] + iq + 1)
                                fz.inc(ve.tensor_tensor(dst[:, dqt * 512:(dqt + 1) * 512],
                                                        t6[:], t7[:], ALU.add))
                        for ul in range(4):
                            for dvh in range(2):
                                iv = ic * 8 + ul * 2 + dvh
                                ve.wait_ge(vm.h, L[vm] + iv + 1)
                                vv.inc(ve.tensor_copy(vps[:, ul * 1024 + dvh * 512: ul * 1024 + (dvh + 1) * 512],
                                                      PV[:]))
                        for h in range(16):
                            for ul in range(4):
                                ia = (ic * 16 + h) * 4 + ul
                                u_abs = ch * 4 + ul
                                ve.wait_ge(am_.h, L[am_] + ia + 1)
                                av.inc(ve.tensor_tensor(amb[:], PA[:], ctx.cstb[:, B_TRI:B_TRI + 128],
                                                        ALU.mult))
                                ve.wait_ge(km.h, L[km] + ia + 1)
                                kv_.inc(ve.tensor_copy(knb[:], PTT[:, 0:64]))
                                ve.wait_ge(nm.h, L[nm] + ia + 1)
                                vsync(ctx, ve, ve.tensor_scalar(dnc[:, 0:1], PND[:, 64:65], 1e-6,
                                                                None, ALU.add))
                                vsync(ctx, ve, ve.reciprocal(dnc[:, 1:2], dnc[:, 0:1]))
                                nv.inc(ve.tensor_scalar(aps[:, ul * 1024 + h * 64: ul * 1024 + (h + 1) * 64],
                                                        PND[:, 0:64], dnc[:, 1:2], None, ALU.mult))
                                ve.wait_ge(um.h, L[um] + ia + 1)
                                hr2 = (h % 2) * 64
                                hcc = (h // 2) * 65
                                if u_abs == 0:
                                    vsync(ctx, ve, ve.tensor_copy(kvf[hr2:hr2 + 64, hcc:hcc + 65],
                                                                  PKV[hr2:hr2 + 64, :]))
                                else:
                                    vsync(ctx, ve, ve.tensor_tensor(kvf[hr2:hr2 + 64, hcc:hcc + 65],
                                                                    kvf[hr2:hr2 + 64, hcc:hcc + 65],
                                                                    PKV[hr2:hr2 + 64, :], ALU.add))
                                uv.inc(ve.tensor_copy(kvb[hr2:hr2 + 64, hcc:hcc + 65],
                                                      kvf[hr2:hr2 + 64, hcc:hcc + 65]))
                        for ul in range(4):
                            for dvt in range(DT):
                                it = (ic * 4 + ul) * 8 + dvt
                                ve.wait_ge(tm.h, L[tm] + it + 1)
                                tv_.inc(ve.tensor_copy(atT[:, dvt * 128:(dvt + 1) * 128], PTT[:, 64:192]))
                            for dto in range(DT):
                                iw = (ic * 4 + ul) * 8 + dto
                                ve.wait_ge(wm.h, L[wm] + iw + 1)
                                lxr.wait_item(ve, iw)
                                if iw >= 2:
                                    lst.wait_item(ve, iw - 2)
                                wv_.inc(ve.tensor_tensor(ot[iw % 2][:], PW[:], xt[iw % 2][:], ALU.add))
            blk.vector(f_vector)

            def f_scalar(se):
                for b in range(2):
                    for ch in range(4):
                        ic = b * 4 + ch
                        for pr in range(2):
                            for dqt in range(DT):
                                iq = (ic * 2 + pr) * 8 + dqt
                                se.wait_ge(fv.h, L[fv] + iq + 1)
                                se.activation(t7[:], PF[:], AF.Relu)
                                fq.inc(se.activation(t6[:], t5[:], AF.Exp))
            blk.scalar(f_scalar)


def emit_mamba(ctx, dr, xn_scr, xn1_scr, x_src, x_dst, scr):
    with ExitStack() as ms:
        ctx.BCs = ms.enter_context(ctx.nc.sbuf_tensor(U("g_BCs"), [32, 4096], BF16))
        ctx.selb = ms.enter_context(ctx.nc.sbuf_tensor(U("g_selb"), [32, 4096], BF16))
        emit_mamba_m1(ctx, dr, xn_scr, scr)
        emit_mamba_m2(ctx, dr, scr)
        emit_mamba_m3(ctx, dr, scr)
        emit_mamba_m45(ctx, dr, xn1_scr, x_src, x_dst, scr)


def emit_mamba_m1(ctx, dr, xn_scr, scr):
    ctx.new_phase()
    nc = ctx.nc
    cst = ctx.cst
    m1x = Stream(ctx, "m1_x", 4)
    m1w = Stream(ctx, "m1_w", 4, kind="gp")
    m1s = Stream(ctx, "m1_s", 2, kind="gp")
    mm1 = ctx.S("m1_mm"); mv1 = ctx.S("m1_ve")
    L = {mm1: mm1.n, mv1: mv1.n}
    with ExitStack() as ps:
        xnfull = ps.enter_context(nc.sbuf_tensor(U("m1_xn"), [128, 32768], BF16))
        wt = [ps.enter_context(nc.sbuf_tensor(U(f"m1_w{i}"), [128, 1024], BF16)) for i in range(2)]
        evb = [ps.enter_context(nc.sbuf_tensor(U(f"m1_e{i}"), [128, 4096], BF16)) for i in range(2)]
        PM = ps.enter_context(nc.psum_tensor(U("m1_pm"), [128, 4096], F32))
        with nc.Block() as blk:
            def f_sync(sy):
                for dt in range(DT):
                    m1x.put(sy, xnfull[:, dt * 4096:(dt + 1) * 4096], xn_scr[dt, :, :])
            blk.sync(f_sync)

            def f_gp(gp):
                for ct in range(32):
                    if ct >= 2:
                        gp.wait_ge(mm1.h, L[mm1] + ct - 1)
                    for hh in range(2):
                        m1w.put(gp, wt[ct % 2][:, hh * 512:(hh + 1) * 512].rearrange(
                                    "p (j c) -> p j c", j=4, c=128),
                                dr["ipjt"][hh * 4:(hh + 1) * 4, :, ct * 128:(ct + 1) * 128].rearrange(
                                    "j p c -> p j c"))
                    if ct >= 1:
                        cts = ct - 1
                        gp.wait_ge(mv1.h, L[mv1] + cts + 1)
                        dst = scr["xc_scr"][cts, :, :] if cts < 16 else scr["z_scr"][cts - 16, :, :]
                        m1s.put(gp, dst, evb[cts % 2][:])
                gp.wait_ge(mv1.h, L[mv1] + 32)
                m1s.put(gp, scr["z_scr"][15, :, :], evb[31 % 2][:])
                m1s.wait_all(gp)
            blk.gpsimd(f_gp)

            def f_tensor(te):
                m1x.wait_all(te)
                for ct in range(32):
                    m1w.wait_item(te, 2 * ct)
                    m1w.wait_item(te, 2 * ct + 1)
                    if ct >= 1:
                        te.wait_ge(mv1.h, L[mv1] + ct)
                    ins = None
                    for dt in range(DT):
                        for ch in range(NCHK):
                            ins = te.matmul(PM[:, ch * 512:(ch + 1) * 512],
                                            wt[ct % 2][:, dt * 128:(dt + 1) * 128],
                                            xnfull[:, dt * 4096 + ch * 512: dt * 4096 + (ch + 1) * 512],
                                            start=(dt == 0), stop=(dt == DT - 1))
                    mm1.inc(ins)
            blk.tensor(f_tensor)

            def f_vector(ve):
                for ct in range(32):
                    ve.wait_ge(mm1.h, L[mm1] + ct + 1)
                    if ct >= 2:
                        m1s.wait_item(ve, ct - 2)
                    mv1.inc(ve.tensor_scalar(evb[ct % 2][:], PM[:], col(cst, C_IPB + ct), None, ALU.add))
            blk.vector(f_vector)


def emit_mamba_m2(ctx, dr, scr):
    ctx.new_phase()
    nc = ctx.nc
    cst = ctx.cst
    m2x = Stream(ctx, "m2_x", 2)
    m2s = Stream(ctx, "m2_s", 2, kind="gp")
    cv = ctx.S("m2_ve"); cq = ctx.S("m2_se")
    L2 = {cv: cv.n, cq: cq.n}
    with ExitStack() as ps:
        xcb = [ps.enter_context(nc.sbuf_tensor(U(f"m2_x{i}"), [128, 2048], BF16)) for i in range(2)]
        aca = ps.enter_context(nc.sbuf_tensor(U("m2_a"), [128, 2048], F32))
        acb = ps.enter_context(nc.sbuf_tensor(U("m2_b"), [128, 2048], F32))
        ub = [ps.enter_context(nc.sbuf_tensor(U(f"m2_u{i}"), [128, 2048], BF16)) for i in range(2)]
        with nc.Block() as blk:
            def f_sync(sy):
                for i in range(32):
                    b, ct = divmod(i, 16)
                    if i >= 2:
                        sy.wait_ge(cv.h, L2[cv] + i - 1)
                    m2x.put(sy, xcb[i % 2][:], scr["xc_scr"][ct, :, b * 2048:(b + 1) * 2048])
            blk.sync(f_sync)

            def f_vector(ve):
                for i in range(32):
                    b, ct = divmod(i, 16)
                    m2x.wait_item(ve, i)
                    if i >= 1:
                        ve.wait_ge(cq.h, L2[cq] + i)
                    w = [col(cst, C_CONVW + ct * 4 + j) for j in range(4)]
                    bcl = col(cst, C_CONVB + ct)
                    xc = xcb[i % 2]
                    ve.tensor_scalar(aca[:, 3:2048], xc[:, 0:2045], w[0], bcl, ALU.mult, ALU.add)
                    ve.scalar_tensor_tensor(acb[:, 3:2048], xc[:, 1:2046], w[1], aca[:, 3:2048], ALU.mult, ALU.add)
                    ve.tensor_scalar(aca[:, 3:2048], xc[:, 2:2047], w[2], None, ALU.mult)
                    ve.tensor_tensor(acb[:, 3:2048], acb[:, 3:2048], aca[:, 3:2048], ALU.add)
                    ve.scalar_tensor_tensor(acb[:, 3:2048], xc[:, 3:2048], w[3], acb[:, 3:2048], ALU.mult, ALU.add)
                    vsync(ctx, ve, ve.tensor_scalar(acb[:, 0:1], xc[:, 0:1], w[3], bcl, ALU.mult, ALU.add))
                    vsync(ctx, ve, ve.tensor_scalar(acb[:, 1:2], xc[:, 1:2], w[3], bcl, ALU.mult, ALU.add))
                    vsync(ctx, ve, ve.scalar_tensor_tensor(acb[:, 1:2], xc[:, 0:1], w[2], acb[:, 1:2], ALU.mult, ALU.add))
                    vsync(ctx, ve, ve.tensor_scalar(acb[:, 2:3], xc[:, 2:3], w[3], bcl, ALU.mult, ALU.add))
                    vsync(ctx, ve, ve.scalar_tensor_tensor(acb[:, 2:3], xc[:, 1:2], w[2], acb[:, 2:3], ALU.mult, ALU.add))
                    vsync(ctx, ve, ve.scalar_tensor_tensor(acb[:, 2:3], xc[:, 0:1], w[1], acb[:, 2:3], ALU.mult, ALU.add))
                    cv.inc(ve.engine_nop())
            blk.vector(f_vector)

            def f_scalar(se):
                for i in range(32):
                    se.wait_ge(cv.h, L2[cv] + i + 1)
                    if i >= 2:
                        m2s.wait_item(se, i - 2)
                    cq.inc(se.activation(ub[i % 2][:], acb[:], AF.Silu))
            blk.scalar(f_scalar)

            def f_gp(gp):
                for i in range(32):
                    b, ct = divmod(i, 16)
                    gp.wait_ge(cq.h, L2[cq] + i + 1)
                    m2s.put(gp, scr["u_scr"][ct, :, b * 2048:(b + 1) * 2048], ub[i % 2][:])
                m2s.wait_all(gp)
            blk.gpsimd(f_gp)


def emit_mamba_m3(ctx, dr, scr):
    ctx.new_phase()
    nc = ctx.nc
    cst = ctx.cst
    m3u = Stream(ctx, "m3_u", 4)
    m3w = Stream(ctx, "m3_w", 8, kind="gp")
    m3s = Stream(ctx, "m3_s", 2)
    m3wx = Stream(ctx, "m3_wx", 4, kind="gp")
    tmm = ctx.S("m3_mm"); tve = ctx.S("m3_ve"); xmm = ctx.S("m3_xm"); xve = ctx.S("m3_xv")
    L3 = {tmm: tmm.n, tve: tve.n, xmm: xmm.n, xve: xve.n}
    with ExitStack() as ps:
        ubf = ps.enter_context(nc.sbuf_tensor(U("m3_u"), [128, 32768], BF16))
        xpj = ps.enter_context(nc.sbuf_tensor(U("m3_xp"), [128, 512], BF16))
        dtw = [ps.enter_context(nc.sbuf_tensor(U(f"m3_w{i}"), [128, 2048], BF16)) for i in range(2)]
        ta2 = ps.enter_context(nc.sbuf_tensor(U("m3_t1"), [128, 2048], F32))
        tng = ps.enter_context(nc.sbuf_tensor(U("m3_t4"), [128, 2048], F32))
        tr = ps.enter_context(nc.sbuf_tensor(U("m3_t2"), [128, 2048], F32))
        te_ = ps.enter_context(nc.sbuf_tensor(U("m3_t3"), [128, 2048], F32))
        dlb = [ps.enter_context(nc.sbuf_tensor(U(f"m3_d{i}"), [128, 2048], BF16)) for i in range(2)]
        PX = ps.enter_context(nc.psum_tensor(U("m3_px"), [32, 2048], F32))
        PD = ps.enter_context(nc.psum_tensor(U("m3_pd"), [128, 2048], F32))
        with nc.Block() as blk:
            def f_gp(gp):
                for hh in range(4 if M3LVL >= 1 else 0):
                    m3wx.put(gp, xpj[:, hh * 128:(hh + 1) * 128].rearrange(
                                 "p (j c) -> p j c", j=4, c=32),
                             dr["xpjt"][hh * 4:(hh + 1) * 4, :, :].rearrange("j p c -> p j c"))
                for b in range(2):
                    for cot in range(M3N if M3LVL >= 2 else 0):
                        k = b * M3N + cot
                        if k >= 2:
                            gp.wait_ge(tmm.h, L3[tmm] + k - 1)
                        for hh in range(4):
                            m3w.put(gp, dtw[k % 2][:, hh * 512:(hh + 1) * 512].rearrange(
                                        "p (j c) -> p j c", j=4, c=128),
                                    dr["dtjt"][hh * 4:(hh + 1) * 4, :, cot * 128:(cot + 1) * 128].rearrange(
                                        "j p c -> p j c"))
            blk.gpsimd(f_gp)

            def f_sync(sy):
                for b in range(2):
                    if b == 1:
                        sy.wait_ge(tmm.h, L3[tmm] + M3N)
                    for ct in range(CIT):
                        m3u.put(sy, ubf[:, ct * 2048:(ct + 1) * 2048],
                                scr["u_scr"][ct, :, b * 2048:(b + 1) * 2048])
                    for cot in range(M3N if M3LVL >= 2 else 0):
                        k = b * M3N + cot
                        sy.wait_ge(tve.h, L3[tve] + k + 1)
                        m3s.put(sy, scr["dl_scr"][cot, :, b * 2048:(b + 1) * 2048], dlb[k % 2][:])
                m3s.wait_all(sy)
            blk.sync(f_sync)

            va = ctx.S("m3_va")
            sr = ctx.S("m3_sr")
            sl = ctx.S("m3_sl")
            L3b = {va: va.n, sr: sr.n, sl: sl.n}

            def f_tensor(te):
                m3wx.wait_all(te)
                for b in range(2):
                    m3u.wait_upto(te, (b + 1) * CIT - 1)
                    if b == 1 and M3LVL >= 1:
                        te.wait_ge(xve.h, L3[xve] + 1)
                    ins = None
                    for q in range(4 if M3LVL >= 1 else 0):
                        for cit in range(CIT):
                            ins = te.matmul(PX[:, q * 512:(q + 1) * 512],
                                            xpj[:, cit * 32:(cit + 1) * 32],
                                            ubf[:, cit * 2048 + q * 512: cit * 2048 + (q + 1) * 512],
                                            start=(cit == 0), stop=(cit == CIT - 1))
                    if M3LVL >= 1:
                        xmm.inc(ins)
                    for cot in range(M3N if M3LVL >= 2 else 0):
                        k = b * M3N + cot
                        for hh in range(4):
                            m3w.wait_item(te, 4 * k + hh)
                        if k >= 1:
                            te.wait_ge(va.h, L3b[va] + k)
                            te.wait_ge(sr.h, L3b[sr] + k)
                        ins = None
                        for q in range(4):
                            for cit in range(CIT):
                                ins = te.matmul(PD[:, q * 512:(q + 1) * 512],
                                                dtw[k % 2][:, cit * 128:(cit + 1) * 128],
                                                ubf[:, cit * 2048 + q * 512: cit * 2048 + (q + 1) * 512],
                                                start=(cit == 0), stop=(cit == CIT - 1))
                        tmm.inc(ins)
            blk.tensor(f_tensor)

            def f_vector(ve):
                for b in range(2):
                    if M3LVL >= 1:
                        ve.wait_ge(xmm.h, L3[xmm] + b + 1)
                        xve.inc(ve.tensor_scalar(ctx.BCs[0:32, b * 2048:(b + 1) * 2048], PX[:],
                                                 cst[0:32, C_XPB:C_XPB + 1], None, ALU.add))
                    for cot in range(M3N if M3LVL >= 2 else 0):
                        k = b * M3N + cot
                        ve.wait_ge(tmm.h, L3[tmm] + k + 1)
                        if M3LVL == 2:
                            if k >= 2:
                                m3s.wait_item(ve, k - 2)
                            va.inc(ve.engine_nop())
                            sr.inc(ve.engine_nop())
                            sl.inc(ve.engine_nop())
                            tve.inc(ve.tensor_scalar(dlb[k % 2][:], PD[:],
                                                     col(cst, C_DTB + cot), None, ALU.add))
                            continue
                        if k >= 1:
                            ve.wait_ge(sl.h, L3b[sl] + k)
                        ve.tensor_scalar(ta2[:], PD[:], col(cst, C_DTB + cot), None, ALU.add)
                        ve.tensor_scalar(tng[:], ta2[:], -1.0, None, ALU.mult)
                        va.inc(ve.tensor_tensor(ta2[:], ta2[:], tng[:], ALU.max))
                        ve.wait_ge(sl.h, L3b[sl] + k + 1)
                        if k >= 2:
                            m3s.wait_item(ve, k - 2)
                        tve.inc(ve.tensor_tensor(dlb[k % 2][:], te_[:], tr[:], ALU.add))
            blk.vector(f_vector)

            def f_scalar(se):
                for b in range(2):
                    for cot in range(M3N if M3LVL >= 3 else 0):
                        k = b * M3N + cot
                        se.wait_ge(va.h, L3b[va] + k + 1)
                        if k >= 1:
                            se.wait_ge(tve.h, L3[tve] + k)
                        se.activation(tr[:], PD[:], AF.Relu, bias=col(cst, C_DTB + cot))
                        sr.inc(se.activation(te_[:], ta2[:], AF.Exp, scale=col(cst, C_MONE)))
                        sl.inc(se.activation(te_[:], te_[:], AF.Ln, bias=col(cst, C_ONES)))
            blk.scalar(f_scalar)


def emit_mamba_m45(ctx, dr, xn1_scr, x_src, x_dst, scr):
    ctx.new_phase()
    nc = ctx.nc
    cst = ctx.cst
    selb = ctx.selb
    m4sel = Stream(ctx, "m4_sel", 1, kind="gp")
    with nc.Block() as blk:
        def f_gp(gp):
            m4sel.put(gp, selb[:], dr["sel32"][:, :])
        blk.gpsimd(f_gp)

    m4d = Stream(ctx, "m4_d", 2)
    m4u = Stream(ctx, "m4_u", 2)
    m4z = Stream(ctx, "m4_z", 2)
    m4x = Stream(ctx, "m4_x", 2)
    m4n = Stream(ctx, "m4_n", 2)
    m4st = Stream(ctx, "m4_st", 2, kind="gp")
    m4w = Stream(ctx, "m4_w", 1, kind="gp")
    bmm = ctx.S("m4_bm")
    sda = ctx.S("m4_da")
    vsc = ctx.S("m4_vs")
    vpb = ctx.S("m4_pb")
    vyq = ctx.S("m4_yq")
    sz1 = ctx.S("m4_sz")
    omm = ctx.S("m5_mm")
    ove = ctx.S("m5_ve")
    L4 = {s: s.n for s in (bmm, sda, vsc, vpb, vyq, sz1, omm, ove)}
    with ExitStack() as ps:
        ystash = ps.enter_context(nc.sbuf_tensor(U("m4_y"), [128, 32768], BF16))
        with nc.Block() as blk:
            pass
        for b in range(2):
            with ExitStack() as p4:
                dlt = [p4.enter_context(nc.sbuf_tensor(U(f"m4_d{i}"), [128, 2048], BF16)) for i in range(2)]
                utt = [p4.enter_context(nc.sbuf_tensor(U(f"m4_u{i}"), [128, 2048], BF16)) for i in range(2)]
                dub = p4.enter_context(nc.sbuf_tensor(U("m4_du"), [128, 2048], BF16))
                dAb = p4.enter_context(nc.sbuf_tensor(U("m4_dA"), [128, 2048], BF16))
                tb2 = p4.enter_context(nc.sbuf_tensor(U("m4_t2"), [128, 2048], BF16))
                hb = p4.enter_context(nc.sbuf_tensor(U("m4_h"), [128, 2048], BF16))
                tm2 = p4.enter_context(nc.sbuf_tensor(U("m4_tm"), [128, 2048], F32))
                ysb = p4.enter_context(nc.sbuf_tensor(U("m4_ys"), [128, 2048], F32))
                ztb = [p4.enter_context(nc.sbuf_tensor(U(f"m4_z{i}"), [128, 2048], BF16)) for i in range(2)]
                szb = p4.enter_context(nc.sbuf_tensor(U("m4_sz"), [128, 2048], BF16))
                tt3 = p4.enter_context(nc.sbuf_tensor(U("m4_t3"), [128, 2048], BF16))
                pB = p4.enter_context(nc.psum_tensor(U("m4_pB"), [128, 2048], F32))
                pC = p4.enter_context(nc.psum_tensor(U("m4_pC"), [128, 2048], F32))
                with nc.Block() as blk:
                    def f_sync(sy):
                        for dt in range(CIT):
                            i = b * CIT + dt
                            if dt >= 2:
                                sy.wait_ge(vsc.h, L4[vsc] + (i - 1) * 16)
                            m4d.put(sy, dlt[dt % 2][:], scr["dl_scr"][dt, :, b * 2048:(b + 1) * 2048])
                            if dt >= 2:
                                sy.wait_ge(vyq.h, L4[vyq] + i - 1)
                            m4u.put(sy, utt[dt % 2][:], scr["u_scr"][dt, :, b * 2048:(b + 1) * 2048])
                            if dt >= 2:
                                sy.wait_ge(sz1.h, L4[sz1] + i - 1)
                            m4z.put(sy, ztb[dt % 2][:], scr["z_scr"][dt, :, b * 2048:(b + 1) * 2048])
                    blk.sync(f_sync)

                    def f_tensor(te):
                        m4sel.wait_all(te)
                        for dt in range(CIT):
                            i = b * CIT + dt
                            for s in range(16):
                                si = i * 16 + s
                                if si >= 1:
                                    te.wait_ge(vpb.h, L4[vpb] + si)
                                ins = None
                                for q in range(4):
                                    te.matmul(pB[:, q * 512:(q + 1) * 512],
                                              selb[0:32, s * 128:(s + 1) * 128],
                                              ctx.BCs[0:32, b * 2048 + q * 512: b * 2048 + (q + 1) * 512],
                                              start=True, stop=True)
                                for q in range(4):
                                    ins = te.matmul(pC[:, q * 512:(q + 1) * 512],
                                                    selb[0:32, (16 + s) * 128:(17 + s) * 128],
                                                    ctx.BCs[0:32, b * 2048 + q * 512: b * 2048 + (q + 1) * 512],
                                                    start=True, stop=True)
                                bmm.inc(ins)
                    blk.tensor(f_tensor)

                    def f_scalar(se):
                        for dt in range(CIT):
                            i = b * CIT + dt
                            m4d.wait_item(se, i)
                            for s in range(16):
                                si = i * 16 + s
                                if si >= 1:
                                    se.wait_ge(vsc.h, L4[vsc] + si)
                                sda.inc(se.activation(dAb[:], dlt[dt % 2][:], AF.Exp,
                                                      scale=col(cst, C_A + dt * 16 + s)))
                            m4z.wait_item(se, i)
                            se.wait_ge(vyq.h, L4[vyq] + i)
                            sz1.inc(se.activation(szb[:], ztb[dt % 2][:], AF.Silu))
                    blk.scalar(f_scalar)

                    def f_vector(ve):
                        for dt in range(CIT):
                            i = b * CIT + dt
                            m4d.wait_item(ve, i)
                            m4u.wait_item(ve, i)
                            ve.tensor_tensor(dub[:], dlt[dt % 2][:], utt[dt % 2][:], ALU.mult)
                            for s in range(16):
                                si = i * 16 + s
                                ve.wait_ge(bmm.h, L4[bmm] + si + 1)
                                ve.tensor_tensor(tb2[:], dub[:], pB[:], ALU.mult)
                                ve.wait_ge(sda.h, L4[sda] + si + 1)
                                vsync(ctx, ve, ve.tensor_tensor_scan(hb[:], dAb[:], tb2[:], 0.0,
                                                                     ALU.mult, ALU.add), "m4_vp")
                                if s == 0:
                                    ins = ve.tensor_tensor(ysb[:], hb[:], pC[:], ALU.mult)
                                else:
                                    ve.tensor_tensor(tm2[:], hb[:], pC[:], ALU.mult)
                                    ins = ve.tensor_tensor(ysb[:], ysb[:], tm2[:], ALU.add)
                                vpb.inc(ins)
                                vsc.inc(ve.engine_nop())
                            ve.wait_ge(sz1.h, L4[sz1] + i + 1)
                            ve.scalar_tensor_tensor(tt3[:], utt[dt % 2][:], col(cst, C_D + dt),
                                                    ysb[:], ALU.mult, ALU.add)
                            vyq.inc(ve.tensor_tensor(ystash[:, dt * 2048:(dt + 1) * 2048],
                                                     tt3[:], szb[:], ALU.mult))
                    blk.vector(f_vector)

            with ExitStack() as p5:
                opw = p5.enter_context(nc.sbuf_tensor(U("m5_w"), [128, 16384], BF16))
                xt5 = [p5.enter_context(nc.sbuf_tensor(U(f"m5_x{i}"), [128, 2048], F32)) for i in range(2)]
                nt5 = [p5.enter_context(nc.sbuf_tensor(U(f"m5_n{i}"), [128, 2048], BF16)) for i in range(2)]
                ov5 = [p5.enter_context(nc.sbuf_tensor(U(f"m5_o{i}"), [128, 2048], F32)) for i in range(2)]
                tv5 = p5.enter_context(nc.sbuf_tensor(U("m5_t"), [128, 2048], F32))
                PO = p5.enter_context(nc.psum_tensor(U("m5_po"), [128, 2048], F32))
                with nc.Block() as blk:
                    def f_gp(gp):
                        for h in range(4):
                            m4w.put(gp, opw[:, h * 4096:(h + 1) * 4096].rearrange(
                                        "p (j c) -> p j c", j=4, c=1024),
                                    dr["opjt"][h * 4:(h + 1) * 4, :, :].rearrange("j p c -> p j c"))
                        for dto in range(DT):
                            i5 = b * DT + dto
                            gp.wait_ge(ove.h, L4[ove] + i5 + 1)
                            m4st.put(gp, x_dst[dto, :, b * 2048:(b + 1) * 2048], ov5[dto % 2][:])
                        m4st.wait_all(gp)
                    blk.gpsimd(f_gp)

                    def f_sync(sy):
                        for dto in range(DT):
                            i5 = b * DT + dto
                            if dto >= 2:
                                sy.wait_ge(ove.h, L4[ove] + i5 - 1)
                            m4x.put(sy, xt5[dto % 2][:], x_src[dto, :, b * 2048:(b + 1) * 2048])
                            m4n.put(sy, nt5[dto % 2][:], xn1_scr[dto, :, b * 2048:(b + 1) * 2048])
                    blk.sync(f_sync)

                    def f_tensor(te):
                        m4w.wait_all(te)
                        for dto in range(DT):
                            i5 = b * DT + dto
                            if dto >= 1:
                                te.wait_ge(ove.h, L4[ove] + i5)
                            ins = None
                            for q in range(4):
                                for cit in range(CIT):
                                    ins = te.matmul(PO[:, q * 512:(q + 1) * 512],
                                                    opw[:, cit * 1024 + dto * 128: cit * 1024 + (dto + 1) * 128],
                                                    ystash[:, cit * 2048 + q * 512: cit * 2048 + (q + 1) * 512],
                                                    start=(cit == 0), stop=(cit == CIT - 1))
                            omm.inc(ins)
                    blk.tensor(f_tensor)

                    def f_vector(ve):
                        for dto in range(DT):
                            i5 = b * DT + dto
                            ve.wait_ge(omm.h, L4[omm] + i5 + 1)
                            m4x.wait_item(ve, i5)
                            m4n.wait_item(ve, i5)
                            if dto >= 2:
                                m4st.wait_item(ve, i5 - 2)
                            ve.tensor_scalar(tv5[:], PO[:], col(cst, C_OPB + dto), None, ALU.add)
                            ve.tensor_tensor(tv5[:], tv5[:], nt5[dto % 2][:], ALU.add)
                            ove.inc(ve.tensor_tensor(ov5[dto % 2][:], tv5[:], xt5[dto % 2][:], ALU.add))
                    blk.vector(f_vector)





# version 02
# speedup vs baseline: 1.0550x; 1.0550x over previous
import numpy as np
import ml_dtypes
from contextlib import ExitStack

import concourse.bass as bass
import concourse.mybir as mybir
from concourse.bass_utils import run_bass_kernel_spmd

F32 = mybir.dt.float32
BF16 = mybir.dt.bfloat16
AF = mybir.ActivationFunctionType
ALU = mybir.AluOpType
AX = mybir.AxisListType

NCORES = 8
T = 4096
SEQ = 2048
D = 1024
DT = 8
CI = 2048
CIT = 16
NE = 8
TCH = 512
NCHK = T // TCH

SUBS = [0, 1, 2, 3, 4, 5]
M3LVL = 3
M3N = 16
MODE = "fused"

BF = ml_dtypes.bfloat16

_UQ = [0]


def U(nm):
    _UQ[0] += 1
    return f"{nm}_u{_UQ[0]}"


C_ONES, C_MHALF, C_EPS5, C_MONE = 0, 1, 2, 3
C_1702 = 20
C_IOTA, C_IOTA1K = 4, 12
C_A = 32
C_D = 288
C_CONVW = 304
C_CONVB = 368
C_IPB = 384
C_DTB = 416
C_OPB = 432
C_NW = 440
C_NB = 448
C_B1M = 456
C_L1B1 = 712
C_L1B2 = 744
C_XPB = 752
C_AL1 = 768
C_BJ1 = 800
C_AL2 = 832
C_BJ2 = 840

B_ID = 0
B_TRI = 128
B_PROJ = 256
B_SEL2 = 384
B_SEL8 = 640
B_ONESC = 1664
B_ONER = 1666


def _moe_perm():
    perm = np.empty(4096, np.int64)
    for k in range(16):
        p = np.arange(128 * k, 128 * (k + 1))
        perm[256 * k:256 * k + 128] = 2 * p
        perm[256 * k + 128:256 * k + 256] = 2 * p + 1
    return perm


def _hbuild_consts(w):
    cst = np.zeros((128, 1024), np.float32)
    cst[:, C_ONES] = 1.0
    cst[:, C_MHALF] = -0.5
    cst[:, C_EPS5] = 1e-5
    cst[:, C_MONE] = -1.0
    cst[:, C_1702] = 1.702
    cst[:, C_IOTA:C_IOTA + 8] = np.arange(8, dtype=np.float32)[None, :]
    cst[:, C_IOTA1K:C_IOTA1K + 8] = np.arange(8, dtype=np.float32)[None, :] + 1000.0
    cst[:, C_A:C_A + 256] = np.asarray(w["L0_attn_A_log"]).reshape(16, 128, 16).transpose(1, 0, 2).reshape(128, 256)
    cst[:, C_D:C_D + 16] = np.asarray(w["L0_attn_D_param"]).reshape(16, 128).T
    cst[:, C_CONVW:C_CONVW + 64] = np.asarray(w["L0_attn_conv_w"]).reshape(2048, 4).reshape(16, 128, 4).transpose(1, 0, 2).reshape(128, 64)
    cst[:, C_CONVB:C_CONVB + 16] = np.asarray(w["L0_attn_conv_b"]).reshape(16, 128).T
    cst[:, C_IPB:C_IPB + 32] = np.asarray(w["L0_attn_in_proj_b"]).reshape(32, 128).T
    cst[:, C_DTB:C_DTB + 16] = np.asarray(w["L0_attn_dt_proj_b"]).reshape(16, 128).T
    cst[:, C_OPB:C_OPB + 8] = np.asarray(w["L0_attn_out_proj_b"]).reshape(8, 128).T
    cst[:, C_NW:C_NW + 8] = np.asarray(w["L0_attn_norm_w"]).reshape(8, 128).T
    cst[:, C_NB:C_NB + 8] = np.asarray(w["L0_attn_norm_b"]).reshape(8, 128).T
    b1p = np.asarray(w["L0_ffn_dense_1_b"])[:, _moe_perm()]
    cst[:, C_B1M:C_B1M + 256] = b1p.reshape(8, 32, 128).transpose(2, 0, 1).reshape(128, 256)
    cst[:, C_L1B1:C_L1B1 + 32] = np.asarray(w["L1_ffn_b1"]).reshape(32, 128).T
    cst[:, C_L1B2:C_L1B2 + 8] = np.asarray(w["L1_ffn_b2"]).reshape(8, 128).T
    cst[0:32, C_XPB] = np.asarray(w["L0_attn_x_proj_b"])

    cstb = np.zeros((128, 2048), BF)
    cstb[:, B_ID:B_ID + 128] = np.eye(128, dtype=BF)
    cstb[:, B_TRI:B_TRI + 128] = (np.arange(128)[None, :] >= np.arange(128)[:, None]).astype(BF)
    for r in range(2):
        cstb[r, B_SEL2 + r * 128:B_SEL2 + (r + 1) * 128] = BF(1.0)
    for r in range(8):
        cstb[r, B_SEL8 + r * 128:B_SEL8 + (r + 1) * 128] = BF(1.0)
    cstb[:, B_ONESC] = BF(1.0)
    cstb[0, B_ONER:B_ONER + 128] = BF(1.0)
    return cst, cstb


def _host_prep(inp):
    w = inp
    perm = _moe_perm()
    d = {}
    d["cst"], d["cstb"] = _hbuild_consts(w)
    x = np.ascontiguousarray(np.asarray(inp["L0_attn_x"], np.float32))
    xs = []
    for c in range(NCORES):
        xc = x[2 * c:2 * c + 2].reshape(T, D).T
        xs.append(np.ascontiguousarray(xc).reshape(8, 128, T))
    d["_x_cores"] = xs
    w1t = np.ascontiguousarray(np.asarray(w["L0_ffn_dense_1_w"]).transpose(0, 2, 1))
    w1t = w1t[:, :, perm]
    d["moe_w1t"] = np.ascontiguousarray(w1t.reshape(8, 8, 128, 4096))
    d["moe_w2t"] = np.ascontiguousarray(np.asarray(w["L0_ffn_dense_2_w"]).transpose(0, 2, 1)).reshape(8, 16, 128, 1024)
    d["moe_b2"] = np.ascontiguousarray(np.asarray(w["L0_ffn_dense_2_b"]))
    d["gate_w"] = np.ascontiguousarray(np.asarray(w["L0_ffn_gate_w"])).reshape(8, 128, 8)
    d["ipjt"] = np.ascontiguousarray(np.asarray(w["L0_attn_in_proj_w"]).T).reshape(8, 128, 4096)
    d["xpjt"] = np.ascontiguousarray(np.asarray(w["L0_attn_x_proj_w"]).T).reshape(16, 128, 32)
    d["dtjt"] = np.ascontiguousarray(np.asarray(w["L0_attn_dt_proj_w"]).T).reshape(16, 128, 2048)
    d["opjt"] = np.ascontiguousarray(np.asarray(w["L0_attn_out_proj_w"]).T).reshape(16, 128, 1024)
    d["wq"] = np.ascontiguousarray(np.asarray(w["L1_attn_Wq"])).reshape(8, 128, 1024)
    d["wk"] = np.ascontiguousarray(np.asarray(w["L1_attn_Wk"])).reshape(8, 128, 1024)
    d["wv"] = np.ascontiguousarray(np.asarray(w["L1_attn_Wv"])).reshape(8, 128, 1024)
    d["wo"] = np.ascontiguousarray(np.asarray(w["L1_attn_Wo"])).reshape(8, 128, 1024)
    pb = np.zeros((128, 128), np.float32)
    pb[0:64, 0:64] = np.asarray(w["L1_attn_proj"])
    pb[64:128, 64:128] = np.asarray(w["L1_attn_proj"])
    d["projbd"] = pb
    d["w1t_l1"] = np.ascontiguousarray(np.asarray(w["L1_ffn_w1"]).T).reshape(8, 128, 4096)
    d["w2t_l1"] = np.ascontiguousarray(np.asarray(w["L1_ffn_w2"]).T).reshape(32, 128, 1024)
    sel32 = np.zeros((32, 4096), BF)
    for r in range(32):
        sel32[r, r * 128:(r + 1) * 128] = BF(1.0)
    d["sel32"] = sel32
    d["idf32"] = np.eye(128, dtype=np.float32)
    return d


class Sem:
    def __init__(self, h):
        self.h = h
        self.n = 0

    def inc(self, instr, k=1):
        self.n += k
        instr.then_inc(self.h, k)
        return instr


class Stream:

    def __init__(self, ctx, name, k, kind="hw"):
        self.sems = [ctx.S(f"{name}_{i}", kind) for i in range(k)]
        self.base = [s.n for s in self.sems]
        self.k = k
        self.kind = kind
        self.n = 0

    def put(self, q, out, in_):
        assert (type(q).__name__ == "BassGpSimd") == (self.kind == "gp"), self.kind
        j = self.n % self.k
        s = self.sems[j]
        if self.n >= self.k:
            q.wait_ge(s.h, self.base[j] + (self.n // self.k) * 16)
        s.inc(q.dma_start(out, in_), 16)
        self.n += 1
        return self.n - 1

    def wait_item(self, eng, i):
        j = i % self.k
        eng.wait_ge(self.sems[j].h, self.base[j] + (i // self.k + 1) * 16)

    def wait_upto(self, eng, i):
        for j in range(self.k):
            cnt = (i - j) // self.k + 1
            if cnt > 0:
                eng.wait_ge(self.sems[j].h, self.base[j] + cnt * 16)

    def wait_all(self, eng):
        if self.n > 0:
            self.wait_upto(eng, self.n - 1)


class Ctx:
    def __init__(self, nc, st):
        self.nc = nc
        self.st = st
        self._pools = {}
        self._phase_maps = {}

    def new_phase(self):
        self._phase_maps = {}

    def S(self, name, kind="cc"):
        pm = self._phase_maps.setdefault(kind, {})
        if name not in pm:
            pool = self._pools.setdefault(kind, [])
            i = len(pm)
            if i >= len(pool):
                pool.append(Sem(self.st.enter_context(self.nc.semaphore(f"sp_{kind}{i}"))))
            pm[name] = pool[i]
        return pm[name]

    def alloc_global(self):
        st, nc = self.st, self.nc
        self.cst = st.enter_context(nc.sbuf_tensor("sb_cst", [128, 1024], F32))
        self.cstb = st.enter_context(nc.sbuf_tensor("sb_cstb", [128, 2048], BF16))
        self.ta = st.enter_context(nc.sbuf_tensor("g_ta", [128, 512], F32))
        self.tb = st.enter_context(nc.sbuf_tensor("g_tb", [128, 512], F32))
        self.mT = st.enter_context(nc.sbuf_tensor("g_mT", [8, 4096], BF16))
        self.idf = st.enter_context(nc.sbuf_tensor("g_idf", [128, 128], F32))
        self.mba = st.enter_context(nc.sbuf_tensor("g_mba", [128, 512], BF16))
        self.mbb = st.enter_context(nc.sbuf_tensor("g_mbb", [128, 512], BF16))
        self.b2bf = st.enter_context(nc.sbuf_tensor("g_b2bf", [8, 1024], BF16))
        self.gwt = st.enter_context(nc.sbuf_tensor("g_gw", [128, 64], F32))
        self.srow = st.enter_context(nc.sbuf_tensor("g_srow", [8, 512], F32))
        self.p0r = st.enter_context(nc.sbuf_tensor("g_p0r", [1, 512], BF16))
        self.p1r = st.enter_context(nc.sbuf_tensor("g_p1r", [1, 512], BF16))
        self.lr = st.enter_context(nc.sbuf_tensor("g_lr", [1, 1536], F32))
        self.g8a = st.enter_context(nc.sbuf_tensor("g_g8a", [128, 64], F32))
        self.g8b = st.enter_context(nc.sbuf_tensor("g_g8b", [128, 8], BF16))



def col(t, c0, n=1):
    return t[:, c0:c0 + n]


def vsync(ctx, eng, instr, name="vsp"):
    s = ctx.S(name)
    s.inc(instr)
    eng.wait_ge(s.h, s.n)



def emit_const_load(ctx, dr):
    ctx.new_phase()
    nc = ctx.nc
    cs = Stream(ctx, "c_ld", 2, kind="gp")
    with nc.Block() as blk:
        def f_g(gp):
            cs.put(gp, ctx.cst[:], dr["cst"][:])
            cs.put(gp, ctx.cstb[:], dr["cstb"][:])
            cs.put(gp, ctx.idf[:], dr["idf32"][:])
            cs.put(gp, ctx.gwt[:, 0:64].rearrange("p (d e) -> p d e", d=8, e=8),
                   dr["gate_w"][:, :, :].rearrange("d p e -> p d e"))
            if "moe_b2" in dr:
                cs.put(gp, ctx.b2bf[:], dr["moe_b2"][:, :])
            if "projbd" in dr:
                cs.put(gp, ctx.cstb[:, B_PROJ:B_PROJ + 128], dr["projbd"][:])
        blk.gpsimd(f_g)

        def f_s(se):
            cs.wait_all(se)
            se.activation(col(ctx.cst, C_A, 256), col(ctx.cst, C_A, 256), AF.Exp)
        blk.scalar(f_s)
    with nc.Block() as blk2:
        def f_v(ve):
            ve.tensor_scalar_mul(col(ctx.cst, C_A, 256), col(ctx.cst, C_A, 256), -1.0)
            if "moe_b2" in dr:
                ve.tensor_scalar_mul(ctx.b2bf[:], ctx.b2bf[:], 1.702)
        blk2.vector(f_v)



def emit_moe_prep(ctx, dr, scr):
    ctx.new_phase()
    nc = ctx.nc
    pc = Stream(ctx, "mp_c", 2, kind="gp")
    pst = Stream(ctx, "mp_s", 2)
    jobs = []
    for e in range(NE):
        for dt in range(DT):
            jobs.append((dr["moe_w1t"][e, dt, :, :], scr["w1bf"][e, dt, :, :], None))
    for e in range(NE):
        for g4 in range(4):
            jobs.append((dr["moe_w2t"][e, g4 * 4:(g4 + 1) * 4, :, :].rearrange("j p c -> p j c"),
                         scr["w2bf"][e, g4 * 4:(g4 + 1) * 4, :, :].rearrange("j p c -> p j c"), 4))
    with ExitStack() as ps:
        sb = [ps.enter_context(nc.sbuf_tensor(U(f"mp_b{i}"), [128, 4096], BF16)) for i in range(2)]
        with nc.Block() as blk:
            def f_g(gp):
                for i, (src_ap, _, j) in enumerate(jobs):
                    if i >= 2:
                        pst.wait_item(gp, i - 2)
                    dst = sb[i % 2][:] if j is None else sb[i % 2][:].rearrange("p (j c) -> p j c", j=j, c=1024)
                    pc.put(gp, dst, src_ap)
            blk.gpsimd(f_g)

            def f_s(sy):
                for i, (_, dst_ap, j) in enumerate(jobs):
                    pc.wait_item(sy, i)
                    srcb = sb[i % 2][:] if j is None else sb[i % 2][:].rearrange("p (j c) -> p j c", j=j, c=1024)
                    pst.put(sy, dst_ap, srcb)
                pst.wait_all(sy)
            blk.sync(f_s)



def emit_ln(ctx, x_src, xn_dst, kind):
    ctx.new_phase()
    nc = ctx.nc
    moe = kind == "moe"
    affine = kind == "affine"
    src_bf = affine
    lnx = Stream(ctx, "ln_x", 4)
    lns = Stream(ctx, "ln_s", 4, kind="gp")
    q1 = ctx.S("ln_q1"); m1 = ctx.S("ln_m1"); v1 = ctx.S("ln_v1")
    q2 = ctx.S("ln_q2"); v2 = ctx.S("ln_v2"); m2 = ctx.S("ln_m2")
    v3 = ctx.S("ln_v3"); m3 = ctx.S("ln_m3"); v4 = ctx.S("ln_v4")
    m4 = ctx.S("ln_m4"); v5 = ctx.S("ln_v5"); q3 = ctx.S("ln_q3")
    v6 = ctx.S("ln_v6"); m5 = ctx.S("ln_m5"); v7 = ctx.S("ln_v7")
    L0 = {s: s.n for s in (q1, m1, v1, q2, v2, m2, v3, m3, v4, m4, v5, q3, v6, m5, v7)}

    with ExitStack() as ps:
        xdt = BF16 if src_bf else F32
        xa = [ps.enter_context(nc.sbuf_tensor(U(f"ln_xa{i}"), [128, 4096], xdt)) for i in range(2)]
        xq = ps.enter_context(nc.sbuf_tensor(U("ln_xq"), [128, 4096], F32))
        nab = [ps.enter_context(nc.sbuf_tensor(U(f"ln_n{i}"), [128, 4096], BF16)) for i in range(2)]
        xnf = ps.enter_context(nc.sbuf_tensor(U("ln_xnf"), [128, 4096], F32)) if moe else None
        psS = ps.enter_context(nc.psum_tensor(U("ln_ps"), [1, 1024], F32))
        psB = ps.enter_context(nc.psum_tensor(U("ln_pb"), [128, 1024], F32))
        psG = ps.enter_context(nc.psum_tensor(U("ln_pg"), [8, 512], F32)) if moe else None
        psT = ps.enter_context(nc.psum_tensor(U("ln_pt"), [128, 8], F32)) if moe else None
        psM = ps.enter_context(nc.psum_tensor(U("ln_pm"), [8, 128], BF16)) if moe else None

        with nc.Block() as blk:
            def f_sync(sy):
                for ch in range(NCHK):
                    if ch >= 2:
                        sy.wait_ge(v3.h, L0[v3] + ch - 1)
                    for h in range(2):
                        lnx.put(sy, xa[ch % 2][:, h * 2048:(h + 1) * 2048].rearrange(
                                    "p (j c) -> p j c", j=4, c=512),
                                x_src[h * 4:(h + 1) * 4, :, ch * 512:(ch + 1) * 512].rearrange(
                                    "j p c -> p j c"))
            blk.sync(f_sync)

            def f_scalar(se):
                for ch in range(NCHK):
                    lnx.wait_upto(se, ch * 2 + 1)
                    if ch >= 1:
                        se.wait_ge(m1.h, L0[m1] + ch)
                    ins = None
                    for dt in range(DT):
                        ins = se.activation(xq[:, dt * 512:(dt + 1) * 512],
                                            xa[ch % 2][:, dt * 512:(dt + 1) * 512], AF.Square)
                    q1.inc(ins)
                    se.wait_ge(v1.h, L0[v1] + ch + 1)
                    q2.inc(se.activation(ctx.lr[:, 512:1024], ctx.lr[:, 1024:1536], AF.Sqrt))
                    if moe:
                        for u in range(4):
                            se.wait_ge(v5.h, L0[v5] + ch * 4 + u + 1)
                            q3.inc(se.activation(ctx.g8a[:, 35:36], ctx.g8a[:, 34:35], AF.Exp))
            blk.scalar(f_scalar)

            def f_tensor(te):
                onesf = col(ctx.cst, C_ONES)
                ones_in = ctx.cstb[:, B_ONESC:B_ONESC + 1] if src_bf else onesf
                for ch in range(NCHK):
                    te.wait_ge(q1.h, L0[q1] + ch + 1)
                    if ch >= 1:
                        te.wait_ge(v1.h, L0[v1] + ch)
                    for dt in range(DT):
                        te.matmul(psS[:, 0:512], ones_in, xa[ch % 2][:, dt * 512:(dt + 1) * 512],
                                  start=(dt == 0), stop=(dt == DT - 1))
                    ins = None
                    for dt in range(DT):
                        ins = te.matmul(psS[:, 512:1024], onesf, xq[:, dt * 512:(dt + 1) * 512],
                                        start=(dt == 0), stop=(dt == DT - 1))
                    m1.inc(ins)
                    te.wait_ge(v2.h, L0[v2] + ch + 1)
                    if ch >= 1:
                        te.wait_ge(v3.h, L0[v3] + ch)
                    onerow = ctx.cstb[0:1, B_ONER:B_ONER + 128]
                    te.matmul(psB[:, 0:512], onerow, ctx.p0r[0:1, :], start=True, stop=True)
                    ins = te.matmul(psB[:, 512:1024], onerow, ctx.p1r[0:1, :], start=True, stop=True)
                    m2.inc(ins)
                    if moe:
                        te.wait_ge(v3.h, L0[v3] + ch + 1)
                        if ch >= 1:
                            te.wait_ge(v4.h, L0[v4] + ch)
                        ins = None
                        for dt in range(DT):
                            ins = te.matmul(psG[:], ctx.gwt[:, dt * 8:(dt + 1) * 8],
                                            xnf[:, dt * 512:(dt + 1) * 512],
                                            start=(dt == 0), stop=(dt == DT - 1))
                        m3.inc(ins)
                        te.wait_ge(v4.h, L0[v4] + ch + 1)
                        for u in range(4):
                            if ch * 4 + u >= 1:
                                te.wait_ge(v5.h, L0[v5] + ch * 4 + u)
                            m4.inc(te.transpose(psT[:], ctx.srow[0:8, u * 128:(u + 1) * 128],
                                                ctx.idf[0:8, 0:8]))
                            te.wait_ge(v6.h, L0[v6] + ch * 4 + u + 1)
                            if ch * 4 + u >= 1:
                                te.wait_ge(v7.h, L0[v7] + ch * 4 + u)
                            m5.inc(te.transpose(psM[:], ctx.g8b[:], ctx.cstb[:, B_ID:B_ID + 128]))
            blk.tensor(f_tensor)

            def f_vector(ve):
                iota = col(ctx.cst, C_IOTA, 8)
                iota1k = col(ctx.cst, C_IOTA1K, 8)
                lr = ctx.lr
                for ch in range(NCHK):
                    ve.wait_ge(m1.h, L0[m1] + ch + 1)
                    if ch >= 1:
                        ve.wait_ge(q2.h, L0[q2] + ch)
                    ve.tensor_scalar_mul(lr[:, 0:512], psS[:, 0:512], 1.0 / D)
                    ve.tensor_scalar_mul(lr[:, 512:1024], psS[:, 512:1024], 1.0 / D)
                    ve.tensor_tensor(lr[:, 1024:1536], lr[:, 0:512], lr[:, 0:512], ALU.mult)
                    ve.tensor_tensor(lr[:, 1024:1536], lr[:, 512:1024], lr[:, 1024:1536], ALU.subtract)
                    v1.inc(ve.tensor_scalar_add(lr[:, 1024:1536], lr[:, 1024:1536], 1e-5))
                    ve.wait_ge(q2.h, L0[q2] + ch + 1)
                    vsync(ctx, ve, ve.reciprocal(lr[:, 1024:1536], lr[:, 512:1024]))
                    ve.tensor_copy(ctx.p0r[0:1, :], lr[:, 1024:1536])
                    ve.tensor_tensor(lr[:, 0:512], lr[:, 0:512], lr[:, 1024:1536], ALU.mult)
                    v2.inc(ve.tensor_scalar_mul(ctx.p1r[0:1, :], lr[:, 0:512], -1.0))
                    ve.wait_ge(m2.h, L0[m2] + ch + 1)
                    if ch >= 2:
                        lns.wait_upto(ve, (ch - 1) * 2 - 1)
                    if moe and ch >= 1:
                        ve.wait_ge(m3.h, L0[m3] + ch)
                    nbuf = nab[ch % 2]
                    ins = None
                    for dt in range(DT):
                        sl = slice(dt * 512, (dt + 1) * 512)
                        ve.tensor_tensor(ctx.ta[:], xa[ch % 2][:, sl], psB[:, 0:512], ALU.mult)
                        if moe:
                            ve.tensor_tensor(xnf[:, sl], ctx.ta[:], psB[:, 512:1024], ALU.add)
                            ins = ve.tensor_copy(nbuf[:, sl], xnf[:, sl])
                        elif affine:
                            ve.tensor_tensor(ctx.tb[:], ctx.ta[:], psB[:, 512:1024], ALU.add)
                            ins = ve.tensor_scalar(nbuf[:, sl], ctx.tb[:],
                                                   col(ctx.cst, C_NW + dt), col(ctx.cst, C_NB + dt),
                                                   ALU.mult, ALU.add)
                        else:
                            ins = ve.tensor_tensor(nbuf[:, sl], ctx.ta[:], psB[:, 512:1024], ALU.add)
                    v3.inc(ins)
                    if moe:
                        ve.wait_ge(m3.h, L0[m3] + ch + 1)
                        v4.inc(ve.tensor_copy(ctx.srow[:], psG[:]))
                        if ch == 0 and getattr(ctx, "dbg", None) is not None:
                            ve.tensor_copy(ctx.dbg[0:8, 0:512], ctx.srow[:])
                        for u in range(4):
                            ve.wait_ge(m4.h, L0[m4] + ch * 4 + u + 1)
                            g8 = ctx.g8a
                            if ch == 0 and u == 0 and getattr(ctx, "dbg", None) is not None:
                                ve.tensor_copy(ctx.dbg[:, 512:520], psT[:])
                            def V(ins):
                                vsync(ctx, ve, ins)
                            V(ve.max(g8[:, 8:16], psT[:]))
                            v1c, v2c = g8[:, 8:9], g8[:, 9:10]
                            V(ve.tensor_scalar(g8[:, 16:24], psT[:], v1c, None, ALU.subtract))
                            V(ve.tensor_single_scalar(g8[:, 16:24], g8[:, 16:24], 0.0, ALU.is_equal))
                            V(ve.scalar_tensor_tensor(g8[:, 24:32], g8[:, 16:24], -1000.0, iota1k,
                                                      ALU.mult, ALU.add))
                            V(ve.tensor_reduce(g8[:, 32:33], g8[:, 24:32], AX.X, ALU.min))
                            V(ve.tensor_scalar(g8[:, 16:24], iota, g8[:, 32:33], None, ALU.subtract))
                            V(ve.tensor_single_scalar(g8[:, 16:24], g8[:, 16:24], 0.0, ALU.is_equal))
                            V(ve.tensor_scalar(g8[:, 40:48], psT[:], v2c, None, ALU.subtract))
                            V(ve.tensor_single_scalar(g8[:, 40:48], g8[:, 40:48], 0.0, ALU.is_equal))
                            V(ve.tensor_tensor(g8[:, 48:56], g8[:, 40:48], g8[:, 16:24], ALU.mult))
                            V(ve.tensor_tensor(g8[:, 40:48], g8[:, 40:48], g8[:, 48:56], ALU.subtract))
                            V(ve.scalar_tensor_tensor(g8[:, 48:56], g8[:, 40:48], -1000.0, iota1k,
                                                      ALU.mult, ALU.add))
                            V(ve.tensor_reduce(g8[:, 33:34], g8[:, 48:56], AX.X, ALU.min))
                            V(ve.tensor_scalar(g8[:, 40:48], iota, g8[:, 33:34], None, ALU.subtract))
                            V(ve.tensor_single_scalar(g8[:, 40:48], g8[:, 40:48], 0.0, ALU.is_equal))
                            v5.inc(ve.tensor_tensor(g8[:, 34:35], v2c, v1c, ALU.subtract))
                            ve.wait_ge(q3.h, L0[q3] + ch * 4 + u + 1)
                            V(ve.tensor_scalar_add(g8[:, 36:37], g8[:, 35:36], 1.0))
                            V(ve.reciprocal(g8[:, 37:38], g8[:, 36:37]))
                            V(ve.tensor_tensor(g8[:, 38:39], g8[:, 35:36], g8[:, 37:38], ALU.mult))
                            V(ve.tensor_scalar_mul(g8[:, 37:38], g8[:, 37:38], 1.0 / 1.702))
                            V(ve.tensor_scalar_mul(g8[:, 38:39], g8[:, 38:39], 1.0 / 1.702))
                            V(ve.tensor_scalar(g8[:, 48:56], g8[:, 16:24], g8[:, 37:38], None, ALU.mult))
                            if ch == 0 and u == 0 and getattr(ctx, "dbg", None) is not None:
                                ve.tensor_copy(ctx.dbg[:, 528:559], g8[:, 8:39])
                                ve.tensor_copy(ctx.dbg[:, 560:576], g8[:, 40:56])
                            v6.inc(ve.scalar_tensor_tensor(ctx.g8b[:], g8[:, 40:48], g8[:, 38:39],
                                                           g8[:, 48:56], ALU.mult, ALU.add))
                            ve.wait_ge(m5.h, L0[m5] + ch * 4 + u + 1)
                            v7.inc(ve.tensor_copy(ctx.mT[0:8, ch * 512 + u * 128:ch * 512 + (u + 1) * 128],
                                                  psM[:]))
            blk.vector(f_vector)

            def f_gp(gp):
                for ch in range(NCHK):
                    gp.wait_ge(v3.h, L0[v3] + ch + 1)
                    nbuf = nab[ch % 2]
                    for h in range(2):
                        lns.put(gp, xn_dst[h * 4:(h + 1) * 4, :, ch * 512:(ch + 1) * 512].rearrange(
                                    "j p c -> p j c"),
                                nbuf[:, h * 2048:(h + 1) * 2048].rearrange("p (j c) -> p j c", j=4, c=512))
                lns.wait_all(gp)
            blk.gpsimd(f_gp)

    assert v3.n == L0[v3] + NCHK and lnx.n % (2 * NCHK) == 0
    if moe:
        assert v7.n == L0[v7] + NCHK * 4 and q3.n == L0[q3] + NCHK * 4



def emit_moe(ctx, dr, scr, xn_scr, x_src, x_dst):
    ctx.new_phase()
    nc = ctx.nc
    w1 = scr["w1bf"]
    w2 = scr["w2bf"]
    mx = Stream(ctx, "mo_x", 2)
    mw1 = Stream(ctx, "mo_w1", 4)
    mw2 = Stream(ctx, "mo_w2", 2)
    mxr = Stream(ctx, "mo_xr", 2, kind="gp")
    mst = Stream(ctx, "mo_st", 2, kind="gp")
    mmb = ctx.S("mo_mb"); seb = ctx.S("mo_se")
    md1 = ctx.S("mo_m1"); vg = ctx.S("mo_vg"); sg = ctx.S("mo_sg"); vq = ctx.S("mo_vq")
    md2 = ctx.S("mo_m2"); m2u = ctx.S("mo_m2u"); vev = ctx.S("mo_ev")
    L = {s: s.n for s in (mmb, seb, md1, vg, sg, vq, md2, m2u, vev)}
    NU = 8
    PPC = NE * NU * 2
    UPC = NE * NU
    U2PC = 2 * NE * 8

    with ExitStack() as ps:
        gpr = ps.enter_context(nc.sbuf_tensor(U("mo_g"), [128, 128 * 512], BF16))
        xt = [ps.enter_context(nc.sbuf_tensor(U(f"mo_xt{i}"), [128, 512], F32)) for i in range(2)]
        ot = [ps.enter_context(nc.sbuf_tensor(U(f"mo_ot{i}"), [128, 512], F32)) for i in range(2)]
        sgb = ps.enter_context(nc.sbuf_tensor(U("mo_sgb"), [128, 512], BF16))
        vvb = ps.enter_context(nc.sbuf_tensor(U("mo_vvb"), [128, 512], BF16))
        nbuf = ps.enter_context(nc.sbuf_tensor(U("mo_n"), [128, 4096], BF16))
        wab = [ps.enter_context(nc.sbuf_tensor(U(f"mo_w{i}"), [128, 4096], BF16)) for i in range(2)]
        w2ab = [ps.enter_context(nc.sbuf_tensor(U(f"mo_w2{i}"), [128, 2048], BF16)) for i in range(2)]
        psg = [ps.enter_context(nc.psum_tensor(U(f"mo_pg{i}"), [128, 512], F32)) for i in range(2)]
        psv = [ps.enter_context(nc.psum_tensor(U(f"mo_pv{i}"), [128, 512], F32)) for i in range(2)]
        pso = [ps.enter_context(nc.psum_tensor(U(f"mo_po{i}"), [128, 512], F32)) for i in range(4)]

        with nc.Block() as blk:
            def f_sync(sy):
                for ch in range(NCHK):
                    if ch >= 1:
                        sy.wait_ge(md1.h, L[md1] + ch * PPC)
                    for h in range(2):
                        mx.put(sy, nbuf[:, h * 2048:(h + 1) * 2048].rearrange(
                                   "p (j c) -> p j c", j=4, c=512),
                               xn_scr[h * 4:(h + 1) * 4, :, ch * 512:(ch + 1) * 512].rearrange(
                                   "j p c -> p j c"))
                    for e in range(NE):
                        for q in range(NU):
                            k = ch * UPC + e * NU + q
                            if k >= 2:
                                sy.wait_ge(md1.h, L[md1] + (k - 1) * 2)
                            for h in range(2):
                                mw1.put(sy, wab[k % 2][:, h * 2048:(h + 1) * 2048].rearrange(
                                            "p (j c) -> p j c", j=4, c=512),
                                        w1[e, h * 4:(h + 1) * 4, :, q * 512:(q + 1) * 512].rearrange(
                                            "j p c -> p j c"))
                    for half in range(2):
                        for e in range(NE):
                            for ug in range(8):
                                k2 = ch * U2PC + half * 64 + e * 8 + ug
                                if k2 >= 2:
                                    sy.wait_ge(m2u.h, L[m2u] + k2 - 1)
                                mw2.put(sy, w2ab[k2 % 2][:].rearrange("p (j c) -> p j c", j=2, c=1024),
                                        w2[e, ug * 2:(ug + 1) * 2, :, :].rearrange("j p c -> p j c"))
            blk.sync(f_sync)

            def f_gp(gp):
                for i in range(NCHK * DT):
                    ch, dt = divmod(i, DT)
                    if i >= 2:
                        gp.wait_ge(vev.h, L[vev] + i - 1)
                    mxr.put(gp, xt[i % 2][:], x_src[dt, :, ch * 512:(ch + 1) * 512])
                    if i >= 1:
                        gp.wait_ge(vev.h, L[vev] + i)
                        mst.put(gp, x_dst[(i - 1) % DT, :, ((i - 1) // DT) * 512:((i - 1) // DT + 1) * 512],
                                ot[(i - 1) % 2][:])
                i = NCHK * DT
                gp.wait_ge(vev.h, L[vev] + i)
                mst.put(gp, x_dst[(i - 1) % DT, :, ((i - 1) // DT) * 512:((i - 1) // DT + 1) * 512],
                        ot[(i - 1) % 2][:])
                mst.wait_all(gp)
            blk.gpsimd(f_gp)

            def f_tensor(te):
                for ch in range(NCHK):
                    mx.wait_upto(te, ch * 2 + 1)
                    if ch >= 1:
                        te.wait_ge(vev.h, L[vev] + ch * 8)
                    for e in range(NE):
                        ie = ch * NE + e
                        if ie >= 1:
                            te.wait_ge(seb.h, L[seb] + ie)
                        mmb.inc(te.matmul(pso[0][:], ctx.cstb[0:8, B_SEL8 + e * 128:B_SEL8 + (e + 1) * 128],
                                          ctx.mT[0:8, ch * 512:(ch + 1) * 512], start=True, stop=True))
                        for q in range(NU):
                            k = ch * UPC + e * NU + q
                            mw1.wait_item(te, 2 * k)
                            mw1.wait_item(te, 2 * k + 1)
                            wbuf = wab[k % 2]
                            for pr in range(2):
                                ip = k * 2 + pr
                                if ip >= 2:
                                    te.wait_ge(vg.h, L[vg] + ip - 1)
                                ins = None
                                for dt in range(DT):
                                    te.matmul(psg[ip % 2][:],
                                              wbuf[:, dt * 512 + (2 * pr) * 128: dt * 512 + (2 * pr + 1) * 128],
                                              nbuf[:, dt * 512:(dt + 1) * 512],
                                              start=(dt == 0), stop=(dt == DT - 1))
                                for dt in range(DT):
                                    ins = te.matmul(psv[ip % 2][:],
                                                    wbuf[:, dt * 512 + (2 * pr + 1) * 128: dt * 512 + (2 * pr + 2) * 128],
                                                    nbuf[:, dt * 512:(dt + 1) * 512],
                                                    start=(dt == 0), stop=(dt == DT - 1))
                                md1.inc(ins)
                    te.wait_ge(vq.h, L[vq] + (ch + 1) * PPC)
                    for half in range(2):
                        if half == 1:
                            te.wait_ge(vev.h, L[vev] + ch * 8 + 4)
                        for e in range(NE):
                            for ug in range(8):
                                k2 = ch * U2PC + half * 64 + e * 8 + ug
                                mw2.wait_item(te, k2)
                                w2buf = w2ab[k2 % 2]
                                ins = None
                                for ft in range(2):
                                    kt = e * 16 + ug * 2 + ft
                                    for dth in range(4):
                                        dt = half * 4 + dth
                                        ins = te.matmul(pso[dth][:],
                                                        w2buf[:, ft * 1024 + dt * 128: ft * 1024 + (dt + 1) * 128],
                                                        gpr[:, kt * 512:(kt + 1) * 512],
                                                        start=(e == 0 and ug == 0 and ft == 0),
                                                        stop=False)
                                m2u.inc(ins)
                        for dth in range(4):
                            dt = half * 4 + dth
                            ins = te.matmul(pso[dth][:], ctx.b2bf[0:8, dt * 128:(dt + 1) * 128],
                                            ctx.mT[0:8, ch * 512:(ch + 1) * 512],
                                            start=False, stop=True)
                            md2.inc(ins)
            blk.tensor(f_tensor)

            def f_scalar(se):
                for ch in range(NCHK):
                    for e in range(NE):
                        ie = ch * NE + e
                        se.wait_ge(mmb.h, L[mmb] + ie + 1)
                        if ie >= 2:
                            se.wait_ge(vq.h, L[vq] + (ie - 1) * 16)
                        mb = ctx.mba if e % 2 == 0 else ctx.mbb
                        seb.inc(se.copy(mb[:], pso[0][:]))
                        for qp in range(16):
                            ip = ie * 16 + qp
                            se.wait_ge(vg.h, L[vg] + ip + 1)
                            sg.inc(se.activation(sgb[:], ctx.tb[:], AF.Silu, scale=col(ctx.cst, C_1702)))
            blk.scalar(f_scalar)

            def f_vector(ve):
                for ch in range(NCHK):
                    if ch >= 1:
                        ve.wait_ge(m2u.h, L[m2u] + ch * U2PC)
                    for e in range(NE):
                        ie = ch * NE + e
                        mb = ctx.mba if e % 2 == 0 else ctx.mbb
                        for qp in range(16):
                            ip = ie * 16 + qp
                            q, pr = divmod(qp, 2)
                            gt = 4 * q + 2 * pr
                            kt = e * 16 + q * 2 + pr
                            ve.wait_ge(md1.h, L[md1] + ip + 1)
                            b1g = col(ctx.cst, C_B1M + e * 32 + gt)
                            b1v = col(ctx.cst, C_B1M + e * 32 + gt + 1)
                            ve.tensor_scalar(ctx.ta[:], psv[ip % 2][:], b1v, 9.0, ALU.add, ALU.min)
                            ve.tensor_scalar(vvb[:], ctx.ta[:], -9.0, 1.0, ALU.max, ALU.add)
                            vg.inc(ve.tensor_scalar(ctx.tb[:], psg[ip % 2][:], b1g, 9.0, ALU.add, ALU.min))
                            ve.wait_ge(sg.h, L[sg] + ip + 1)
                            if ip == ie * 16:
                                ve.wait_ge(seb.h, L[seb] + ie + 1)
                            ve.tensor_tensor(sgb[:], sgb[:], mb[:], ALU.mult)
                            vq.inc(ve.tensor_tensor(gpr[:, kt * 512:(kt + 1) * 512], sgb[:], vvb[:], ALU.mult))
                    for half in range(2):
                        for dth in range(4):
                            i = ch * 8 + half * 4 + dth
                            ve.wait_ge(md2.h, L[md2] + ch * 8 + half * 4 + dth + 1)
                            mxr.wait_item(ve, i)
                            if i >= 2:
                                mst.wait_item(ve, i - 2)
                            vev.inc(ve.scalar_tensor_tensor(ot[i % 2][:], pso[dth][:], 0.5,
                                                            xt[i % 2][:], ALU.mult, ALU.add))
            blk.vector(f_vector)

    assert md1.n == L[md1] + NCHK * PPC and vq.n == L[vq] + NCHK * PPC
    assert md2.n == L[md2] + NCHK * 8 and vev.n == L[vev] + NCHK * 8
    assert m2u.n == L[m2u] + NCHK * U2PC



def _dram_in(nc, name, arr):
    return nc.dram_tensor(name, list(arr.shape), F32 if arr.dtype == np.float32 else BF16,
                          kind="ExternalInput")


def _needed(subs):
    need = {"cst", "cstb", "idf32", "gate_w", "sel32"}
    for s in subs:
        if s in (0, 2):
            need |= {"moe_w1t", "moe_w2t", "moe_b2"}
        if s == 1:
            need |= {"ipjt", "xpjt", "dtjt", "opjt"}
        if s in (3, 5):
            need |= {"w1t_l1", "w2t_l1"}
        if s == 4:
            need |= {"wq", "wk", "wv", "wo", "projbd"}
    return need


def _build_program(dr_arrays, subs):
    nc = bass.Bass("TRN2", target_bir_lowering=False, detect_race_conditions=False)
    dr = {}
    for name in sorted(_needed(subs)):
        dr[name] = _dram_in(nc, name, dr_arrays[name])
    xin = nc.dram_tensor("XIN", [8, 128, T], F32, kind="ExternalInput")
    xout = nc.dram_tensor("XOUT", [8, 128, T], F32, kind="ExternalOutput")
    xres = nc.dram_tensor("xres", [8, 128, T], F32, kind="Internal")
    xn_scr = nc.dram_tensor("xn_scr", [8, 128, T], BF16, kind="Internal")
    xn1_scr = nc.dram_tensor("xn1_scr", [8, 128, T], BF16, kind="Internal")
    scr = {}
    if 1 in subs:
        for nm in ("xc_scr", "z_scr", "u_scr", "dl_scr"):
            scr[nm] = nc.dram_tensor(nm, [16, 128, T], BF16, kind="Internal")
    if 3 in subs or 5 in subs:
        scr["s1_scr"] = nc.dram_tensor("s1_scr", [8, 128, 4096], BF16, kind="Internal")
        scr["s2_scr"] = nc.dram_tensor("s2_scr", [32, 128, 1024], BF16, kind="Internal")
    if 0 in subs or 2 in subs:
        scr["w1bf"] = nc.dram_tensor("w1bf", [8, 8, 128, 4096], BF16, kind="Internal")
        scr["w2bf"] = nc.dram_tensor("w2bf", [8, 16, 128, 1024], BF16, kind="Internal")

    with ExitStack() as st:
        ctx = Ctx(nc, st)
        ctx.alloc_global()
        emit_const_load(ctx, dr)
        if 0 in subs or 2 in subs:
            emit_moe_prep(ctx, dr, scr)
        if 3 in subs or 5 in subs:
            emit_l1prep(ctx, dr, scr)
        cur = xin
        for idx, s in enumerate(subs):
            last = idx == len(subs) - 1
            dst = xout if last else xres
            if s in (0, 2):
                emit_ln(ctx, cur, xn_scr, "moe")
                emit_moe(ctx, dr, scr, xn_scr, cur, dst)
            elif s == 1:
                emit_ln(ctx, cur, xn1_scr, "plain")
                emit_ln(ctx, xn1_scr, xn_scr, "affine")
                emit_mamba(ctx, dr, xn_scr, xn1_scr, cur, dst, scr)
            elif s in (3, 5):
                emit_ln(ctx, cur, xn_scr, "plain")
                emit_l1ffn(ctx, dr, scr, xn_scr, cur, dst)
            elif s == 4:
                emit_ln(ctx, cur, xn_scr, "plain")
                emit_l1attn(ctx, dr, xn_scr, cur, dst)
            cur = dst
    return nc


def _in_map_for(dr_arrays, subs, xarr):
    m = {"XIN": xarr}
    for name in sorted(_needed(subs)):
        m[name] = dr_arrays[name]
    return m


def kernel(**inputs):
    d = _host_prep(inputs)
    xs = d["_x_cores"]
    if MODE == "fused":
        groups = [list(SUBS)]
    else:
        groups = [[s] for s in SUBS]
    cur = xs
    for g in groups:
        nc = _build_program(d, g)
        in_maps = [_in_map_for(d, g, cur[c]) for c in range(NCORES)]
        res = run_bass_kernel_spmd(nc, in_maps, core_ids=list(range(NCORES)))
        cur = [np.asarray(res.results[c]["XOUT"]) for c in range(NCORES)]
    out = np.empty((16, SEQ, D), np.float32)
    for c in range(NCORES):
        xc = np.asarray(cur[c], np.float32).reshape(D, T).T.reshape(2, SEQ, D)
        out[2 * c:2 * c + 2] = xc
    return out


def emit_l1prep(ctx, dr, scr):
    ctx.new_phase()
    nc = ctx.nc
    cst = ctx.cst
    lw = Stream(ctx, "lp_w", 2)
    ls = Stream(ctx, "lp_s", 2, kind="gp")
    pmm = ctx.S("lp_mm"); pse = ctx.S("lp_se"); pve = ctx.S("lp_ve")
    L = {pmm: pmm.n, pse: pse.n, pve: pve.n}
    onesb = ctx.cstb[:, B_ONESC:B_ONESC + 1]
    with ExitStack() as ps:
        wf = [ps.enter_context(nc.sbuf_tensor(U(f"lp_w{i}"), [128, 4096], F32)) for i in range(2)]
        sg = [ps.enter_context(nc.sbuf_tensor(U(f"lp_g{i}"), [128, 4096], BF16)) for i in range(2)]
        ab8 = ps.enter_context(nc.sbuf_tensor(U("lp_a"), [128, 32768], BF16))
        psA = ps.enter_context(nc.psum_tensor(U("lp_pA"), [128, 32], F32))
        psB2 = ps.enter_context(nc.psum_tensor(U("lp_pB"), [128, 8], F32))
        with nc.Block() as blk:
            def f_sync(sy):
                for dt in range(DT):
                    if dt >= 2:
                        sy.wait_ge(pse.h, L[pse] + dt - 1)
                    lw.put(sy, wf[dt % 2][:], dr["w1t_l1"][dt, :, :])
                for ht in range(32):
                    i = 8 + ht
                    sy.wait_ge(pse.h, L[pse] + i - 1)
                    lw.put(sy, wf[i % 2][:, 0:1024], dr["w2t_l1"][ht, :, :])
            blk.sync(f_sync)

            def f_scalar(se):
                for dt in range(DT):
                    lw.wait_item(se, dt)
                    if dt >= 2:
                        ls.wait_item(se, dt - 2)
                    se.activation(sg[dt % 2][:], wf[dt % 2][:], AF.Sign)
                    pse.inc(se.activation(ab8[:, dt * 4096:(dt + 1) * 4096], wf[dt % 2][:], AF.Abs))
                for ht in range(32):
                    i = 8 + ht
                    lw.wait_item(se, i)
                    ls.wait_item(se, i - 2)
                    se.activation(sg[i % 2][:, 0:1024], wf[i % 2][:, 0:1024], AF.Sign)
                    if ht == 0:
                        se.wait_ge(pmm.h, L[pmm] + 8)
                    pse.inc(se.activation(ab8[:, ht * 1024:(ht + 1) * 1024], wf[i % 2][:, 0:1024], AF.Abs))
            blk.scalar(f_scalar)

            def f_gp(gp):
                for dt in range(DT):
                    gp.wait_ge(pse.h, L[pse] + dt + 1)
                    ls.put(gp, scr["s1_scr"][dt, :, :], sg[dt % 2][:])
                for ht in range(32):
                    i = 8 + ht
                    gp.wait_ge(pse.h, L[pse] + i + 1)
                    ls.put(gp, scr["s2_scr"][ht, :, :], sg[i % 2][:, 0:1024])
                ls.wait_all(gp)
            blk.gpsimd(f_gp)

            def f_tensor(te):
                te.wait_ge(pse.h, L[pse] + 8)
                ins = None
                for ht in range(32):
                    for dt in range(DT):
                        ins = te.matmul(psA[:, ht:ht + 1],
                                        ab8[:, dt * 4096 + ht * 128: dt * 4096 + (ht + 1) * 128],
                                        onesb, start=(dt == 0), stop=(dt == DT - 1))
                pmm.inc(ins, 8)
                te.wait_ge(pse.h, L[pse] + 40)
                ins = None
                for dto in range(DT):
                    for ht in range(32):
                        ins = te.matmul(psB2[:, dto:dto + 1],
                                        ab8[:, ht * 1024 + dto * 128: ht * 1024 + (dto + 1) * 128],
                                        onesb, start=(ht == 0), stop=(ht == 31))
                pmm.inc(ins, 32)
            blk.tensor(f_tensor)

            def f_vector(ve):
                ve.wait_ge(pmm.h, L[pmm] + 8)
                vsync(ctx, ve, ve.tensor_scalar(col(cst, C_AL1, 32), psA[:], 1.0 / 1024, None, ALU.mult))
                vsync(ctx, ve, ve.reciprocal(col(cst, C_BJ1, 32), col(cst, C_AL1, 32)))
                vsync(ctx, ve, ve.tensor_tensor(col(cst, C_BJ1, 32), col(cst, C_L1B1, 32),
                                                col(cst, C_BJ1, 32), ALU.mult))
                ve.wait_ge(pmm.h, L[pmm] + 40)
                vsync(ctx, ve, ve.tensor_scalar(col(cst, C_AL2, 8), psB2[:], 0.5 / 4096, None, ALU.mult))
                vsync(ctx, ve, ve.tensor_scalar(col(cst, C_BJ2, 8), col(cst, C_AL2, 8), 2.0, None, ALU.mult))
                vsync(ctx, ve, ve.reciprocal(col(cst, C_BJ2, 8), col(cst, C_BJ2, 8)))
                pve.inc(ve.tensor_tensor(col(cst, C_BJ2, 8), col(cst, C_L1B2, 8),
                                         col(cst, C_BJ2, 8), ALU.mult))
            blk.vector(f_vector)


def emit_l1ffn(ctx, dr, scr, xn_scr, x_src, x_dst):
    ctx.new_phase()
    nc = ctx.nc
    cst = ctx.cst
    ls1 = Stream(ctx, "lf_s1", 2)
    lx = Stream(ctx, "lf_x", 2)
    lw2 = Stream(ctx, "lf_w2", 2)
    lxr = Stream(ctx, "lf_xr", 2, kind="gp")
    lst = Stream(ctx, "lf_st", 2, kind="gp")
    d1m = ctx.S("lf_d1")
    d1v = ctx.S("lf_v1")
    d1s = ctx.S("lf_q1")
    d2m = ctx.S("lf_d2")
    d2u = ctx.S("lf_d2u")
    d2v = ctx.S("lf_v2")
    L = {s: s.n for s in (d1m, d1v, d1s, d2m, d2u, d2v)}
    with ExitStack() as ps:
        s1r = ps.enter_context(nc.sbuf_tensor(U("lf_S1"), [128, 32768], BF16))
        nbuf = ps.enter_context(nc.sbuf_tensor(U("lf_n"), [128, 4096], BF16))
        g1 = ps.enter_context(nc.sbuf_tensor(U("lf_g1"), [128, 16384], BF16))
        tg = [ps.enter_context(nc.sbuf_tensor(U(f"lf_tg{i}"), [128, 512], F32)) for i in range(2)]
        w2l = [ps.enter_context(nc.sbuf_tensor(U(f"lf_w2{i}"), [128, 1024], BF16)) for i in range(2)]
        xt = [ps.enter_context(nc.sbuf_tensor(U(f"lf_xt{i}"), [128, 512], F32)) for i in range(2)]
        ot = [ps.enter_context(nc.sbuf_tensor(U(f"lf_ot{i}"), [128, 512], F32)) for i in range(2)]
        tv = ps.enter_context(nc.sbuf_tensor(U("lf_tv"), [128, 512], F32))
        p1 = [ps.enter_context(nc.psum_tensor(U(f"lf_p1{i}"), [128, 512], F32)) for i in range(2)]
        p2 = [ps.enter_context(nc.psum_tensor(U(f"lf_p2{i}"), [128, 512], F32)) for i in range(4)]
        with nc.Block() as blk:
            def f_sync(sy):
                for dt in range(DT):
                    ls1.put(sy, s1r[:, dt * 4096:(dt + 1) * 4096], scr["s1_scr"][dt, :, :])
                for ch in range(NCHK):
                    if ch >= 1:
                        sy.wait_ge(d1m.h, L[d1m] + ch * 32)
                    for h in range(2):
                        lx.put(sy, nbuf[:, h * 2048:(h + 1) * 2048].rearrange(
                                   "p (j c) -> p j c", j=4, c=512),
                               xn_scr[h * 4:(h + 1) * 4, :, ch * 512:(ch + 1) * 512].rearrange(
                                   "j p c -> p j c"))
                    for half in range(2):
                        for ht in range(32):
                            k2 = (ch * 2 + half) * 32 + ht
                            if k2 >= 2:
                                sy.wait_ge(d2u.h, L[d2u] + k2 - 1)
                            lw2.put(sy, w2l[k2 % 2][:], scr["s2_scr"][ht, :, :])
            blk.sync(f_sync)

            def f_gp(gp):
                for i in range(NCHK * DT):
                    ch, dt = divmod(i, DT)
                    if i >= 2:
                        gp.wait_ge(d2v.h, L[d2v] + i - 1)
                    lxr.put(gp, xt[i % 2][:], x_src[dt, :, ch * 512:(ch + 1) * 512])
                    if i >= 1:
                        gp.wait_ge(d2v.h, L[d2v] + i)
                        lst.put(gp, x_dst[(i - 1) % DT, :, ((i - 1) // DT) * 512:((i - 1) // DT + 1) * 512],
                                ot[(i - 1) % 2][:])
                i = NCHK * DT
                gp.wait_ge(d2v.h, L[d2v] + i)
                lst.put(gp, x_dst[(i - 1) % DT, :, ((i - 1) // DT) * 512:((i - 1) // DT + 1) * 512],
                        ot[(i - 1) % 2][:])
                lst.wait_all(gp)
            blk.gpsimd(f_gp)

            def f_tensor(te):
                ls1.wait_all(te)
                for ch in range(NCHK):
                    lx.wait_upto(te, ch * 2 + 1)
                    for ht in range(32):
                        ik = ch * 32 + ht
                        if ik >= 2:
                            te.wait_ge(d1v.h, L[d1v] + ik - 1)
                        ins = None
                        for dt in range(DT):
                            ins = te.matmul(p1[ik % 2][:],
                                            s1r[:, dt * 4096 + ht * 128: dt * 4096 + (ht + 1) * 128],
                                            nbuf[:, dt * 512:(dt + 1) * 512],
                                            start=(dt == 0), stop=(dt == DT - 1))
                        d1m.inc(ins)
                    te.wait_ge(d1s.h, L[d1s] + (ch + 1) * 32)
                    for half in range(2):
                        if ch * 2 + half >= 1:
                            te.wait_ge(d2v.h, L[d2v] + (ch * 2 + half) * 4)
                        for ht in range(32):
                            k2 = (ch * 2 + half) * 32 + ht
                            lw2.wait_item(te, k2)
                            ins = None
                            for dth in range(4):
                                dt = half * 4 + dth
                                ins = te.matmul(p2[dth][:],
                                                w2l[k2 % 2][:, dt * 128:(dt + 1) * 128],
                                                g1[:, ht * 512:(ht + 1) * 512],
                                                start=(ht == 0), stop=(ht == 31))
                            d2u.inc(ins)
                        for dth in range(4):
                            d2m.inc(te.matmul(p2[dth][:], w2l[(ch * 2 + half) * 32 % 2][:, 0:128],
                                              g1[:, 0:512], start=False, stop=True)) if False else None
            blk.tensor(f_tensor)

            def f_vector(ve):
                for ch in range(NCHK):
                    for ht in range(32):
                        ik = ch * 32 + ht
                        ve.wait_ge(d1m.h, L[d1m] + ik + 1)
                        if ik >= 2:
                            ve.wait_ge(d1s.h, L[d1s] + ik - 1)
                        d1v.inc(ve.tensor_scalar(tg[ik % 2][:], p1[ik % 2][:],
                                                 col(cst, C_BJ1 + ht), col(cst, C_AL1 + ht),
                                                 ALU.add, ALU.mult))
                    for half in range(2):
                        for dth in range(4):
                            i = ch * 8 + half * 4 + dth
                            dt = half * 4 + dth
                            ve.wait_ge(d2u.h, L[d2u] + (ch * 2 + half + 1) * 32)
                            lxr.wait_item(ve, i)
                            if i >= 2:
                                lst.wait_item(ve, i - 2)
                            ve.tensor_scalar(tv[:], p2[dth][:], col(cst, C_BJ2 + dt),
                                             col(cst, C_AL2 + dt), ALU.add, ALU.mult)
                            d2v.inc(ve.tensor_tensor(ot[i % 2][:], tv[:], xt[i % 2][:], ALU.add))
            blk.vector(f_vector)

            def f_scalar(se):
                for ch in range(NCHK):
                    for ht in range(32):
                        ik = ch * 32 + ht
                        se.wait_ge(d1v.h, L[d1v] + ik + 1)
                        d1s.inc(se.activation(g1[:, ht * 512:(ht + 1) * 512], tg[ik % 2][:], AF.Gelu))
            blk.scalar(f_scalar)

    assert d1m.n == L[d1m] + NCHK * 32 and d2v.n == L[d2v] + NCHK * 8


def emit_l1attn(ctx, dr, xn_scr, x_src, x_dst):
    ctx.new_phase()
    nc = ctx.nc
    cst = ctx.cst
    lwq = Stream(ctx, "la_w", 2, kind="gp")
    lx = Stream(ctx, "la_x", 2)
    lxr = Stream(ctx, "la_xr", 2, kind="gp")
    lst = Stream(ctx, "la_st", 2, kind="gp")
    qm = ctx.S("la_qm")
    qv = ctx.S("la_qv")
    fm = ctx.S("la_fm")
    fv = ctx.S("la_fv")
    fq = ctx.S("la_fq")
    fz = ctx.S("la_fz")
    vm = ctx.S("la_vm")
    vv = ctx.S("la_vv")
    am_ = ctx.S("la_am")
    av = ctx.S("la_av")
    nm = ctx.S("la_nm")
    nv = ctx.S("la_nv")
    km = ctx.S("la_km")
    kv_ = ctx.S("la_kv")
    um = ctx.S("la_um")
    uv = ctx.S("la_uv")
    tm = ctx.S("la_tm")
    tv_ = ctx.S("la_tv")
    wm = ctx.S("la_wm")
    wv_ = ctx.S("la_wv")
    L = {s: s.n for s in (qm, qv, fm, fv, fq, fz, vm, vv, am_, av, nm, nv, km, kv_, um, uv, tm, tv_, wm, wv_)}
    with ExitStack() as ps:
        wsb = {}
        for nmw in ("wq", "wk", "wv", "wo"):
            wsb[nmw] = ps.enter_context(nc.sbuf_tensor(U(f"la_{nmw}"), [128, 8192], BF16))
        nbuf = ps.enter_context(nc.sbuf_tensor(U("la_n"), [128, 4096], BF16))
        qps = ps.enter_context(nc.sbuf_tensor(U("la_q"), [128, 4096], BF16))
        kps = ps.enter_context(nc.sbuf_tensor(U("la_k"), [128, 4096], BF16))
        vps = ps.enter_context(nc.sbuf_tensor(U("la_v"), [128, 4096], BF16))
        aps = ps.enter_context(nc.sbuf_tensor(U("la_a"), [128, 4096], BF16))
        rawb = ps.enter_context(nc.sbuf_tensor(U("la_r"), [128, 512], BF16))
        t5 = ps.enter_context(nc.sbuf_tensor(U("la_t5"), [128, 512], F32))
        t6 = ps.enter_context(nc.sbuf_tensor(U("la_t6"), [128, 512], F32))
        t7 = ps.enter_context(nc.sbuf_tensor(U("la_t7"), [128, 512], F32))
        amb = ps.enter_context(nc.sbuf_tensor(U("la_amb"), [128, 128], BF16))
        knb = ps.enter_context(nc.sbuf_tensor(U("la_kn"), [128, 64], BF16))
        kvf = ps.enter_context(nc.sbuf_tensor(U("la_kvf"), [128, 520], F32))
        kvb = ps.enter_context(nc.sbuf_tensor(U("la_kvb"), [128, 520], BF16))
        dnc = ps.enter_context(nc.sbuf_tensor(U("la_dn"), [128, 2], F32))
        atT = ps.enter_context(nc.sbuf_tensor(U("la_atT"), [128, 1024], BF16))
        xt = [ps.enter_context(nc.sbuf_tensor(U(f"la_xt{i}"), [128, 128], F32)) for i in range(2)]
        ot = [ps.enter_context(nc.sbuf_tensor(U(f"la_ot{i}"), [128, 128], F32)) for i in range(2)]
        PQ = ps.enter_context(nc.psum_tensor(U("la_PQ"), [128, 512], F32))
        PF = ps.enter_context(nc.psum_tensor(U("la_PF"), [128, 512], F32))
        PV = ps.enter_context(nc.psum_tensor(U("la_PV"), [128, 512], F32))
        PA = ps.enter_context(nc.psum_tensor(U("la_PA"), [128, 128], F32))
        PND = ps.enter_context(nc.psum_tensor(U("la_PND"), [128, 66], F32))
        PKV = ps.enter_context(nc.psum_tensor(U("la_PKV"), [128, 65], F32))
        PTT = ps.enter_context(nc.psum_tensor(U("la_PTT"), [128, 192], BF16))
        PW = ps.enter_context(nc.psum_tensor(U("la_PW"), [128, 128], F32))

        with nc.Block() as blk:
            def f_gp(gp):
                for i, nmw in enumerate(("wq", "wk", "wv", "wo")):
                    for h in range(2):
                        lwq.put(gp, wsb[nmw][:, h * 4096:(h + 1) * 4096].rearrange(
                                    "p (j c) -> p j c", j=4, c=1024),
                                dr[nmw][h * 4:(h + 1) * 4, :, :].rearrange("j p c -> p j c"))
                NIT = 2 * 4 * 4 * 8
                for i in range(NIT):
                    if i >= 2:
                        gp.wait_ge(wv_.h, L[wv_] + i - 1)
                    b = i // 128
                    r = i % 128
                    ch, r2 = divmod(r, 32)
                    ul, dto = divmod(r2, 8)
                    tcol = b * 2048 + ch * 512 + ul * 128
                    lxr.put(gp, xt[i % 2][:], x_src[dto, :, tcol:tcol + 128])
                    if i >= 1:
                        gp.wait_ge(wv_.h, L[wv_] + i)
                        j = i - 1
                        bj = j // 128
                        rj = j % 128
                        chj, r2j = divmod(rj, 32)
                        ulj, dtoj = divmod(r2j, 8)
                        tcj = bj * 2048 + chj * 512 + ulj * 128
                        lst.put(gp, x_dst[dtoj, :, tcj:tcj + 128], ot[j % 2][:])
                i = NIT
                gp.wait_ge(wv_.h, L[wv_] + i)
                j = i - 1
                bj = j // 128
                rj = j % 128
                chj, r2j = divmod(rj, 32)
                ulj, dtoj = divmod(r2j, 8)
                tcj = bj * 2048 + chj * 512 + ulj * 128
                lst.put(gp, x_dst[dtoj, :, tcj:tcj + 128], ot[j % 2][:])
                lst.wait_all(gp)
            blk.gpsimd(f_gp)

            def f_sync(sy):
                for b in range(2):
                    for ch in range(4):
                        ic = b * 4 + ch
                        if ic >= 1:
                            sy.wait_ge(vm.h, L[vm] + ic * 8)
                        for h in range(2):
                            lx.put(sy, nbuf[:, h * 2048:(h + 1) * 2048].rearrange(
                                       "p (j c) -> p j c", j=4, c=512),
                                   xn_scr[h * 4:(h + 1) * 4, :, b * 2048 + ch * 512:
                                          b * 2048 + (ch + 1) * 512].rearrange("j p c -> p j c"))
            blk.sync(f_sync)

            def f_tensor(te):
                lwq.wait_all(te)
                for b in range(2):
                    for ch in range(4):
                        ic = b * 4 + ch
                        lx.wait_upto(te, ic * 2 + 1)
                        for pr in range(2):
                            wname = "wq" if pr == 0 else "wk"
                            for dqt in range(DT):
                                iq = (ic * 2 + pr) * 8 + dqt
                                if iq >= 1:
                                    te.wait_ge(qv.h, L[qv] + iq)
                                ins = None
                                for dt in range(DT):
                                    ins = te.matmul(PQ[:], wsb[wname][:, dt * 1024 + dqt * 128:
                                                                      dt * 1024 + (dqt + 1) * 128],
                                                    nbuf[:, dt * 512:(dt + 1) * 512],
                                                    start=(dt == 0), stop=(dt == DT - 1))
                                qm.inc(ins)
                                te.wait_ge(qv.h, L[qv] + iq + 1)
                                if iq >= 1:
                                    te.wait_ge(fv.h, L[fv] + iq)
                                    te.wait_ge(fq.h, L[fq] + iq)
                                fm.inc(te.matmul(PF[:], ctx.cstb[:, B_PROJ:B_PROJ + 128],
                                                 rawb[:], start=True, stop=True))
                        for ul in range(4):
                            for dvh in range(2):
                                iv = ic * 8 + ul * 2 + dvh
                                if iv >= 1:
                                    te.wait_ge(vv.h, L[vv] + iv)
                                ins = None
                                for dt in range(DT):
                                    ins = te.matmul(PV[:],
                                                    nbuf[:, dt * 512 + ul * 128: dt * 512 + (ul + 1) * 128],
                                                    wsb["wv"][:, dt * 1024 + dvh * 512:
                                                              dt * 1024 + (dvh + 1) * 512],
                                                    start=(dt == 0), stop=(dt == DT - 1))
                                vm.inc(ins)
                        te.wait_ge(fz.h, L[fz] + (ic + 1) * 16)
                        for h in range(16):
                            hrow = (h % 2) * 64
                            hcol = (h // 2) * 512
                            for ul in range(4):
                                ia = (ic * 16 + h) * 4 + ul
                                u_abs = ch * 4 + ul
                                if ia >= 1:
                                    te.wait_ge(av.h, L[av] + ia)
                                am_.inc(te.matmul(PA[:],
                                                  kps[hrow:hrow + 64, hcol + ul * 128: hcol + (ul + 1) * 128],
                                                  qps[hrow:hrow + 64, hcol + ul * 128: hcol + (ul + 1) * 128],
                                                  start=True, stop=True))
                                te.wait_ge(av.h, L[av] + ia + 1)
                                if ia >= 1:
                                    te.wait_ge(nv.h, L[nv] + ia)
                                te.matmul(PND[:, 0:64], amb[:],
                                          vps[:, ul * 1024 + h * 64: ul * 1024 + (h + 1) * 64],
                                          start=True, stop=False)
                                ins = te.matmul(PND[:, 64:65], amb[:], ctx.cstb[:, B_ONESC:B_ONESC + 1],
                                                start=False, stop=(u_abs == 0))
                                if u_abs > 0:
                                    te.wait_ge(uv.h, L[uv] + ia)
                                    ins = te.matmul(PND[:, 0:65],
                                                    qps[hrow:hrow + 64, hcol + ul * 128: hcol + (ul + 1) * 128],
                                                    kvb[hrow:hrow + 64, (h // 2) * 65:(h // 2 + 1) * 65],
                                                    start=False, stop=True)
                                nm.inc(ins)
                                if ia >= 1:
                                    te.wait_ge(kv_.h, L[kv_] + ia)
                                km.inc(te.transpose(PTT[:, 0:64],
                                                    kps[hrow:hrow + 64, hcol + ul * 128: hcol + (ul + 1) * 128],
                                                    ctx.cstb[hrow:hrow + 64, B_ID + hrow:B_ID + hrow + 64]))
                                te.wait_ge(kv_.h, L[kv_] + ia + 1)
                                if ia >= 1:
                                    te.wait_ge(uv.h, L[uv] + ia)
                                te.matmul(PKV[hrow:hrow + 64, 0:64], knb[:],
                                          vps[:, ul * 1024 + h * 64: ul * 1024 + (h + 1) * 64],
                                          start=True, stop=False)
                                um.inc(te.matmul(PKV[hrow:hrow + 64, 64:65], knb[:],
                                                 ctx.cstb[:, B_ONESC:B_ONESC + 1],
                                                 start=False, stop=True))
                        te.wait_ge(nv.h, L[nv] + (ic * 16 + 16) * 4)
                        for ul in range(4):
                            for dvt in range(DT):
                                it = (ic * 4 + ul) * 8 + dvt
                                if it >= 1:
                                    te.wait_ge(tv_.h, L[tv_] + it)
                                tm.inc(te.transpose(PTT[:, 64:192],
                                                    aps[:, ul * 1024 + dvt * 128: ul * 1024 + (dvt + 1) * 128],
                                                    ctx.cstb[:, B_ID:B_ID + 128]))
                            te.wait_ge(tv_.h, L[tv_] + (ic * 4 + ul) * 8 + 8)
                            for dto in range(DT):
                                iw = (ic * 4 + ul) * 8 + dto
                                if iw >= 1:
                                    te.wait_ge(wv_.h, L[wv_] + iw)
                                ins = None
                                for dvt in range(DT):
                                    ins = te.matmul(PW[:], wsb["wo"][:, dvt * 1024 + dto * 128:
                                                                     dvt * 1024 + (dto + 1) * 128],
                                                    atT[:, dvt * 128:(dvt + 1) * 128],
                                                    start=(dvt == 0), stop=(dvt == DT - 1))
                                wm.inc(ins)
            blk.tensor(f_tensor)

            def f_vector(ve):
                for b in range(2):
                    for ch in range(4):
                        ic = b * 4 + ch
                        for pr in range(2):
                            dst = qps if pr == 0 else kps
                            for dqt in range(DT):
                                iq = (ic * 2 + pr) * 8 + dqt
                                ve.wait_ge(qm.h, L[qm] + iq + 1)
                                if iq >= 1:
                                    ve.wait_ge(fm.h, L[fm] + iq)
                                qv.inc(ve.tensor_copy(rawb[:], PQ[:]))
                                ve.wait_ge(fm.h, L[fm] + iq + 1)
                                fv.inc(ve.tensor_single_scalar(t5[:], PF[:], 0.0, ALU.min))
                                ve.wait_ge(fq.h, L[fq] + iq + 1)
                                fz.inc(ve.tensor_tensor(dst[:, dqt * 512:(dqt + 1) * 512],
                                                        t6[:], t7[:], ALU.add))
                        for ul in range(4):
                            for dvh in range(2):
                                iv = ic * 8 + ul * 2 + dvh
                                ve.wait_ge(vm.h, L[vm] + iv + 1)
                                vv.inc(ve.tensor_copy(vps[:, ul * 1024 + dvh * 512: ul * 1024 + (dvh + 1) * 512],
                                                      PV[:]))
                        for h in range(16):
                            for ul in range(4):
                                ia = (ic * 16 + h) * 4 + ul
                                u_abs = ch * 4 + ul
                                ve.wait_ge(am_.h, L[am_] + ia + 1)
                                av.inc(ve.tensor_tensor(amb[:], PA[:], ctx.cstb[:, B_TRI:B_TRI + 128],
                                                        ALU.mult))
                                ve.wait_ge(km.h, L[km] + ia + 1)
                                kv_.inc(ve.tensor_copy(knb[:], PTT[:, 0:64]))
                                ve.wait_ge(nm.h, L[nm] + ia + 1)
                                vsync(ctx, ve, ve.tensor_scalar(dnc[:, 0:1], PND[:, 64:65], 1e-6,
                                                                None, ALU.add))
                                vsync(ctx, ve, ve.reciprocal(dnc[:, 1:2], dnc[:, 0:1]))
                                nv.inc(ve.tensor_scalar(aps[:, ul * 1024 + h * 64: ul * 1024 + (h + 1) * 64],
                                                        PND[:, 0:64], dnc[:, 1:2], None, ALU.mult))
                                ve.wait_ge(um.h, L[um] + ia + 1)
                                hr2 = (h % 2) * 64
                                hcc = (h // 2) * 65
                                if u_abs == 0:
                                    vsync(ctx, ve, ve.tensor_copy(kvf[hr2:hr2 + 64, hcc:hcc + 65],
                                                                  PKV[hr2:hr2 + 64, :]))
                                else:
                                    vsync(ctx, ve, ve.tensor_tensor(kvf[hr2:hr2 + 64, hcc:hcc + 65],
                                                                    kvf[hr2:hr2 + 64, hcc:hcc + 65],
                                                                    PKV[hr2:hr2 + 64, :], ALU.add))
                                uv.inc(ve.tensor_copy(kvb[hr2:hr2 + 64, hcc:hcc + 65],
                                                      kvf[hr2:hr2 + 64, hcc:hcc + 65]))
                        for ul in range(4):
                            for dvt in range(DT):
                                it = (ic * 4 + ul) * 8 + dvt
                                ve.wait_ge(tm.h, L[tm] + it + 1)
                                tv_.inc(ve.tensor_copy(atT[:, dvt * 128:(dvt + 1) * 128], PTT[:, 64:192]))
                            for dto in range(DT):
                                iw = (ic * 4 + ul) * 8 + dto
                                ve.wait_ge(wm.h, L[wm] + iw + 1)
                                lxr.wait_item(ve, iw)
                                if iw >= 2:
                                    lst.wait_item(ve, iw - 2)
                                wv_.inc(ve.tensor_tensor(ot[iw % 2][:], PW[:], xt[iw % 2][:], ALU.add))
            blk.vector(f_vector)

            def f_scalar(se):
                for b in range(2):
                    for ch in range(4):
                        ic = b * 4 + ch
                        for pr in range(2):
                            for dqt in range(DT):
                                iq = (ic * 2 + pr) * 8 + dqt
                                se.wait_ge(fv.h, L[fv] + iq + 1)
                                se.activation(t7[:], PF[:], AF.Relu)
                                fq.inc(se.activation(t6[:], t5[:], AF.Exp))
            blk.scalar(f_scalar)


def emit_mamba(ctx, dr, xn_scr, xn1_scr, x_src, x_dst, scr):
    with ExitStack() as ms:
        ctx.BCs = ms.enter_context(ctx.nc.sbuf_tensor(U("g_BCs"), [32, 4096], BF16))
        ctx.selb = ms.enter_context(ctx.nc.sbuf_tensor(U("g_selb"), [32, 4096], BF16))
        emit_mamba_m1(ctx, dr, xn_scr, scr)
        emit_mamba_m2(ctx, dr, scr)
        emit_mamba_m3(ctx, dr, scr)
        emit_mamba_m45(ctx, dr, xn1_scr, x_src, x_dst, scr)


def emit_mamba_m1(ctx, dr, xn_scr, scr):
    ctx.new_phase()
    nc = ctx.nc
    cst = ctx.cst
    m1x = Stream(ctx, "m1_x", 4)
    m1w = Stream(ctx, "m1_w", 4, kind="gp")
    m1s = Stream(ctx, "m1_s", 2, kind="gp")
    mm1 = ctx.S("m1_mm"); mv1 = ctx.S("m1_ve")
    L = {mm1: mm1.n, mv1: mv1.n}
    with ExitStack() as ps:
        xnfull = ps.enter_context(nc.sbuf_tensor(U("m1_xn"), [128, 32768], BF16))
        wt = [ps.enter_context(nc.sbuf_tensor(U(f"m1_w{i}"), [128, 1024], BF16)) for i in range(2)]
        evb = [ps.enter_context(nc.sbuf_tensor(U(f"m1_e{i}"), [128, 4096], BF16)) for i in range(2)]
        PM = ps.enter_context(nc.psum_tensor(U("m1_pm"), [128, 4096], F32))
        with nc.Block() as blk:
            def f_sync(sy):
                for dt in range(DT):
                    m1x.put(sy, xnfull[:, dt * 4096:(dt + 1) * 4096], xn_scr[dt, :, :])
            blk.sync(f_sync)

            def f_gp(gp):
                for ct in range(32):
                    if ct >= 2:
                        gp.wait_ge(mm1.h, L[mm1] + ct - 1)
                    for hh in range(2):
                        m1w.put(gp, wt[ct % 2][:, hh * 512:(hh + 1) * 512].rearrange(
                                    "p (j c) -> p j c", j=4, c=128),
                                dr["ipjt"][hh * 4:(hh + 1) * 4, :, ct * 128:(ct + 1) * 128].rearrange(
                                    "j p c -> p j c"))
                    if ct >= 1:
                        cts = ct - 1
                        gp.wait_ge(mv1.h, L[mv1] + cts + 1)
                        dst = scr["xc_scr"][cts, :, :] if cts < 16 else scr["z_scr"][cts - 16, :, :]
                        m1s.put(gp, dst, evb[cts % 2][:])
                gp.wait_ge(mv1.h, L[mv1] + 32)
                m1s.put(gp, scr["z_scr"][15, :, :], evb[31 % 2][:])
                m1s.wait_all(gp)
            blk.gpsimd(f_gp)

            def f_tensor(te):
                m1x.wait_all(te)
                for ct in range(32):
                    m1w.wait_item(te, 2 * ct)
                    m1w.wait_item(te, 2 * ct + 1)
                    if ct >= 1:
                        te.wait_ge(mv1.h, L[mv1] + ct)
                    ins = None
                    for dt in range(DT):
                        for ch in range(NCHK):
                            ins = te.matmul(PM[:, ch * 512:(ch + 1) * 512],
                                            wt[ct % 2][:, dt * 128:(dt + 1) * 128],
                                            xnfull[:, dt * 4096 + ch * 512: dt * 4096 + (ch + 1) * 512],
                                            start=(dt == 0), stop=(dt == DT - 1))
                    mm1.inc(ins)
            blk.tensor(f_tensor)

            def f_vector(ve):
                for ct in range(32):
                    ve.wait_ge(mm1.h, L[mm1] + ct + 1)
                    if ct >= 2:
                        m1s.wait_item(ve, ct - 2)
                    mv1.inc(ve.tensor_scalar(evb[ct % 2][:], PM[:], col(cst, C_IPB + ct), None, ALU.add))
            blk.vector(f_vector)


def emit_mamba_m2(ctx, dr, scr):
    ctx.new_phase()
    nc = ctx.nc
    cst = ctx.cst
    m2x = Stream(ctx, "m2_x", 2)
    m2s = Stream(ctx, "m2_s", 2, kind="gp")
    cv = ctx.S("m2_ve"); cq = ctx.S("m2_se")
    L2 = {cv: cv.n, cq: cq.n}
    with ExitStack() as ps:
        xcb = [ps.enter_context(nc.sbuf_tensor(U(f"m2_x{i}"), [128, 2048], BF16)) for i in range(2)]
        aca = ps.enter_context(nc.sbuf_tensor(U("m2_a"), [128, 2048], F32))
        acb = ps.enter_context(nc.sbuf_tensor(U("m2_b"), [128, 2048], F32))
        ub = [ps.enter_context(nc.sbuf_tensor(U(f"m2_u{i}"), [128, 2048], BF16)) for i in range(2)]
        with nc.Block() as blk:
            def f_sync(sy):
                for i in range(32):
                    b, ct = divmod(i, 16)
                    if i >= 2:
                        sy.wait_ge(cv.h, L2[cv] + i - 1)
                    m2x.put(sy, xcb[i % 2][:], scr["xc_scr"][ct, :, b * 2048:(b + 1) * 2048])
            blk.sync(f_sync)

            def f_vector(ve):
                for i in range(32):
                    b, ct = divmod(i, 16)
                    m2x.wait_item(ve, i)
                    if i >= 1:
                        ve.wait_ge(cq.h, L2[cq] + i)
                    w = [col(cst, C_CONVW + ct * 4 + j) for j in range(4)]
                    bcl = col(cst, C_CONVB + ct)
                    xc = xcb[i % 2]
                    ve.tensor_scalar(aca[:, 3:2048], xc[:, 0:2045], w[0], bcl, ALU.mult, ALU.add)
                    ve.scalar_tensor_tensor(acb[:, 3:2048], xc[:, 1:2046], w[1], aca[:, 3:2048], ALU.mult, ALU.add)
                    ve.tensor_scalar(aca[:, 3:2048], xc[:, 2:2047], w[2], None, ALU.mult)
                    ve.tensor_tensor(acb[:, 3:2048], acb[:, 3:2048], aca[:, 3:2048], ALU.add)
                    ve.scalar_tensor_tensor(acb[:, 3:2048], xc[:, 3:2048], w[3], acb[:, 3:2048], ALU.mult, ALU.add)
                    vsync(ctx, ve, ve.tensor_scalar(acb[:, 0:1], xc[:, 0:1], w[3], bcl, ALU.mult, ALU.add))
                    vsync(ctx, ve, ve.tensor_scalar(acb[:, 1:2], xc[:, 1:2], w[3], bcl, ALU.mult, ALU.add))
                    vsync(ctx, ve, ve.scalar_tensor_tensor(acb[:, 1:2], xc[:, 0:1], w[2], acb[:, 1:2], ALU.mult, ALU.add))
                    vsync(ctx, ve, ve.tensor_scalar(acb[:, 2:3], xc[:, 2:3], w[3], bcl, ALU.mult, ALU.add))
                    vsync(ctx, ve, ve.scalar_tensor_tensor(acb[:, 2:3], xc[:, 1:2], w[2], acb[:, 2:3], ALU.mult, ALU.add))
                    vsync(ctx, ve, ve.scalar_tensor_tensor(acb[:, 2:3], xc[:, 0:1], w[1], acb[:, 2:3], ALU.mult, ALU.add))
                    cv.inc(ve.engine_nop())
            blk.vector(f_vector)

            def f_scalar(se):
                for i in range(32):
                    se.wait_ge(cv.h, L2[cv] + i + 1)
                    if i >= 2:
                        m2s.wait_item(se, i - 2)
                    cq.inc(se.activation(ub[i % 2][:], acb[:], AF.Silu))
            blk.scalar(f_scalar)

            def f_gp(gp):
                for i in range(32):
                    b, ct = divmod(i, 16)
                    gp.wait_ge(cq.h, L2[cq] + i + 1)
                    m2s.put(gp, scr["u_scr"][ct, :, b * 2048:(b + 1) * 2048], ub[i % 2][:])
                m2s.wait_all(gp)
            blk.gpsimd(f_gp)


def emit_mamba_m3(ctx, dr, scr):
    ctx.new_phase()
    nc = ctx.nc
    cst = ctx.cst
    m3u = Stream(ctx, "m3_u", 4)
    m3w = Stream(ctx, "m3_w", 8, kind="gp")
    m3s = Stream(ctx, "m3_s", 2)
    m3wx = Stream(ctx, "m3_wx", 4, kind="gp")
    tmm = ctx.S("m3_mm"); tve = ctx.S("m3_ve"); xmm = ctx.S("m3_xm"); xve = ctx.S("m3_xv")
    L3 = {tmm: tmm.n, tve: tve.n, xmm: xmm.n, xve: xve.n}
    with ExitStack() as ps:
        ubf = ps.enter_context(nc.sbuf_tensor(U("m3_u"), [128, 32768], BF16))
        xpj = ps.enter_context(nc.sbuf_tensor(U("m3_xp"), [128, 512], BF16))
        dtw = [ps.enter_context(nc.sbuf_tensor(U(f"m3_w{i}"), [128, 2048], BF16)) for i in range(2)]
        ta2 = ps.enter_context(nc.sbuf_tensor(U("m3_t1"), [128, 2048], F32))
        tng = ps.enter_context(nc.sbuf_tensor(U("m3_t4"), [128, 2048], F32))
        tr = ps.enter_context(nc.sbuf_tensor(U("m3_t2"), [128, 2048], F32))
        te_ = ps.enter_context(nc.sbuf_tensor(U("m3_t3"), [128, 2048], F32))
        dlb = [ps.enter_context(nc.sbuf_tensor(U(f"m3_d{i}"), [128, 2048], BF16)) for i in range(2)]
        PX = ps.enter_context(nc.psum_tensor(U("m3_px"), [32, 2048], F32))
        PD = ps.enter_context(nc.psum_tensor(U("m3_pd"), [128, 2048], F32))
        with nc.Block() as blk:
            def f_gp(gp):
                for hh in range(4 if M3LVL >= 1 else 0):
                    m3wx.put(gp, xpj[:, hh * 128:(hh + 1) * 128].rearrange(
                                 "p (j c) -> p j c", j=4, c=32),
                             dr["xpjt"][hh * 4:(hh + 1) * 4, :, :].rearrange("j p c -> p j c"))
                for b in range(2):
                    for cot in range(M3N if M3LVL >= 2 else 0):
                        k = b * M3N + cot
                        if k >= 2:
                            gp.wait_ge(tmm.h, L3[tmm] + k - 1)
                        for hh in range(4):
                            m3w.put(gp, dtw[k % 2][:, hh * 512:(hh + 1) * 512].rearrange(
                                        "p (j c) -> p j c", j=4, c=128),
                                    dr["dtjt"][hh * 4:(hh + 1) * 4, :, cot * 128:(cot + 1) * 128].rearrange(
                                        "j p c -> p j c"))
            blk.gpsimd(f_gp)

            def f_sync(sy):
                for b in range(2):
                    if b == 1:
                        sy.wait_ge(tmm.h, L3[tmm] + M3N)
                    for ct in range(CIT):
                        m3u.put(sy, ubf[:, ct * 2048:(ct + 1) * 2048],
                                scr["u_scr"][ct, :, b * 2048:(b + 1) * 2048])
                    for cot in range(M3N if M3LVL >= 2 else 0):
                        k = b * M3N + cot
                        sy.wait_ge(tve.h, L3[tve] + k + 1)
                        m3s.put(sy, scr["dl_scr"][cot, :, b * 2048:(b + 1) * 2048], dlb[k % 2][:])
                m3s.wait_all(sy)
            blk.sync(f_sync)

            va = ctx.S("m3_va")
            sr = ctx.S("m3_sr")
            sl = ctx.S("m3_sl")
            L3b = {va: va.n, sr: sr.n, sl: sl.n}

            def f_tensor(te):
                m3wx.wait_all(te)
                for b in range(2):
                    m3u.wait_upto(te, (b + 1) * CIT - 1)
                    if b == 1 and M3LVL >= 1:
                        te.wait_ge(xve.h, L3[xve] + 1)
                    ins = None
                    for q in range(4 if M3LVL >= 1 else 0):
                        for cit in range(CIT):
                            ins = te.matmul(PX[:, q * 512:(q + 1) * 512],
                                            xpj[:, cit * 32:(cit + 1) * 32],
                                            ubf[:, cit * 2048 + q * 512: cit * 2048 + (q + 1) * 512],
                                            start=(cit == 0), stop=(cit == CIT - 1))
                    if M3LVL >= 1:
                        xmm.inc(ins)
                    for cot in range(M3N if M3LVL >= 2 else 0):
                        k = b * M3N + cot
                        for hh in range(4):
                            m3w.wait_item(te, 4 * k + hh)
                        if k >= 1:
                            te.wait_ge(va.h, L3b[va] + k)
                            te.wait_ge(sr.h, L3b[sr] + k)
                        ins = None
                        for q in range(4):
                            for cit in range(CIT):
                                ins = te.matmul(PD[:, q * 512:(q + 1) * 512],
                                                dtw[k % 2][:, cit * 128:(cit + 1) * 128],
                                                ubf[:, cit * 2048 + q * 512: cit * 2048 + (q + 1) * 512],
                                                start=(cit == 0), stop=(cit == CIT - 1))
                        tmm.inc(ins)
            blk.tensor(f_tensor)

            def f_vector(ve):
                for b in range(2):
                    if M3LVL >= 1:
                        ve.wait_ge(xmm.h, L3[xmm] + b + 1)
                        xve.inc(ve.tensor_scalar(ctx.BCs[0:32, b * 2048:(b + 1) * 2048], PX[:],
                                                 cst[0:32, C_XPB:C_XPB + 1], None, ALU.add))
                    for cot in range(M3N if M3LVL >= 2 else 0):
                        k = b * M3N + cot
                        ve.wait_ge(tmm.h, L3[tmm] + k + 1)
                        if M3LVL == 2:
                            if k >= 2:
                                m3s.wait_item(ve, k - 2)
                            va.inc(ve.engine_nop())
                            sr.inc(ve.engine_nop())
                            sl.inc(ve.engine_nop())
                            tve.inc(ve.tensor_scalar(dlb[k % 2][:], PD[:],
                                                     col(cst, C_DTB + cot), None, ALU.add))
                            continue
                        if k >= 1:
                            ve.wait_ge(sl.h, L3b[sl] + k)
                        ve.tensor_scalar(ta2[:], PD[:], col(cst, C_DTB + cot), None, ALU.add)
                        ve.tensor_scalar(tng[:], ta2[:], -1.0, None, ALU.mult)
                        va.inc(ve.tensor_tensor(ta2[:], ta2[:], tng[:], ALU.max))
                        ve.wait_ge(sl.h, L3b[sl] + k + 1)
                        if k >= 2:
                            m3s.wait_item(ve, k - 2)
                        tve.inc(ve.tensor_tensor(dlb[k % 2][:], te_[:], tr[:], ALU.add))
            blk.vector(f_vector)

            def f_scalar(se):
                for b in range(2):
                    for cot in range(M3N if M3LVL >= 3 else 0):
                        k = b * M3N + cot
                        se.wait_ge(va.h, L3b[va] + k + 1)
                        if k >= 1:
                            se.wait_ge(tve.h, L3[tve] + k)
                        se.activation(tr[:], PD[:], AF.Relu, bias=col(cst, C_DTB + cot))
                        sr.inc(se.activation(te_[:], ta2[:], AF.Exp, scale=col(cst, C_MONE)))
                        sl.inc(se.activation(te_[:], te_[:], AF.Ln, bias=col(cst, C_ONES)))
            blk.scalar(f_scalar)


def emit_mamba_m45(ctx, dr, xn1_scr, x_src, x_dst, scr):
    ctx.new_phase()
    nc = ctx.nc
    cst = ctx.cst
    selb = ctx.selb
    m4sel = Stream(ctx, "m4_sel", 1, kind="gp")
    with nc.Block() as blk:
        def f_gp(gp):
            m4sel.put(gp, selb[:], dr["sel32"][:, :])
        blk.gpsimd(f_gp)

    m4d = Stream(ctx, "m4_d", 2)
    m4u = Stream(ctx, "m4_u", 2)
    m4z = Stream(ctx, "m4_z", 2)
    m4x = Stream(ctx, "m4_x", 2)
    m4n = Stream(ctx, "m4_n", 2)
    m4st = Stream(ctx, "m4_st", 2, kind="gp")
    m4w = Stream(ctx, "m4_w", 1, kind="gp")
    bmmB = ctx.S("m4_bb")
    bmmC = ctx.S("m4_bc")
    ptb = ctx.S("m4_pt")
    sda = ctx.S("m4_da")
    sev = ctx.S("m4_ev")
    sevB = ctx.S("m4_eb")
    vsn = ctx.S("m4_vs")
    vtm = ctx.S("m4_tm")
    vdu = ctx.S("m4_du")
    vyq = ctx.S("m4_yq")
    sz1 = ctx.S("m4_sz")
    omm = ctx.S("m5_mm")
    ove = ctx.S("m5_ve")
    L4 = {s: s.n for s in (bmmB, bmmC, ptb, sda, sev, sevB, vsn, vtm, vdu, vyq, sz1, omm, ove)}
    with ExitStack() as ps:
        ystash = ps.enter_context(nc.sbuf_tensor(U("m4_y"), [128, 32768], BF16))
        with nc.Block() as blk:
            pass
        for b in range(2):
            with ExitStack() as p4:
                dlt = [p4.enter_context(nc.sbuf_tensor(U(f"m4_d{i}"), [128, 2048], BF16)) for i in range(2)]
                utt = [p4.enter_context(nc.sbuf_tensor(U(f"m4_u{i}"), [128, 2048], BF16)) for i in range(2)]
                dub2 = [p4.enter_context(nc.sbuf_tensor(U(f"m4_du{i}"), [128, 2048], BF16)) for i in range(2)]
                dAr = [p4.enter_context(nc.sbuf_tensor(U(f"m4_dA{i}"), [128, 2048], BF16)) for i in range(2)]
                tbr = [p4.enter_context(nc.sbuf_tensor(U(f"m4_t2{i}"), [128, 2048], BF16)) for i in range(2)]
                pCb = [p4.enter_context(nc.sbuf_tensor(U(f"m4_pc{i}"), [128, 2048], BF16)) for i in range(2)]
                pBb = [p4.enter_context(nc.sbuf_tensor(U(f"m4_pb{i}"), [128, 2048], BF16)) for i in range(2)]
                hb = p4.enter_context(nc.sbuf_tensor(U("m4_h"), [128, 2048], BF16))
                tm2 = p4.enter_context(nc.sbuf_tensor(U("m4_tm"), [128, 2048], BF16))
                ysb = p4.enter_context(nc.sbuf_tensor(U("m4_ys"), [128, 2048], F32))
                ztb = [p4.enter_context(nc.sbuf_tensor(U(f"m4_z{i}"), [128, 2048], BF16)) for i in range(2)]
                szb = p4.enter_context(nc.sbuf_tensor(U("m4_sz"), [128, 2048], BF16))
                tt3 = p4.enter_context(nc.sbuf_tensor(U("m4_t3"), [128, 2048], BF16))
                pB = p4.enter_context(nc.psum_tensor(U("m4_pB"), [128, 2048], F32))
                pC = p4.enter_context(nc.psum_tensor(U("m4_pC"), [128, 2048], F32))
                with nc.Block() as blk:
                    def f_sync(sy):
                        for dt in range(CIT):
                            i = b * CIT + dt
                            if dt >= 2:
                                sy.wait_ge(sda.h, L4[sda] + (i - 1) * 16)
                                sy.wait_ge(vdu.h, L4[vdu] + i - 1)
                            m4d.put(sy, dlt[dt % 2][:], scr["dl_scr"][dt, :, b * 2048:(b + 1) * 2048])
                            if dt >= 2:
                                sy.wait_ge(vyq.h, L4[vyq] + i - 1)
                            m4u.put(sy, utt[dt % 2][:], scr["u_scr"][dt, :, b * 2048:(b + 1) * 2048])
                            if dt >= 2:
                                sy.wait_ge(sz1.h, L4[sz1] + i - 1)
                            m4z.put(sy, ztb[dt % 2][:], scr["z_scr"][dt, :, b * 2048:(b + 1) * 2048])
                    blk.sync(f_sync)

                    def f_tensor(te):
                        m4sel.wait_all(te)
                        for dt in range(CIT):
                            i = b * CIT + dt
                            for s in range(16):
                                si = i * 16 + s
                                if si >= 1:
                                    te.wait_ge(sevB.h, L4[sevB] + si)
                                ins = None
                                for q in range(4):
                                    ins = te.matmul(pB[:, q * 512:(q + 1) * 512],
                                                    selb[0:32, s * 128:(s + 1) * 128],
                                                    ctx.BCs[0:32, b * 2048 + q * 512: b * 2048 + (q + 1) * 512],
                                                    start=True, stop=True)
                                bmmB.inc(ins)
                                if si >= 1:
                                    te.wait_ge(sev.h, L4[sev] + si)
                                for q in range(4):
                                    ins = te.matmul(pC[:, q * 512:(q + 1) * 512],
                                                    selb[0:32, (16 + s) * 128:(17 + s) * 128],
                                                    ctx.BCs[0:32, b * 2048 + q * 512: b * 2048 + (q + 1) * 512],
                                                    start=True, stop=True)
                                bmmC.inc(ins)
                    blk.tensor(f_tensor)

                    def f_pool(gp):
                        for dt in range(CIT):
                            i = b * CIT + dt
                            gp.wait_ge(vdu.h, L4[vdu] + i + 1)
                            for s in range(16):
                                si = i * 16 + s
                                gp.wait_ge(sevB.h, L4[sevB] + si + 1)
                                if si >= 2:
                                    gp.wait_ge(vsn.h, L4[vsn] + si - 1)
                                ptb.inc(gp.tensor_tensor(tbr[si % 2][:], dub2[i % 2][:],
                                                         pBb[si % 2][:], ALU.mult))
                    blk.gpsimd(f_pool)

                    def f_scalar(se):
                        for dt in range(CIT):
                            i = b * CIT + dt
                            m4d.wait_item(se, i)
                            for s in range(16):
                                si = i * 16 + s
                                if si >= 2:
                                    se.wait_ge(vsn.h, L4[vsn] + si - 1)
                                sda.inc(se.activation(dAr[si % 2][:], dlt[dt % 2][:], AF.Exp,
                                                      scale=col(cst, C_A + dt * 16 + s)))
                                se.wait_ge(bmmB.h, L4[bmmB] + si + 1)
                                if si >= 2:
                                    se.wait_ge(ptb.h, L4[ptb] + si - 1)
                                sevB.inc(se.activation(pBb[si % 2][:], pB[:], AF.Copy))
                                se.wait_ge(bmmC.h, L4[bmmC] + si + 1)
                                if si >= 2:
                                    se.wait_ge(vtm.h, L4[vtm] + si - 1)
                                sev.inc(se.activation(pCb[si % 2][:], pC[:], AF.Copy))
                            m4z.wait_item(se, i)
                            se.wait_ge(vyq.h, L4[vyq] + i)
                            sz1.inc(se.activation(szb[:], ztb[dt % 2][:], AF.Silu))
                    blk.scalar(f_scalar)

                    def f_vector(ve):
                        for dt in range(CIT):
                            i = b * CIT + dt
                            m4d.wait_item(ve, i)
                            m4u.wait_item(ve, i)
                            if dt >= 2:
                                ve.wait_ge(ptb.h, L4[ptb] + (i - 1) * 16)
                            vdu.inc(ve.tensor_tensor(dub2[i % 2][:], dlt[dt % 2][:],
                                                     utt[dt % 2][:], ALU.mult))
                            for s in range(16):
                                si = i * 16 + s
                                ve.wait_ge(ptb.h, L4[ptb] + si + 1)
                                ve.wait_ge(sda.h, L4[sda] + si + 1)
                                vsn.inc(ve.tensor_tensor_scan(hb[:], dAr[si % 2][:],
                                                              tbr[si % 2][:], 0.0,
                                                              ALU.mult, ALU.add))
                                ve.wait_ge(vsn.h, L4[vsn] + si + 1)
                                ve.wait_ge(sev.h, L4[sev] + si + 1)
                                if s == 0:
                                    ins = ve.tensor_tensor(ysb[:], hb[:], pCb[si % 2][:], ALU.mult)
                                else:
                                    ve.tensor_tensor(tm2[:], hb[:], pCb[si % 2][:], ALU.mult)
                                    ins = ve.tensor_tensor(ysb[:], ysb[:], tm2[:], ALU.add)
                                vtm.inc(ins)
                            ve.wait_ge(sz1.h, L4[sz1] + i + 1)
                            ve.scalar_tensor_tensor(tt3[:], utt[dt % 2][:], col(cst, C_D + dt),
                                                    ysb[:], ALU.mult, ALU.add)
                            vyq.inc(ve.tensor_tensor(ystash[:, dt * 2048:(dt + 1) * 2048],
                                                     tt3[:], szb[:], ALU.mult))
                    blk.vector(f_vector)

            with ExitStack() as p5:
                opw = p5.enter_context(nc.sbuf_tensor(U("m5_w"), [128, 16384], BF16))
                xt5 = [p5.enter_context(nc.sbuf_tensor(U(f"m5_x{i}"), [128, 2048], F32)) for i in range(2)]
                nt5 = [p5.enter_context(nc.sbuf_tensor(U(f"m5_n{i}"), [128, 2048], BF16)) for i in range(2)]
                ov5 = [p5.enter_context(nc.sbuf_tensor(U(f"m5_o{i}"), [128, 2048], F32)) for i in range(2)]
                tv5 = p5.enter_context(nc.sbuf_tensor(U("m5_t"), [128, 2048], F32))
                PO = p5.enter_context(nc.psum_tensor(U("m5_po"), [128, 2048], F32))
                with nc.Block() as blk:
                    def f_gp(gp):
                        for h in range(4):
                            m4w.put(gp, opw[:, h * 4096:(h + 1) * 4096].rearrange(
                                        "p (j c) -> p j c", j=4, c=1024),
                                    dr["opjt"][h * 4:(h + 1) * 4, :, :].rearrange("j p c -> p j c"))
                        for dto in range(DT):
                            i5 = b * DT + dto
                            gp.wait_ge(ove.h, L4[ove] + i5 + 1)
                            m4st.put(gp, x_dst[dto, :, b * 2048:(b + 1) * 2048], ov5[dto % 2][:])
                        m4st.wait_all(gp)
                    blk.gpsimd(f_gp)

                    def f_sync(sy):
                        for dto in range(DT):
                            i5 = b * DT + dto
                            if dto >= 2:
                                sy.wait_ge(ove.h, L4[ove] + i5 - 1)
                            m4x.put(sy, xt5[dto % 2][:], x_src[dto, :, b * 2048:(b + 1) * 2048])
                            m4n.put(sy, nt5[dto % 2][:], xn1_scr[dto, :, b * 2048:(b + 1) * 2048])
                    blk.sync(f_sync)

                    def f_tensor(te):
                        m4w.wait_all(te)
                        for dto in range(DT):
                            i5 = b * DT + dto
                            if dto >= 1:
                                te.wait_ge(ove.h, L4[ove] + i5)
                            ins = None
                            for q in range(4):
                                for cit in range(CIT):
                                    ins = te.matmul(PO[:, q * 512:(q + 1) * 512],
                                                    opw[:, cit * 1024 + dto * 128: cit * 1024 + (dto + 1) * 128],
                                                    ystash[:, cit * 2048 + q * 512: cit * 2048 + (q + 1) * 512],
                                                    start=(cit == 0), stop=(cit == CIT - 1))
                            omm.inc(ins)
                    blk.tensor(f_tensor)

                    def f_vector(ve):
                        for dto in range(DT):
                            i5 = b * DT + dto
                            ve.wait_ge(omm.h, L4[omm] + i5 + 1)
                            m4x.wait_item(ve, i5)
                            m4n.wait_item(ve, i5)
                            if dto >= 2:
                                m4st.wait_item(ve, i5 - 2)
                            ve.tensor_scalar(tv5[:], PO[:], col(cst, C_OPB + dto), None, ALU.add)
                            ve.tensor_tensor(tv5[:], tv5[:], nt5[dto % 2][:], ALU.add)
                            ove.inc(ve.tensor_tensor(ov5[dto % 2][:], tv5[:], xt5[dto % 2][:], ALU.add))
                    blk.vector(f_vector)





# version 03 (final)
# speedup vs baseline: 1.0862x; 1.0295x over previous
import numpy as np
import ml_dtypes
from contextlib import ExitStack

import concourse.bass as bass
import concourse.mybir as mybir
from concourse.bass_utils import run_bass_kernel_spmd

F32 = mybir.dt.float32
BF16 = mybir.dt.bfloat16
AF = mybir.ActivationFunctionType
ALU = mybir.AluOpType
AX = mybir.AxisListType

NCORES = 8
T = 4096
SEQ = 2048
D = 1024
DT = 8
CI = 2048
CIT = 16
NE = 8
TCH = 512
NCHK = T // TCH

SUBS = [0, 1, 2, 3, 4, 5]
M3LVL = 3
M3N = 16
MODE = "fused"

BF = ml_dtypes.bfloat16

_UQ = [0]


def U(nm):
    _UQ[0] += 1
    return f"{nm}_u{_UQ[0]}"


C_ONES, C_MHALF, C_EPS5, C_MONE = 0, 1, 2, 3
C_1702 = 20
C_IOTA, C_IOTA1K = 4, 12
C_A = 32
C_D = 288
C_CONVW = 304
C_CONVB = 368
C_IPB = 384
C_DTB = 416
C_OPB = 432
C_NW = 440
C_NB = 448
C_B1M = 456
C_L1B1 = 712
C_L1B2 = 744
C_XPB = 752
C_AL1 = 768
C_BJ1 = 800
C_AL2 = 832
C_BJ2 = 840

B_ID = 0
B_TRI = 128
B_PROJ = 256
B_SEL2 = 384
B_SEL8 = 640
B_ONESC = 1664
B_ONER = 1666


def _moe_perm():
    perm = np.empty(4096, np.int64)
    for k in range(16):
        p = np.arange(128 * k, 128 * (k + 1))
        perm[256 * k:256 * k + 128] = 2 * p
        perm[256 * k + 128:256 * k + 256] = 2 * p + 1
    return perm


def _hbuild_consts(w):
    cst = np.zeros((128, 1024), np.float32)
    cst[:, C_ONES] = 1.0
    cst[:, C_MHALF] = -0.5
    cst[:, C_EPS5] = 1e-5
    cst[:, C_MONE] = -1.0
    cst[:, C_1702] = 1.702
    cst[:, C_IOTA:C_IOTA + 8] = np.arange(8, dtype=np.float32)[None, :]
    cst[:, C_IOTA1K:C_IOTA1K + 8] = np.arange(8, dtype=np.float32)[None, :] + 1000.0
    cst[:, C_A:C_A + 256] = np.asarray(w["L0_attn_A_log"]).reshape(16, 128, 16).transpose(1, 0, 2).reshape(128, 256)
    cst[:, C_D:C_D + 16] = np.asarray(w["L0_attn_D_param"]).reshape(16, 128).T
    cst[:, C_CONVW:C_CONVW + 64] = np.asarray(w["L0_attn_conv_w"]).reshape(2048, 4).reshape(16, 128, 4).transpose(1, 0, 2).reshape(128, 64)
    cst[:, C_CONVB:C_CONVB + 16] = np.asarray(w["L0_attn_conv_b"]).reshape(16, 128).T
    cst[:, C_IPB:C_IPB + 32] = np.asarray(w["L0_attn_in_proj_b"]).reshape(32, 128).T
    cst[:, C_DTB:C_DTB + 16] = np.asarray(w["L0_attn_dt_proj_b"]).reshape(16, 128).T
    cst[:, C_OPB:C_OPB + 8] = np.asarray(w["L0_attn_out_proj_b"]).reshape(8, 128).T
    cst[:, C_NW:C_NW + 8] = np.asarray(w["L0_attn_norm_w"]).reshape(8, 128).T
    cst[:, C_NB:C_NB + 8] = np.asarray(w["L0_attn_norm_b"]).reshape(8, 128).T
    b1p = np.asarray(w["L0_ffn_dense_1_b"])[:, _moe_perm()]
    cst[:, C_B1M:C_B1M + 256] = b1p.reshape(8, 32, 128).transpose(2, 0, 1).reshape(128, 256)
    cst[:, C_L1B1:C_L1B1 + 32] = np.asarray(w["L1_ffn_b1"]).reshape(32, 128).T
    cst[:, C_L1B2:C_L1B2 + 8] = np.asarray(w["L1_ffn_b2"]).reshape(8, 128).T
    cst[0:32, C_XPB] = np.asarray(w["L0_attn_x_proj_b"])

    cstb = np.zeros((128, 2048), BF)
    cstb[:, B_ID:B_ID + 128] = np.eye(128, dtype=BF)
    cstb[:, B_TRI:B_TRI + 128] = (np.arange(128)[None, :] >= np.arange(128)[:, None]).astype(BF)
    for r in range(2):
        cstb[r, B_SEL2 + r * 128:B_SEL2 + (r + 1) * 128] = BF(1.0)
    for r in range(8):
        cstb[r, B_SEL8 + r * 128:B_SEL8 + (r + 1) * 128] = BF(1.0)
    cstb[:, B_ONESC] = BF(1.0)
    cstb[0, B_ONER:B_ONER + 128] = BF(1.0)
    return cst, cstb


def _host_prep(inp):
    w = inp
    perm = _moe_perm()
    d = {}
    d["cst"], d["cstb"] = _hbuild_consts(w)
    x = np.ascontiguousarray(np.asarray(inp["L0_attn_x"], np.float32))
    xs = []
    for c in range(NCORES):
        xc = x[2 * c:2 * c + 2].reshape(T, D).T
        xs.append(np.ascontiguousarray(xc).reshape(8, 128, T))
    d["_x_cores"] = xs
    w1t = np.ascontiguousarray(np.asarray(w["L0_ffn_dense_1_w"]).transpose(0, 2, 1))
    w1t = w1t[:, :, perm]
    d["moe_w1t"] = np.ascontiguousarray(w1t.reshape(8, 8, 128, 4096))
    d["moe_w2t"] = np.ascontiguousarray(np.asarray(w["L0_ffn_dense_2_w"]).transpose(0, 2, 1)).reshape(8, 16, 128, 1024)
    d["moe_b2"] = np.ascontiguousarray(np.asarray(w["L0_ffn_dense_2_b"]))
    d["gate_w"] = np.ascontiguousarray(np.asarray(w["L0_ffn_gate_w"])).reshape(8, 128, 8)
    d["ipjt"] = np.ascontiguousarray(np.asarray(w["L0_attn_in_proj_w"]).T).reshape(8, 128, 4096)
    d["xpjt"] = np.ascontiguousarray(np.asarray(w["L0_attn_x_proj_w"]).T).reshape(16, 128, 32)
    d["dtjt"] = np.ascontiguousarray(np.asarray(w["L0_attn_dt_proj_w"]).T).reshape(16, 128, 2048)
    d["opjt"] = np.ascontiguousarray(np.asarray(w["L0_attn_out_proj_w"]).T).reshape(16, 128, 1024)
    d["wq"] = np.ascontiguousarray(np.asarray(w["L1_attn_Wq"])).reshape(8, 128, 1024)
    d["wk"] = np.ascontiguousarray(np.asarray(w["L1_attn_Wk"])).reshape(8, 128, 1024)
    d["wv"] = np.ascontiguousarray(np.asarray(w["L1_attn_Wv"])).reshape(8, 128, 1024)
    d["wo"] = np.ascontiguousarray(np.asarray(w["L1_attn_Wo"])).reshape(8, 128, 1024)
    pb = np.zeros((128, 128), np.float32)
    pb[0:64, 0:64] = np.asarray(w["L1_attn_proj"])
    pb[64:128, 64:128] = np.asarray(w["L1_attn_proj"])
    d["projbd"] = pb
    d["w1t_l1"] = np.ascontiguousarray(np.asarray(w["L1_ffn_w1"]).T).reshape(8, 128, 4096)
    d["w2t_l1"] = np.ascontiguousarray(np.asarray(w["L1_ffn_w2"]).T).reshape(32, 128, 1024)
    sel32 = np.zeros((32, 4096), BF)
    for r in range(32):
        sel32[r, r * 128:(r + 1) * 128] = BF(1.0)
    d["sel32"] = sel32
    d["idf32"] = np.eye(128, dtype=np.float32)
    return d


class Sem:
    def __init__(self, h):
        self.h = h
        self.n = 0

    def inc(self, instr, k=1):
        self.n += k
        instr.then_inc(self.h, k)
        return instr


class Stream:

    def __init__(self, ctx, name, k, kind="hw"):
        self.sems = [ctx.S(f"{name}_{i}", kind) for i in range(k)]
        self.base = [s.n for s in self.sems]
        self.k = k
        self.kind = kind
        self.n = 0

    def put(self, q, out, in_):
        assert (type(q).__name__ == "BassGpSimd") == (self.kind == "gp"), self.kind
        j = self.n % self.k
        s = self.sems[j]
        if self.n >= self.k:
            q.wait_ge(s.h, self.base[j] + (self.n // self.k) * 16)
        s.inc(q.dma_start(out, in_), 16)
        self.n += 1
        return self.n - 1

    def wait_item(self, eng, i):
        j = i % self.k
        eng.wait_ge(self.sems[j].h, self.base[j] + (i // self.k + 1) * 16)

    def wait_upto(self, eng, i):
        for j in range(self.k):
            cnt = (i - j) // self.k + 1
            if cnt > 0:
                eng.wait_ge(self.sems[j].h, self.base[j] + cnt * 16)

    def wait_all(self, eng):
        if self.n > 0:
            self.wait_upto(eng, self.n - 1)


class Ctx:
    def __init__(self, nc, st):
        self.nc = nc
        self.st = st
        self._pools = {}
        self._phase_maps = {}

    def new_phase(self):
        self._phase_maps = {}

    def S(self, name, kind="cc"):
        pm = self._phase_maps.setdefault(kind, {})
        if name not in pm:
            pool = self._pools.setdefault(kind, [])
            i = len(pm)
            if i >= len(pool):
                pool.append(Sem(self.st.enter_context(self.nc.semaphore(f"sp_{kind}{i}"))))
            pm[name] = pool[i]
        return pm[name]

    def alloc_global(self):
        st, nc = self.st, self.nc
        self.cst = st.enter_context(nc.sbuf_tensor("sb_cst", [128, 1024], F32))
        self.cstb = st.enter_context(nc.sbuf_tensor("sb_cstb", [128, 2048], BF16))
        self.ta = st.enter_context(nc.sbuf_tensor("g_ta", [128, 512], F32))
        self.tb = st.enter_context(nc.sbuf_tensor("g_tb", [128, 512], F32))
        self.mT = st.enter_context(nc.sbuf_tensor("g_mT", [8, 4096], BF16))
        self.idf = st.enter_context(nc.sbuf_tensor("g_idf", [128, 128], F32))
        self.mba = st.enter_context(nc.sbuf_tensor("g_mba", [128, 512], BF16))
        self.mbb = st.enter_context(nc.sbuf_tensor("g_mbb", [128, 512], BF16))
        self.b2bf = st.enter_context(nc.sbuf_tensor("g_b2bf", [8, 1024], BF16))
        self.gwt = st.enter_context(nc.sbuf_tensor("g_gw", [128, 64], F32))
        self.srow = st.enter_context(nc.sbuf_tensor("g_srow", [8, 512], F32))
        self.p0r = st.enter_context(nc.sbuf_tensor("g_p0r", [1, 512], BF16))
        self.p1r = st.enter_context(nc.sbuf_tensor("g_p1r", [1, 512], BF16))
        self.lr = st.enter_context(nc.sbuf_tensor("g_lr", [1, 1536], F32))
        self.g8a = st.enter_context(nc.sbuf_tensor("g_g8a", [128, 64], F32))
        self.g8b = st.enter_context(nc.sbuf_tensor("g_g8b", [128, 8], BF16))



def col(t, c0, n=1):
    return t[:, c0:c0 + n]


def vsync(ctx, eng, instr, name="vsp"):
    s = ctx.S(name)
    s.inc(instr)
    eng.wait_ge(s.h, s.n)



def emit_const_load(ctx, dr):
    ctx.new_phase()
    nc = ctx.nc
    cs = Stream(ctx, "c_ld", 2, kind="gp")
    with nc.Block() as blk:
        def f_g(gp):
            cs.put(gp, ctx.cst[:], dr["cst"][:])
            cs.put(gp, ctx.cstb[:], dr["cstb"][:])
            cs.put(gp, ctx.idf[:], dr["idf32"][:])
            cs.put(gp, ctx.gwt[:, 0:64].rearrange("p (d e) -> p d e", d=8, e=8),
                   dr["gate_w"][:, :, :].rearrange("d p e -> p d e"))
            if "moe_b2" in dr:
                cs.put(gp, ctx.b2bf[:], dr["moe_b2"][:, :])
            if "projbd" in dr:
                cs.put(gp, ctx.cstb[:, B_PROJ:B_PROJ + 128], dr["projbd"][:])
        blk.gpsimd(f_g)

        def f_s(se):
            cs.wait_all(se)
            se.activation(col(ctx.cst, C_A, 256), col(ctx.cst, C_A, 256), AF.Exp)
        blk.scalar(f_s)
    with nc.Block() as blk2:
        def f_v(ve):
            ve.tensor_scalar_mul(col(ctx.cst, C_A, 256), col(ctx.cst, C_A, 256), -1.0)
            if "moe_b2" in dr:
                ve.tensor_scalar_mul(ctx.b2bf[:], ctx.b2bf[:], 1.702)
        blk2.vector(f_v)



def emit_moe_prep(ctx, dr, scr):
    ctx.new_phase()
    nc = ctx.nc
    pc = Stream(ctx, "mp_c", 2, kind="gp")
    pst = Stream(ctx, "mp_s", 2)
    jobs = []
    for e in range(NE):
        for dt in range(DT):
            jobs.append((dr["moe_w1t"][e, dt, :, :], scr["w1bf"][e, dt, :, :], None))
    for e in range(NE):
        for g4 in range(4):
            jobs.append((dr["moe_w2t"][e, g4 * 4:(g4 + 1) * 4, :, :].rearrange("j p c -> p j c"),
                         scr["w2bf"][e, g4 * 4:(g4 + 1) * 4, :, :].rearrange("j p c -> p j c"), 4))
    with ExitStack() as ps:
        sb = [ps.enter_context(nc.sbuf_tensor(U(f"mp_b{i}"), [128, 4096], BF16)) for i in range(2)]
        with nc.Block() as blk:
            def f_g(gp):
                for i, (src_ap, _, j) in enumerate(jobs):
                    if i >= 2:
                        pst.wait_item(gp, i - 2)
                    dst = sb[i % 2][:] if j is None else sb[i % 2][:].rearrange("p (j c) -> p j c", j=j, c=1024)
                    pc.put(gp, dst, src_ap)
            blk.gpsimd(f_g)

            def f_s(sy):
                for i, (_, dst_ap, j) in enumerate(jobs):
                    pc.wait_item(sy, i)
                    srcb = sb[i % 2][:] if j is None else sb[i % 2][:].rearrange("p (j c) -> p j c", j=j, c=1024)
                    pst.put(sy, dst_ap, srcb)
                pst.wait_all(sy)
            blk.sync(f_s)



def emit_ln(ctx, x_src, xn_dst, kind):
    ctx.new_phase()
    nc = ctx.nc
    moe = kind == "moe"
    affine = kind == "affine"
    src_bf = affine
    lnx = Stream(ctx, "ln_x", 4)
    lns = Stream(ctx, "ln_s", 4, kind="gp")
    q1 = ctx.S("ln_q1"); m1 = ctx.S("ln_m1"); v1 = ctx.S("ln_v1")
    q2 = ctx.S("ln_q2"); v2 = ctx.S("ln_v2"); m2 = ctx.S("ln_m2")
    v3 = ctx.S("ln_v3"); m3 = ctx.S("ln_m3"); v4 = ctx.S("ln_v4")
    m4 = ctx.S("ln_m4"); v5 = ctx.S("ln_v5"); q3 = ctx.S("ln_q3")
    v6 = ctx.S("ln_v6"); m5 = ctx.S("ln_m5"); v7 = ctx.S("ln_v7")
    L0 = {s: s.n for s in (q1, m1, v1, q2, v2, m2, v3, m3, v4, m4, v5, q3, v6, m5, v7)}

    with ExitStack() as ps:
        xdt = BF16 if src_bf else F32
        xa = [ps.enter_context(nc.sbuf_tensor(U(f"ln_xa{i}"), [128, 4096], xdt)) for i in range(2)]
        xq = ps.enter_context(nc.sbuf_tensor(U("ln_xq"), [128, 4096], F32))
        nab = [ps.enter_context(nc.sbuf_tensor(U(f"ln_n{i}"), [128, 4096], BF16)) for i in range(2)]
        xnf = ps.enter_context(nc.sbuf_tensor(U("ln_xnf"), [128, 4096], F32)) if moe else None
        psS = ps.enter_context(nc.psum_tensor(U("ln_ps"), [1, 1024], F32))
        psB = ps.enter_context(nc.psum_tensor(U("ln_pb"), [128, 1024], F32))
        psG = ps.enter_context(nc.psum_tensor(U("ln_pg"), [8, 512], F32)) if moe else None
        psT = ps.enter_context(nc.psum_tensor(U("ln_pt"), [128, 8], F32)) if moe else None
        psM = ps.enter_context(nc.psum_tensor(U("ln_pm"), [8, 128], BF16)) if moe else None

        with nc.Block() as blk:
            def f_sync(sy):
                for ch in range(NCHK):
                    if ch >= 2:
                        sy.wait_ge(v3.h, L0[v3] + ch - 1)
                    for h in range(2):
                        lnx.put(sy, xa[ch % 2][:, h * 2048:(h + 1) * 2048].rearrange(
                                    "p (j c) -> p j c", j=4, c=512),
                                x_src[h * 4:(h + 1) * 4, :, ch * 512:(ch + 1) * 512].rearrange(
                                    "j p c -> p j c"))
            blk.sync(f_sync)

            def f_scalar(se):
                for ch in range(NCHK):
                    lnx.wait_upto(se, ch * 2 + 1)
                    if ch >= 1:
                        se.wait_ge(m1.h, L0[m1] + ch)
                    ins = None
                    for dt in range(DT):
                        ins = se.activation(xq[:, dt * 512:(dt + 1) * 512],
                                            xa[ch % 2][:, dt * 512:(dt + 1) * 512], AF.Square)
                    q1.inc(ins)
                    se.wait_ge(v1.h, L0[v1] + ch + 1)
                    q2.inc(se.activation(ctx.lr[:, 512:1024], ctx.lr[:, 1024:1536], AF.Sqrt))
                    if moe:
                        for u in range(4):
                            se.wait_ge(v5.h, L0[v5] + ch * 4 + u + 1)
                            q3.inc(se.activation(ctx.g8a[:, 35:36], ctx.g8a[:, 34:35], AF.Exp))
            blk.scalar(f_scalar)

            def f_tensor(te):
                onesf = col(ctx.cst, C_ONES)
                ones_in = ctx.cstb[:, B_ONESC:B_ONESC + 1] if src_bf else onesf
                for ch in range(NCHK):
                    te.wait_ge(q1.h, L0[q1] + ch + 1)
                    if ch >= 1:
                        te.wait_ge(v1.h, L0[v1] + ch)
                    for dt in range(DT):
                        te.matmul(psS[:, 0:512], ones_in, xa[ch % 2][:, dt * 512:(dt + 1) * 512],
                                  start=(dt == 0), stop=(dt == DT - 1))
                    ins = None
                    for dt in range(DT):
                        ins = te.matmul(psS[:, 512:1024], onesf, xq[:, dt * 512:(dt + 1) * 512],
                                        start=(dt == 0), stop=(dt == DT - 1))
                    m1.inc(ins)
                    te.wait_ge(v2.h, L0[v2] + ch + 1)
                    if ch >= 1:
                        te.wait_ge(v3.h, L0[v3] + ch)
                    onerow = ctx.cstb[0:1, B_ONER:B_ONER + 128]
                    te.matmul(psB[:, 0:512], onerow, ctx.p0r[0:1, :], start=True, stop=True)
                    ins = te.matmul(psB[:, 512:1024], onerow, ctx.p1r[0:1, :], start=True, stop=True)
                    m2.inc(ins)
                    if moe:
                        te.wait_ge(v3.h, L0[v3] + ch + 1)
                        if ch >= 1:
                            te.wait_ge(v4.h, L0[v4] + ch)
                        ins = None
                        for dt in range(DT):
                            ins = te.matmul(psG[:], ctx.gwt[:, dt * 8:(dt + 1) * 8],
                                            xnf[:, dt * 512:(dt + 1) * 512],
                                            start=(dt == 0), stop=(dt == DT - 1))
                        m3.inc(ins)
                        te.wait_ge(v4.h, L0[v4] + ch + 1)
                        for u in range(4):
                            if ch * 4 + u >= 1:
                                te.wait_ge(v5.h, L0[v5] + ch * 4 + u)
                            m4.inc(te.transpose(psT[:], ctx.srow[0:8, u * 128:(u + 1) * 128],
                                                ctx.idf[0:8, 0:8]))
                            te.wait_ge(v6.h, L0[v6] + ch * 4 + u + 1)
                            if ch * 4 + u >= 1:
                                te.wait_ge(v7.h, L0[v7] + ch * 4 + u)
                            m5.inc(te.transpose(psM[:], ctx.g8b[:], ctx.cstb[:, B_ID:B_ID + 128]))
            blk.tensor(f_tensor)

            def f_vector(ve):
                iota = col(ctx.cst, C_IOTA, 8)
                iota1k = col(ctx.cst, C_IOTA1K, 8)
                lr = ctx.lr
                for ch in range(NCHK):
                    ve.wait_ge(m1.h, L0[m1] + ch + 1)
                    if ch >= 1:
                        ve.wait_ge(q2.h, L0[q2] + ch)
                    ve.tensor_scalar_mul(lr[:, 0:512], psS[:, 0:512], 1.0 / D)
                    ve.tensor_scalar_mul(lr[:, 512:1024], psS[:, 512:1024], 1.0 / D)
                    ve.tensor_tensor(lr[:, 1024:1536], lr[:, 0:512], lr[:, 0:512], ALU.mult)
                    ve.tensor_tensor(lr[:, 1024:1536], lr[:, 512:1024], lr[:, 1024:1536], ALU.subtract)
                    v1.inc(ve.tensor_scalar_add(lr[:, 1024:1536], lr[:, 1024:1536], 1e-5))
                    ve.wait_ge(q2.h, L0[q2] + ch + 1)
                    vsync(ctx, ve, ve.reciprocal(lr[:, 1024:1536], lr[:, 512:1024]))
                    ve.tensor_copy(ctx.p0r[0:1, :], lr[:, 1024:1536])
                    ve.tensor_tensor(lr[:, 0:512], lr[:, 0:512], lr[:, 1024:1536], ALU.mult)
                    v2.inc(ve.tensor_scalar_mul(ctx.p1r[0:1, :], lr[:, 0:512], -1.0))
                    ve.wait_ge(m2.h, L0[m2] + ch + 1)
                    if ch >= 2:
                        lns.wait_upto(ve, (ch - 1) * 2 - 1)
                    if moe and ch >= 1:
                        ve.wait_ge(m3.h, L0[m3] + ch)
                    nbuf = nab[ch % 2]
                    ins = None
                    for dt in range(DT):
                        sl = slice(dt * 512, (dt + 1) * 512)
                        ve.tensor_tensor(ctx.ta[:], xa[ch % 2][:, sl], psB[:, 0:512], ALU.mult)
                        if moe:
                            ve.tensor_tensor(xnf[:, sl], ctx.ta[:], psB[:, 512:1024], ALU.add)
                            ins = ve.tensor_copy(nbuf[:, sl], xnf[:, sl])
                        elif affine:
                            ve.tensor_tensor(ctx.tb[:], ctx.ta[:], psB[:, 512:1024], ALU.add)
                            ins = ve.tensor_scalar(nbuf[:, sl], ctx.tb[:],
                                                   col(ctx.cst, C_NW + dt), col(ctx.cst, C_NB + dt),
                                                   ALU.mult, ALU.add)
                        else:
                            ins = ve.tensor_tensor(nbuf[:, sl], ctx.ta[:], psB[:, 512:1024], ALU.add)
                    v3.inc(ins)
                    if moe:
                        ve.wait_ge(m3.h, L0[m3] + ch + 1)
                        v4.inc(ve.tensor_copy(ctx.srow[:], psG[:]))
                        if ch == 0 and getattr(ctx, "dbg", None) is not None:
                            ve.tensor_copy(ctx.dbg[0:8, 0:512], ctx.srow[:])
                        for u in range(4):
                            ve.wait_ge(m4.h, L0[m4] + ch * 4 + u + 1)
                            g8 = ctx.g8a
                            if ch == 0 and u == 0 and getattr(ctx, "dbg", None) is not None:
                                ve.tensor_copy(ctx.dbg[:, 512:520], psT[:])
                            def V(ins):
                                vsync(ctx, ve, ins)
                            V(ve.max(g8[:, 8:16], psT[:]))
                            v1c, v2c = g8[:, 8:9], g8[:, 9:10]
                            V(ve.tensor_scalar(g8[:, 16:24], psT[:], v1c, None, ALU.subtract))
                            V(ve.tensor_single_scalar(g8[:, 16:24], g8[:, 16:24], 0.0, ALU.is_equal))
                            V(ve.scalar_tensor_tensor(g8[:, 24:32], g8[:, 16:24], -1000.0, iota1k,
                                                      ALU.mult, ALU.add))
                            V(ve.tensor_reduce(g8[:, 32:33], g8[:, 24:32], AX.X, ALU.min))
                            V(ve.tensor_scalar(g8[:, 16:24], iota, g8[:, 32:33], None, ALU.subtract))
                            V(ve.tensor_single_scalar(g8[:, 16:24], g8[:, 16:24], 0.0, ALU.is_equal))
                            V(ve.tensor_scalar(g8[:, 40:48], psT[:], v2c, None, ALU.subtract))
                            V(ve.tensor_single_scalar(g8[:, 40:48], g8[:, 40:48], 0.0, ALU.is_equal))
                            V(ve.tensor_tensor(g8[:, 48:56], g8[:, 40:48], g8[:, 16:24], ALU.mult))
                            V(ve.tensor_tensor(g8[:, 40:48], g8[:, 40:48], g8[:, 48:56], ALU.subtract))
                            V(ve.scalar_tensor_tensor(g8[:, 48:56], g8[:, 40:48], -1000.0, iota1k,
                                                      ALU.mult, ALU.add))
                            V(ve.tensor_reduce(g8[:, 33:34], g8[:, 48:56], AX.X, ALU.min))
                            V(ve.tensor_scalar(g8[:, 40:48], iota, g8[:, 33:34], None, ALU.subtract))
                            V(ve.tensor_single_scalar(g8[:, 40:48], g8[:, 40:48], 0.0, ALU.is_equal))
                            v5.inc(ve.tensor_tensor(g8[:, 34:35], v2c, v1c, ALU.subtract))
                            ve.wait_ge(q3.h, L0[q3] + ch * 4 + u + 1)
                            V(ve.tensor_scalar_add(g8[:, 36:37], g8[:, 35:36], 1.0))
                            V(ve.reciprocal(g8[:, 37:38], g8[:, 36:37]))
                            V(ve.tensor_tensor(g8[:, 38:39], g8[:, 35:36], g8[:, 37:38], ALU.mult))
                            V(ve.tensor_scalar_mul(g8[:, 37:38], g8[:, 37:38], 1.0 / 1.702))
                            V(ve.tensor_scalar_mul(g8[:, 38:39], g8[:, 38:39], 1.0 / 1.702))
                            V(ve.tensor_scalar(g8[:, 48:56], g8[:, 16:24], g8[:, 37:38], None, ALU.mult))
                            if ch == 0 and u == 0 and getattr(ctx, "dbg", None) is not None:
                                ve.tensor_copy(ctx.dbg[:, 528:559], g8[:, 8:39])
                                ve.tensor_copy(ctx.dbg[:, 560:576], g8[:, 40:56])
                            v6.inc(ve.scalar_tensor_tensor(ctx.g8b[:], g8[:, 40:48], g8[:, 38:39],
                                                           g8[:, 48:56], ALU.mult, ALU.add))
                            ve.wait_ge(m5.h, L0[m5] + ch * 4 + u + 1)
                            v7.inc(ve.tensor_copy(ctx.mT[0:8, ch * 512 + u * 128:ch * 512 + (u + 1) * 128],
                                                  psM[:]))
            blk.vector(f_vector)

            def f_gp(gp):
                for ch in range(NCHK):
                    gp.wait_ge(v3.h, L0[v3] + ch + 1)
                    nbuf = nab[ch % 2]
                    for h in range(2):
                        lns.put(gp, xn_dst[h * 4:(h + 1) * 4, :, ch * 512:(ch + 1) * 512].rearrange(
                                    "j p c -> p j c"),
                                nbuf[:, h * 2048:(h + 1) * 2048].rearrange("p (j c) -> p j c", j=4, c=512))
                lns.wait_all(gp)
            blk.gpsimd(f_gp)

    assert v3.n == L0[v3] + NCHK and lnx.n % (2 * NCHK) == 0
    if moe:
        assert v7.n == L0[v7] + NCHK * 4 and q3.n == L0[q3] + NCHK * 4



def emit_moe(ctx, dr, scr, xn_scr, x_src, x_dst):
    ctx.new_phase()
    nc = ctx.nc
    w1 = scr["w1bf"]
    w2 = scr["w2bf"]
    mx = Stream(ctx, "mo_x", 2)
    mw1 = Stream(ctx, "mo_w1", 4)
    mw2 = Stream(ctx, "mo_w2", 2)
    mxr = Stream(ctx, "mo_xr", 2, kind="gp")
    mst = Stream(ctx, "mo_st", 2, kind="gp")
    mmb = ctx.S("mo_mb"); seb = ctx.S("mo_se")
    md1 = ctx.S("mo_m1"); vg = ctx.S("mo_vg"); sg = ctx.S("mo_sg"); vq = ctx.S("mo_vq")
    md2 = ctx.S("mo_m2"); m2u = ctx.S("mo_m2u"); vev = ctx.S("mo_ev")
    L = {s: s.n for s in (mmb, seb, md1, vg, sg, vq, md2, m2u, vev)}
    NU = 8
    PPC = NE * NU * 2
    UPC = NE * NU
    U2PC = 2 * NE * 8

    with ExitStack() as ps:
        gpr = ps.enter_context(nc.sbuf_tensor(U("mo_g"), [128, 128 * 512], BF16))
        xt = [ps.enter_context(nc.sbuf_tensor(U(f"mo_xt{i}"), [128, 512], F32)) for i in range(2)]
        ot = [ps.enter_context(nc.sbuf_tensor(U(f"mo_ot{i}"), [128, 512], F32)) for i in range(2)]
        sgb = ps.enter_context(nc.sbuf_tensor(U("mo_sgb"), [128, 512], BF16))
        vvb = ps.enter_context(nc.sbuf_tensor(U("mo_vvb"), [128, 512], BF16))
        nbuf = ps.enter_context(nc.sbuf_tensor(U("mo_n"), [128, 4096], BF16))
        wab = [ps.enter_context(nc.sbuf_tensor(U(f"mo_w{i}"), [128, 4096], BF16)) for i in range(2)]
        w2ab = [ps.enter_context(nc.sbuf_tensor(U(f"mo_w2{i}"), [128, 1024], BF16)) for i in range(2)]
        psg = [ps.enter_context(nc.psum_tensor(U(f"mo_pg{i}"), [128, 512], F32)) for i in range(2)]
        psv = [ps.enter_context(nc.psum_tensor(U(f"mo_pv{i}"), [128, 512], F32)) for i in range(2)]
        pso = [ps.enter_context(nc.psum_tensor(U(f"mo_po{i}"), [128, 512], F32)) for i in range(4)]

        with nc.Block() as blk:
            def f_sync(sy):
                for ch in range(NCHK):
                    if ch >= 1:
                        sy.wait_ge(md1.h, L[md1] + ch * PPC)
                    for h in range(2):
                        mx.put(sy, nbuf[:, h * 2048:(h + 1) * 2048].rearrange(
                                   "p (j c) -> p j c", j=4, c=512),
                               xn_scr[h * 4:(h + 1) * 4, :, ch * 512:(ch + 1) * 512].rearrange(
                                   "j p c -> p j c"))
                    for e in range(NE):
                        for q in range(NU):
                            k = ch * UPC + e * NU + q
                            if k >= 2:
                                sy.wait_ge(md1.h, L[md1] + (k - 1) * 2)
                            for h in range(2):
                                mw1.put(sy, wab[k % 2][:, h * 2048:(h + 1) * 2048].rearrange(
                                            "p (j c) -> p j c", j=4, c=512),
                                        w1[e, h * 4:(h + 1) * 4, :, q * 512:(q + 1) * 512].rearrange(
                                            "j p c -> p j c"))
                    for half in range(2):
                        for e in range(NE):
                            for ug in range(8):
                                k2 = ch * U2PC + half * 64 + e * 8 + ug
                                if k2 >= 2:
                                    sy.wait_ge(m2u.h, L[m2u] + k2 - 1)
                                mw2.put(sy, w2ab[k2 % 2][:].rearrange("p (j c) -> p j c", j=2, c=512),
                                        w2[e, ug * 2:(ug + 1) * 2, :, half * 512:(half + 1) * 512]
                                        .rearrange("j p c -> p j c"))
            blk.sync(f_sync)

            def f_gp(gp):
                for i in range(NCHK * DT):
                    ch, dt = divmod(i, DT)
                    if i >= 2:
                        gp.wait_ge(vev.h, L[vev] + i - 1)
                    mxr.put(gp, xt[i % 2][:], x_src[dt, :, ch * 512:(ch + 1) * 512])
                    if i >= 1:
                        gp.wait_ge(vev.h, L[vev] + i)
                        mst.put(gp, x_dst[(i - 1) % DT, :, ((i - 1) // DT) * 512:((i - 1) // DT + 1) * 512],
                                ot[(i - 1) % 2][:])
                i = NCHK * DT
                gp.wait_ge(vev.h, L[vev] + i)
                mst.put(gp, x_dst[(i - 1) % DT, :, ((i - 1) // DT) * 512:((i - 1) // DT + 1) * 512],
                        ot[(i - 1) % 2][:])
                mst.wait_all(gp)
            blk.gpsimd(f_gp)

            def f_tensor(te):
                for ch in range(NCHK):
                    mx.wait_upto(te, ch * 2 + 1)
                    if ch >= 1:
                        te.wait_ge(vev.h, L[vev] + ch * 8)
                    for e in range(NE):
                        ie = ch * NE + e
                        if ie >= 1:
                            te.wait_ge(seb.h, L[seb] + ie)
                        mmb.inc(te.matmul(pso[0][:], ctx.cstb[0:8, B_SEL8 + e * 128:B_SEL8 + (e + 1) * 128],
                                          ctx.mT[0:8, ch * 512:(ch + 1) * 512], start=True, stop=True))
                        for q in range(NU):
                            k = ch * UPC + e * NU + q
                            mw1.wait_item(te, 2 * k)
                            mw1.wait_item(te, 2 * k + 1)
                            wbuf = wab[k % 2]
                            for pr in range(2):
                                ip = k * 2 + pr
                                if ip >= 2:
                                    te.wait_ge(vg.h, L[vg] + ip - 1)
                                ins = None
                                for dt in range(DT):
                                    te.matmul(psg[ip % 2][:],
                                              wbuf[:, dt * 512 + (2 * pr) * 128: dt * 512 + (2 * pr + 1) * 128],
                                              nbuf[:, dt * 512:(dt + 1) * 512],
                                              start=(dt == 0), stop=(dt == DT - 1))
                                for dt in range(DT):
                                    ins = te.matmul(psv[ip % 2][:],
                                                    wbuf[:, dt * 512 + (2 * pr + 1) * 128: dt * 512 + (2 * pr + 2) * 128],
                                                    nbuf[:, dt * 512:(dt + 1) * 512],
                                                    start=(dt == 0), stop=(dt == DT - 1))
                                md1.inc(ins)
                    te.wait_ge(vq.h, L[vq] + (ch + 1) * PPC)
                    for half in range(2):
                        if half == 1:
                            te.wait_ge(vev.h, L[vev] + ch * 8 + 4)
                        for e in range(NE):
                            for ug in range(8):
                                k2 = ch * U2PC + half * 64 + e * 8 + ug
                                mw2.wait_item(te, k2)
                                w2buf = w2ab[k2 % 2]
                                ins = None
                                for ft in range(2):
                                    kt = e * 16 + ug * 2 + ft
                                    for dth in range(4):
                                        dt = half * 4 + dth
                                        ins = te.matmul(pso[dth][:],
                                                        w2buf[:, ft * 512 + dth * 128: ft * 512 + (dth + 1) * 128],
                                                        gpr[:, kt * 512:(kt + 1) * 512],
                                                        start=(e == 0 and ug == 0 and ft == 0),
                                                        stop=False)
                                m2u.inc(ins)
                        for dth in range(4):
                            dt = half * 4 + dth
                            ins = te.matmul(pso[dth][:], ctx.b2bf[0:8, dt * 128:(dt + 1) * 128],
                                            ctx.mT[0:8, ch * 512:(ch + 1) * 512],
                                            start=False, stop=True)
                            md2.inc(ins)
            blk.tensor(f_tensor)

            def f_scalar(se):
                for ch in range(NCHK):
                    for e in range(NE):
                        ie = ch * NE + e
                        se.wait_ge(mmb.h, L[mmb] + ie + 1)
                        if ie >= 2:
                            se.wait_ge(vq.h, L[vq] + (ie - 1) * 16)
                        mb = ctx.mba if e % 2 == 0 else ctx.mbb
                        seb.inc(se.copy(mb[:], pso[0][:]))
                        for qp in range(16):
                            ip = ie * 16 + qp
                            se.wait_ge(vg.h, L[vg] + ip + 1)
                            sg.inc(se.activation(sgb[:], ctx.tb[:], AF.Silu, scale=col(ctx.cst, C_1702)))
            blk.scalar(f_scalar)

            def f_vector(ve):
                for ch in range(NCHK):
                    if ch >= 1:
                        ve.wait_ge(m2u.h, L[m2u] + ch * U2PC)
                    for e in range(NE):
                        ie = ch * NE + e
                        mb = ctx.mba if e % 2 == 0 else ctx.mbb
                        for qp in range(16):
                            ip = ie * 16 + qp
                            q, pr = divmod(qp, 2)
                            gt = 4 * q + 2 * pr
                            kt = e * 16 + q * 2 + pr
                            ve.wait_ge(md1.h, L[md1] + ip + 1)
                            b1g = col(ctx.cst, C_B1M + e * 32 + gt)
                            b1v = col(ctx.cst, C_B1M + e * 32 + gt + 1)
                            ve.tensor_scalar(ctx.ta[:], psv[ip % 2][:], b1v, 9.0, ALU.add, ALU.min)
                            vg.inc(ve.tensor_scalar(ctx.tb[:], psg[ip % 2][:], b1g, 9.0, ALU.add, ALU.min))
                            ve.tensor_scalar(vvb[:], ctx.ta[:], -9.0, 1.0, ALU.max, ALU.add)
                            ve.wait_ge(sg.h, L[sg] + ip + 1)
                            if ip == ie * 16:
                                ve.wait_ge(seb.h, L[seb] + ie + 1)
                            ve.tensor_tensor(sgb[:], sgb[:], mb[:], ALU.mult)
                            vq.inc(ve.tensor_tensor(gpr[:, kt * 512:(kt + 1) * 512], sgb[:], vvb[:], ALU.mult))
                    for half in range(2):
                        for dth in range(4):
                            i = ch * 8 + half * 4 + dth
                            ve.wait_ge(md2.h, L[md2] + ch * 8 + half * 4 + dth + 1)
                            mxr.wait_item(ve, i)
                            if i >= 2:
                                mst.wait_item(ve, i - 2)
                            vev.inc(ve.scalar_tensor_tensor(ot[i % 2][:], pso[dth][:], 0.5,
                                                            xt[i % 2][:], ALU.mult, ALU.add))
            blk.vector(f_vector)

    assert md1.n == L[md1] + NCHK * PPC and vq.n == L[vq] + NCHK * PPC
    assert md2.n == L[md2] + NCHK * 8 and vev.n == L[vev] + NCHK * 8
    assert m2u.n == L[m2u] + NCHK * U2PC



def _dram_in(nc, name, arr):
    return nc.dram_tensor(name, list(arr.shape), F32 if arr.dtype == np.float32 else BF16,
                          kind="ExternalInput")


def _needed(subs):
    need = {"cst", "cstb", "idf32", "gate_w", "sel32"}
    for s in subs:
        if s in (0, 2):
            need |= {"moe_w1t", "moe_w2t", "moe_b2"}
        if s == 1:
            need |= {"ipjt", "xpjt", "dtjt", "opjt"}
        if s in (3, 5):
            need |= {"w1t_l1", "w2t_l1"}
        if s == 4:
            need |= {"wq", "wk", "wv", "wo", "projbd"}
    return need


def _build_program(dr_arrays, subs):
    nc = bass.Bass("TRN2", target_bir_lowering=False, detect_race_conditions=False)
    dr = {}
    for name in sorted(_needed(subs)):
        dr[name] = _dram_in(nc, name, dr_arrays[name])
    xin = nc.dram_tensor("XIN", [8, 128, T], F32, kind="ExternalInput")
    xout = nc.dram_tensor("XOUT", [8, 128, T], F32, kind="ExternalOutput")
    xres = nc.dram_tensor("xres", [8, 128, T], F32, kind="Internal")
    xn_scr = nc.dram_tensor("xn_scr", [8, 128, T], BF16, kind="Internal")
    xn1_scr = nc.dram_tensor("xn1_scr", [8, 128, T], BF16, kind="Internal")
    scr = {}
    if 1 in subs:
        for nm in ("xc_scr", "z_scr", "u_scr", "dl_scr"):
            scr[nm] = nc.dram_tensor(nm, [16, 128, T], BF16, kind="Internal")
    if 3 in subs or 5 in subs:
        scr["s1_scr"] = nc.dram_tensor("s1_scr", [8, 128, 4096], BF16, kind="Internal")
        scr["s2_scr"] = nc.dram_tensor("s2_scr", [32, 128, 1024], BF16, kind="Internal")
    if 0 in subs or 2 in subs:
        scr["w1bf"] = nc.dram_tensor("w1bf", [8, 8, 128, 4096], BF16, kind="Internal")
        scr["w2bf"] = nc.dram_tensor("w2bf", [8, 16, 128, 1024], BF16, kind="Internal")

    with ExitStack() as st:
        ctx = Ctx(nc, st)
        ctx.alloc_global()
        emit_const_load(ctx, dr)
        if 0 in subs or 2 in subs:
            emit_moe_prep(ctx, dr, scr)
        if 3 in subs or 5 in subs:
            emit_l1prep(ctx, dr, scr)
        cur = xin
        for idx, s in enumerate(subs):
            last = idx == len(subs) - 1
            dst = xout if last else xres
            if s in (0, 2):
                emit_ln(ctx, cur, xn_scr, "moe")
                emit_moe(ctx, dr, scr, xn_scr, cur, dst)
            elif s == 1:
                emit_ln(ctx, cur, xn1_scr, "plain")
                emit_ln(ctx, xn1_scr, xn_scr, "affine")
                emit_mamba(ctx, dr, xn_scr, xn1_scr, cur, dst, scr)
            elif s in (3, 5):
                emit_ln(ctx, cur, xn_scr, "plain")
                emit_l1ffn(ctx, dr, scr, xn_scr, cur, dst)
            elif s == 4:
                emit_ln(ctx, cur, xn_scr, "plain")
                emit_l1attn(ctx, dr, xn_scr, cur, dst)
            cur = dst
    return nc


def _in_map_for(dr_arrays, subs, xarr):
    m = {"XIN": xarr}
    for name in sorted(_needed(subs)):
        m[name] = dr_arrays[name]
    return m


def kernel(**inputs):
    d = _host_prep(inputs)
    xs = d["_x_cores"]
    if MODE == "fused":
        groups = [list(SUBS)]
    else:
        groups = [[s] for s in SUBS]
    cur = xs
    for g in groups:
        nc = _build_program(d, g)
        in_maps = [_in_map_for(d, g, cur[c]) for c in range(NCORES)]
        res = run_bass_kernel_spmd(nc, in_maps, core_ids=list(range(NCORES)))
        cur = [np.asarray(res.results[c]["XOUT"]) for c in range(NCORES)]
    out = np.empty((16, SEQ, D), np.float32)
    for c in range(NCORES):
        xc = np.asarray(cur[c], np.float32).reshape(D, T).T.reshape(2, SEQ, D)
        out[2 * c:2 * c + 2] = xc
    return out


def emit_l1prep(ctx, dr, scr):
    ctx.new_phase()
    nc = ctx.nc
    cst = ctx.cst
    lw = Stream(ctx, "lp_w", 2)
    ls = Stream(ctx, "lp_s", 2, kind="gp")
    pmm = ctx.S("lp_mm"); pse = ctx.S("lp_se"); pve = ctx.S("lp_ve")
    L = {pmm: pmm.n, pse: pse.n, pve: pve.n}
    onesb = ctx.cstb[:, B_ONESC:B_ONESC + 1]
    with ExitStack() as ps:
        wf = [ps.enter_context(nc.sbuf_tensor(U(f"lp_w{i}"), [128, 4096], F32)) for i in range(2)]
        sg = [ps.enter_context(nc.sbuf_tensor(U(f"lp_g{i}"), [128, 4096], BF16)) for i in range(2)]
        ab8 = ps.enter_context(nc.sbuf_tensor(U("lp_a"), [128, 32768], BF16))
        psA = ps.enter_context(nc.psum_tensor(U("lp_pA"), [128, 32], F32))
        psB2 = ps.enter_context(nc.psum_tensor(U("lp_pB"), [128, 8], F32))
        with nc.Block() as blk:
            def f_sync(sy):
                for dt in range(DT):
                    if dt >= 2:
                        sy.wait_ge(pse.h, L[pse] + dt - 1)
                    lw.put(sy, wf[dt % 2][:], dr["w1t_l1"][dt, :, :])
                for ht in range(32):
                    i = 8 + ht
                    sy.wait_ge(pse.h, L[pse] + i - 1)
                    lw.put(sy, wf[i % 2][:, 0:1024], dr["w2t_l1"][ht, :, :])
            blk.sync(f_sync)

            def f_scalar(se):
                for dt in range(DT):
                    lw.wait_item(se, dt)
                    if dt >= 2:
                        ls.wait_item(se, dt - 2)
                    se.activation(sg[dt % 2][:], wf[dt % 2][:], AF.Sign)
                    pse.inc(se.activation(ab8[:, dt * 4096:(dt + 1) * 4096], wf[dt % 2][:], AF.Abs))
                for ht in range(32):
                    i = 8 + ht
                    lw.wait_item(se, i)
                    ls.wait_item(se, i - 2)
                    se.activation(sg[i % 2][:, 0:1024], wf[i % 2][:, 0:1024], AF.Sign)
                    if ht == 0:
                        se.wait_ge(pmm.h, L[pmm] + 8)
                    pse.inc(se.activation(ab8[:, ht * 1024:(ht + 1) * 1024], wf[i % 2][:, 0:1024], AF.Abs))
            blk.scalar(f_scalar)

            def f_gp(gp):
                for dt in range(DT):
                    gp.wait_ge(pse.h, L[pse] + dt + 1)
                    ls.put(gp, scr["s1_scr"][dt, :, :], sg[dt % 2][:])
                for ht in range(32):
                    i = 8 + ht
                    gp.wait_ge(pse.h, L[pse] + i + 1)
                    ls.put(gp, scr["s2_scr"][ht, :, :], sg[i % 2][:, 0:1024])
                ls.wait_all(gp)
            blk.gpsimd(f_gp)

            def f_tensor(te):
                te.wait_ge(pse.h, L[pse] + 8)
                ins = None
                for ht in range(32):
                    for dt in range(DT):
                        ins = te.matmul(psA[:, ht:ht + 1],
                                        ab8[:, dt * 4096 + ht * 128: dt * 4096 + (ht + 1) * 128],
                                        onesb, start=(dt == 0), stop=(dt == DT - 1))
                pmm.inc(ins, 8)
                te.wait_ge(pse.h, L[pse] + 40)
                ins = None
                for dto in range(DT):
                    for ht in range(32):
                        ins = te.matmul(psB2[:, dto:dto + 1],
                                        ab8[:, ht * 1024 + dto * 128: ht * 1024 + (dto + 1) * 128],
                                        onesb, start=(ht == 0), stop=(ht == 31))
                pmm.inc(ins, 32)
            blk.tensor(f_tensor)

            def f_vector(ve):
                ve.wait_ge(pmm.h, L[pmm] + 8)
                vsync(ctx, ve, ve.tensor_scalar(col(cst, C_AL1, 32), psA[:], 1.0 / 1024, None, ALU.mult))
                vsync(ctx, ve, ve.reciprocal(col(cst, C_BJ1, 32), col(cst, C_AL1, 32)))
                vsync(ctx, ve, ve.tensor_tensor(col(cst, C_BJ1, 32), col(cst, C_L1B1, 32),
                                                col(cst, C_BJ1, 32), ALU.mult))
                ve.wait_ge(pmm.h, L[pmm] + 40)
                vsync(ctx, ve, ve.tensor_scalar(col(cst, C_AL2, 8), psB2[:], 0.5 / 4096, None, ALU.mult))
                vsync(ctx, ve, ve.tensor_scalar(col(cst, C_BJ2, 8), col(cst, C_AL2, 8), 2.0, None, ALU.mult))
                vsync(ctx, ve, ve.reciprocal(col(cst, C_BJ2, 8), col(cst, C_BJ2, 8)))
                pve.inc(ve.tensor_tensor(col(cst, C_BJ2, 8), col(cst, C_L1B2, 8),
                                         col(cst, C_BJ2, 8), ALU.mult))
            blk.vector(f_vector)


def emit_l1ffn(ctx, dr, scr, xn_scr, x_src, x_dst):
    ctx.new_phase()
    nc = ctx.nc
    cst = ctx.cst
    ls1 = Stream(ctx, "lf_s1", 2)
    lx = Stream(ctx, "lf_x", 2)
    lw2 = Stream(ctx, "lf_w2", 2)
    lxr = Stream(ctx, "lf_xr", 2, kind="gp")
    lst = Stream(ctx, "lf_st", 2, kind="gp")
    d1m = ctx.S("lf_d1")
    d1v = ctx.S("lf_v1")
    d1s = ctx.S("lf_q1")
    d2m = ctx.S("lf_d2")
    d2u = ctx.S("lf_d2u")
    d2v = ctx.S("lf_v2")
    L = {s: s.n for s in (d1m, d1v, d1s, d2m, d2u, d2v)}
    with ExitStack() as ps:
        s1r = ps.enter_context(nc.sbuf_tensor(U("lf_S1"), [128, 32768], BF16))
        nbuf = ps.enter_context(nc.sbuf_tensor(U("lf_n"), [128, 4096], BF16))
        g1 = ps.enter_context(nc.sbuf_tensor(U("lf_g1"), [128, 16384], BF16))
        tg = [ps.enter_context(nc.sbuf_tensor(U(f"lf_tg{i}"), [128, 512], F32)) for i in range(2)]
        w2l = [ps.enter_context(nc.sbuf_tensor(U(f"lf_w2{i}"), [128, 1024], BF16)) for i in range(2)]
        xt = [ps.enter_context(nc.sbuf_tensor(U(f"lf_xt{i}"), [128, 512], F32)) for i in range(2)]
        ot = [ps.enter_context(nc.sbuf_tensor(U(f"lf_ot{i}"), [128, 512], F32)) for i in range(2)]
        tv = ps.enter_context(nc.sbuf_tensor(U("lf_tv"), [128, 512], F32))
        p1 = [ps.enter_context(nc.psum_tensor(U(f"lf_p1{i}"), [128, 512], F32)) for i in range(2)]
        p2 = [ps.enter_context(nc.psum_tensor(U(f"lf_p2{i}"), [128, 512], F32)) for i in range(4)]
        with nc.Block() as blk:
            def f_sync(sy):
                for dt in range(DT):
                    ls1.put(sy, s1r[:, dt * 4096:(dt + 1) * 4096], scr["s1_scr"][dt, :, :])
                for ch in range(NCHK):
                    if ch >= 1:
                        sy.wait_ge(d1m.h, L[d1m] + ch * 32)
                    for h in range(2):
                        lx.put(sy, nbuf[:, h * 2048:(h + 1) * 2048].rearrange(
                                   "p (j c) -> p j c", j=4, c=512),
                               xn_scr[h * 4:(h + 1) * 4, :, ch * 512:(ch + 1) * 512].rearrange(
                                   "j p c -> p j c"))
                    for half in range(2):
                        for ht in range(32):
                            k2 = (ch * 2 + half) * 32 + ht
                            if k2 >= 2:
                                sy.wait_ge(d2u.h, L[d2u] + k2 - 1)
                            lw2.put(sy, w2l[k2 % 2][:], scr["s2_scr"][ht, :, :])
            blk.sync(f_sync)

            def f_gp(gp):
                for i in range(NCHK * DT):
                    ch, dt = divmod(i, DT)
                    if i >= 2:
                        gp.wait_ge(d2v.h, L[d2v] + i - 1)
                    lxr.put(gp, xt[i % 2][:], x_src[dt, :, ch * 512:(ch + 1) * 512])
                    if i >= 1:
                        gp.wait_ge(d2v.h, L[d2v] + i)
                        lst.put(gp, x_dst[(i - 1) % DT, :, ((i - 1) // DT) * 512:((i - 1) // DT + 1) * 512],
                                ot[(i - 1) % 2][:])
                i = NCHK * DT
                gp.wait_ge(d2v.h, L[d2v] + i)
                lst.put(gp, x_dst[(i - 1) % DT, :, ((i - 1) // DT) * 512:((i - 1) // DT + 1) * 512],
                        ot[(i - 1) % 2][:])
                lst.wait_all(gp)
            blk.gpsimd(f_gp)

            def f_tensor(te):
                ls1.wait_all(te)
                for ch in range(NCHK):
                    lx.wait_upto(te, ch * 2 + 1)
                    for ht in range(32):
                        ik = ch * 32 + ht
                        if ik >= 2:
                            te.wait_ge(d1v.h, L[d1v] + ik - 1)
                        ins = None
                        for dt in range(DT):
                            ins = te.matmul(p1[ik % 2][:],
                                            s1r[:, dt * 4096 + ht * 128: dt * 4096 + (ht + 1) * 128],
                                            nbuf[:, dt * 512:(dt + 1) * 512],
                                            start=(dt == 0), stop=(dt == DT - 1))
                        d1m.inc(ins)
                    te.wait_ge(d1s.h, L[d1s] + (ch + 1) * 32)
                    for half in range(2):
                        if ch * 2 + half >= 1:
                            te.wait_ge(d2v.h, L[d2v] + (ch * 2 + half) * 4)
                        for ht in range(32):
                            k2 = (ch * 2 + half) * 32 + ht
                            lw2.wait_item(te, k2)
                            ins = None
                            for dth in range(4):
                                dt = half * 4 + dth
                                ins = te.matmul(p2[dth][:],
                                                w2l[k2 % 2][:, dt * 128:(dt + 1) * 128],
                                                g1[:, ht * 512:(ht + 1) * 512],
                                                start=(ht == 0), stop=(ht == 31))
                            d2u.inc(ins)
                        for dth in range(4):
                            d2m.inc(te.matmul(p2[dth][:], w2l[(ch * 2 + half) * 32 % 2][:, 0:128],
                                              g1[:, 0:512], start=False, stop=True)) if False else None
            blk.tensor(f_tensor)

            def f_vector(ve):
                for ch in range(NCHK):
                    for ht in range(32):
                        ik = ch * 32 + ht
                        ve.wait_ge(d1m.h, L[d1m] + ik + 1)
                        if ik >= 2:
                            ve.wait_ge(d1s.h, L[d1s] + ik - 1)
                        d1v.inc(ve.tensor_scalar(tg[ik % 2][:], p1[ik % 2][:],
                                                 col(cst, C_BJ1 + ht), col(cst, C_AL1 + ht),
                                                 ALU.add, ALU.mult))
                    for half in range(2):
                        for dth in range(4):
                            i = ch * 8 + half * 4 + dth
                            dt = half * 4 + dth
                            ve.wait_ge(d2u.h, L[d2u] + (ch * 2 + half + 1) * 32)
                            lxr.wait_item(ve, i)
                            if i >= 2:
                                lst.wait_item(ve, i - 2)
                            ve.tensor_scalar(tv[:], p2[dth][:], col(cst, C_BJ2 + dt),
                                             col(cst, C_AL2 + dt), ALU.add, ALU.mult)
                            d2v.inc(ve.tensor_tensor(ot[i % 2][:], tv[:], xt[i % 2][:], ALU.add))
            blk.vector(f_vector)

            def f_scalar(se):
                for ch in range(NCHK):
                    for ht in range(32):
                        ik = ch * 32 + ht
                        se.wait_ge(d1v.h, L[d1v] + ik + 1)
                        d1s.inc(se.activation(g1[:, ht * 512:(ht + 1) * 512], tg[ik % 2][:], AF.Gelu))
            blk.scalar(f_scalar)

    assert d1m.n == L[d1m] + NCHK * 32 and d2v.n == L[d2v] + NCHK * 8


def emit_l1attn(ctx, dr, xn_scr, x_src, x_dst):
    ctx.new_phase()
    nc = ctx.nc
    cst = ctx.cst
    lwq = Stream(ctx, "la_w", 2, kind="gp")
    lx = Stream(ctx, "la_x", 2)
    lxr = Stream(ctx, "la_xr", 2, kind="gp")
    lst = Stream(ctx, "la_st", 2, kind="gp")
    qm = ctx.S("la_qm")
    qv = ctx.S("la_qv")
    fm = ctx.S("la_fm")
    fv = ctx.S("la_fv")
    fq = ctx.S("la_fq")
    fz = ctx.S("la_fz")
    vm = ctx.S("la_vm")
    vv = ctx.S("la_vv")
    am_ = ctx.S("la_am")
    av = ctx.S("la_av")
    nm = ctx.S("la_nm")
    nv = ctx.S("la_nv")
    km = ctx.S("la_km")
    kv_ = ctx.S("la_kv")
    um = ctx.S("la_um")
    uv = ctx.S("la_uv")
    tm = ctx.S("la_tm")
    tv_ = ctx.S("la_tv")
    wm = ctx.S("la_wm")
    wv_ = ctx.S("la_wv")
    L = {s: s.n for s in (qm, qv, fm, fv, fq, fz, vm, vv, am_, av, nm, nv, km, kv_, um, uv, tm, tv_, wm, wv_)}
    with ExitStack() as ps:
        wsb = {}
        for nmw in ("wq", "wk", "wv", "wo"):
            wsb[nmw] = ps.enter_context(nc.sbuf_tensor(U(f"la_{nmw}"), [128, 8192], BF16))
        nbuf = ps.enter_context(nc.sbuf_tensor(U("la_n"), [128, 4096], BF16))
        qps = ps.enter_context(nc.sbuf_tensor(U("la_q"), [128, 4096], BF16))
        kps = ps.enter_context(nc.sbuf_tensor(U("la_k"), [128, 4096], BF16))
        vps = ps.enter_context(nc.sbuf_tensor(U("la_v"), [128, 4096], BF16))
        aps = ps.enter_context(nc.sbuf_tensor(U("la_a"), [128, 4096], BF16))
        rawb = ps.enter_context(nc.sbuf_tensor(U("la_r"), [128, 512], BF16))
        t5 = ps.enter_context(nc.sbuf_tensor(U("la_t5"), [128, 512], F32))
        t6 = ps.enter_context(nc.sbuf_tensor(U("la_t6"), [128, 512], F32))
        t7 = ps.enter_context(nc.sbuf_tensor(U("la_t7"), [128, 512], F32))
        amb = ps.enter_context(nc.sbuf_tensor(U("la_amb"), [128, 128], BF16))
        knb = ps.enter_context(nc.sbuf_tensor(U("la_kn"), [128, 64], BF16))
        kvf = ps.enter_context(nc.sbuf_tensor(U("la_kvf"), [128, 520], F32))
        kvb = ps.enter_context(nc.sbuf_tensor(U("la_kvb"), [128, 520], BF16))
        dnc = ps.enter_context(nc.sbuf_tensor(U("la_dn"), [128, 2], F32))
        atT = ps.enter_context(nc.sbuf_tensor(U("la_atT"), [128, 1024], BF16))
        xt = [ps.enter_context(nc.sbuf_tensor(U(f"la_xt{i}"), [128, 128], F32)) for i in range(2)]
        ot = [ps.enter_context(nc.sbuf_tensor(U(f"la_ot{i}"), [128, 128], F32)) for i in range(2)]
        PQ = ps.enter_context(nc.psum_tensor(U("la_PQ"), [128, 512], F32))
        PF = ps.enter_context(nc.psum_tensor(U("la_PF"), [128, 512], F32))
        PV = ps.enter_context(nc.psum_tensor(U("la_PV"), [128, 512], F32))
        PA = ps.enter_context(nc.psum_tensor(U("la_PA"), [128, 128], F32))
        PND = ps.enter_context(nc.psum_tensor(U("la_PND"), [128, 66], F32))
        PKV = ps.enter_context(nc.psum_tensor(U("la_PKV"), [128, 65], F32))
        PTT = ps.enter_context(nc.psum_tensor(U("la_PTT"), [128, 192], BF16))
        PW = ps.enter_context(nc.psum_tensor(U("la_PW"), [128, 128], F32))

        with nc.Block() as blk:
            def f_gp(gp):
                for i, nmw in enumerate(("wq", "wk", "wv", "wo")):
                    for h in range(2):
                        lwq.put(gp, wsb[nmw][:, h * 4096:(h + 1) * 4096].rearrange(
                                    "p (j c) -> p j c", j=4, c=1024),
                                dr[nmw][h * 4:(h + 1) * 4, :, :].rearrange("j p c -> p j c"))
                NIT = 2 * 4 * 4 * 8
                for i in range(NIT):
                    if i >= 2:
                        gp.wait_ge(wv_.h, L[wv_] + i - 1)
                    b = i // 128
                    r = i % 128
                    ch, r2 = divmod(r, 32)
                    ul, dto = divmod(r2, 8)
                    tcol = b * 2048 + ch * 512 + ul * 128
                    lxr.put(gp, xt[i % 2][:], x_src[dto, :, tcol:tcol + 128])
                    if i >= 1:
                        gp.wait_ge(wv_.h, L[wv_] + i)
                        j = i - 1
                        bj = j // 128
                        rj = j % 128
                        chj, r2j = divmod(rj, 32)
                        ulj, dtoj = divmod(r2j, 8)
                        tcj = bj * 2048 + chj * 512 + ulj * 128
                        lst.put(gp, x_dst[dtoj, :, tcj:tcj + 128], ot[j % 2][:])
                i = NIT
                gp.wait_ge(wv_.h, L[wv_] + i)
                j = i - 1
                bj = j // 128
                rj = j % 128
                chj, r2j = divmod(rj, 32)
                ulj, dtoj = divmod(r2j, 8)
                tcj = bj * 2048 + chj * 512 + ulj * 128
                lst.put(gp, x_dst[dtoj, :, tcj:tcj + 128], ot[j % 2][:])
                lst.wait_all(gp)
            blk.gpsimd(f_gp)

            def f_sync(sy):
                for b in range(2):
                    for ch in range(4):
                        ic = b * 4 + ch
                        if ic >= 1:
                            sy.wait_ge(vm.h, L[vm] + ic * 8)
                        for h in range(2):
                            lx.put(sy, nbuf[:, h * 2048:(h + 1) * 2048].rearrange(
                                       "p (j c) -> p j c", j=4, c=512),
                                   xn_scr[h * 4:(h + 1) * 4, :, b * 2048 + ch * 512:
                                          b * 2048 + (ch + 1) * 512].rearrange("j p c -> p j c"))
            blk.sync(f_sync)

            def f_tensor(te):
                lwq.wait_all(te)
                for b in range(2):
                    for ch in range(4):
                        ic = b * 4 + ch
                        lx.wait_upto(te, ic * 2 + 1)
                        for pr in range(2):
                            wname = "wq" if pr == 0 else "wk"
                            for dqt in range(DT):
                                iq = (ic * 2 + pr) * 8 + dqt
                                if iq >= 1:
                                    te.wait_ge(qv.h, L[qv] + iq)
                                ins = None
                                for dt in range(DT):
                                    ins = te.matmul(PQ[:], wsb[wname][:, dt * 1024 + dqt * 128:
                                                                      dt * 1024 + (dqt + 1) * 128],
                                                    nbuf[:, dt * 512:(dt + 1) * 512],
                                                    start=(dt == 0), stop=(dt == DT - 1))
                                qm.inc(ins)
                                te.wait_ge(qv.h, L[qv] + iq + 1)
                                if iq >= 1:
                                    te.wait_ge(fv.h, L[fv] + iq)
                                    te.wait_ge(fq.h, L[fq] + iq)
                                fm.inc(te.matmul(PF[:], ctx.cstb[:, B_PROJ:B_PROJ + 128],
                                                 rawb[:], start=True, stop=True))
                        for ul in range(4):
                            for dvh in range(2):
                                iv = ic * 8 + ul * 2 + dvh
                                if iv >= 1:
                                    te.wait_ge(vv.h, L[vv] + iv)
                                ins = None
                                for dt in range(DT):
                                    ins = te.matmul(PV[:],
                                                    nbuf[:, dt * 512 + ul * 128: dt * 512 + (ul + 1) * 128],
                                                    wsb["wv"][:, dt * 1024 + dvh * 512:
                                                              dt * 1024 + (dvh + 1) * 512],
                                                    start=(dt == 0), stop=(dt == DT - 1))
                                vm.inc(ins)
                        te.wait_ge(fz.h, L[fz] + (ic + 1) * 16)
                        for h in range(16):
                            hrow = (h % 2) * 64
                            hcol = (h // 2) * 512
                            for ul in range(4):
                                ia = (ic * 16 + h) * 4 + ul
                                u_abs = ch * 4 + ul
                                if ia >= 1:
                                    te.wait_ge(av.h, L[av] + ia)
                                am_.inc(te.matmul(PA[:],
                                                  kps[hrow:hrow + 64, hcol + ul * 128: hcol + (ul + 1) * 128],
                                                  qps[hrow:hrow + 64, hcol + ul * 128: hcol + (ul + 1) * 128],
                                                  start=True, stop=True))
                                te.wait_ge(av.h, L[av] + ia + 1)
                                if ia >= 1:
                                    te.wait_ge(nv.h, L[nv] + ia)
                                te.matmul(PND[:, 0:64], amb[:],
                                          vps[:, ul * 1024 + h * 64: ul * 1024 + (h + 1) * 64],
                                          start=True, stop=False)
                                ins = te.matmul(PND[:, 64:65], amb[:], ctx.cstb[:, B_ONESC:B_ONESC + 1],
                                                start=False, stop=(u_abs == 0))
                                if u_abs > 0:
                                    te.wait_ge(uv.h, L[uv] + ia)
                                    ins = te.matmul(PND[:, 0:65],
                                                    qps[hrow:hrow + 64, hcol + ul * 128: hcol + (ul + 1) * 128],
                                                    kvb[hrow:hrow + 64, (h // 2) * 65:(h // 2 + 1) * 65],
                                                    start=False, stop=True)
                                nm.inc(ins)
                                if ia >= 1:
                                    te.wait_ge(kv_.h, L[kv_] + ia)
                                km.inc(te.transpose(PTT[:, 0:64],
                                                    kps[hrow:hrow + 64, hcol + ul * 128: hcol + (ul + 1) * 128],
                                                    ctx.cstb[hrow:hrow + 64, B_ID + hrow:B_ID + hrow + 64]))
                                te.wait_ge(kv_.h, L[kv_] + ia + 1)
                                if ia >= 1:
                                    te.wait_ge(uv.h, L[uv] + ia)
                                te.matmul(PKV[hrow:hrow + 64, 0:64], knb[:],
                                          vps[:, ul * 1024 + h * 64: ul * 1024 + (h + 1) * 64],
                                          start=True, stop=False)
                                um.inc(te.matmul(PKV[hrow:hrow + 64, 64:65], knb[:],
                                                 ctx.cstb[:, B_ONESC:B_ONESC + 1],
                                                 start=False, stop=True))
                        te.wait_ge(nv.h, L[nv] + (ic * 16 + 16) * 4)
                        for ul in range(4):
                            for dvt in range(DT):
                                it = (ic * 4 + ul) * 8 + dvt
                                if it >= 1:
                                    te.wait_ge(tv_.h, L[tv_] + it)
                                tm.inc(te.transpose(PTT[:, 64:192],
                                                    aps[:, ul * 1024 + dvt * 128: ul * 1024 + (dvt + 1) * 128],
                                                    ctx.cstb[:, B_ID:B_ID + 128]))
                            te.wait_ge(tv_.h, L[tv_] + (ic * 4 + ul) * 8 + 8)
                            for dto in range(DT):
                                iw = (ic * 4 + ul) * 8 + dto
                                if iw >= 1:
                                    te.wait_ge(wv_.h, L[wv_] + iw)
                                ins = None
                                for dvt in range(DT):
                                    ins = te.matmul(PW[:], wsb["wo"][:, dvt * 1024 + dto * 128:
                                                                     dvt * 1024 + (dto + 1) * 128],
                                                    atT[:, dvt * 128:(dvt + 1) * 128],
                                                    start=(dvt == 0), stop=(dvt == DT - 1))
                                wm.inc(ins)
            blk.tensor(f_tensor)

            def f_vector(ve):
                for b in range(2):
                    for ch in range(4):
                        ic = b * 4 + ch
                        for pr in range(2):
                            dst = qps if pr == 0 else kps
                            for dqt in range(DT):
                                iq = (ic * 2 + pr) * 8 + dqt
                                ve.wait_ge(qm.h, L[qm] + iq + 1)
                                if iq >= 1:
                                    ve.wait_ge(fm.h, L[fm] + iq)
                                qv.inc(ve.tensor_copy(rawb[:], PQ[:]))
                                ve.wait_ge(fm.h, L[fm] + iq + 1)
                                fv.inc(ve.tensor_single_scalar(t5[:], PF[:], 0.0, ALU.min))
                                ve.wait_ge(fq.h, L[fq] + iq + 1)
                                fz.inc(ve.tensor_tensor(dst[:, dqt * 512:(dqt + 1) * 512],
                                                        t6[:], t7[:], ALU.add))
                        for ul in range(4):
                            for dvh in range(2):
                                iv = ic * 8 + ul * 2 + dvh
                                ve.wait_ge(vm.h, L[vm] + iv + 1)
                                vv.inc(ve.tensor_copy(vps[:, ul * 1024 + dvh * 512: ul * 1024 + (dvh + 1) * 512],
                                                      PV[:]))
                        for h in range(16):
                            for ul in range(4):
                                ia = (ic * 16 + h) * 4 + ul
                                u_abs = ch * 4 + ul
                                ve.wait_ge(am_.h, L[am_] + ia + 1)
                                av.inc(ve.tensor_tensor(amb[:], PA[:], ctx.cstb[:, B_TRI:B_TRI + 128],
                                                        ALU.mult))
                                ve.wait_ge(km.h, L[km] + ia + 1)
                                kv_.inc(ve.tensor_copy(knb[:], PTT[:, 0:64]))
                                ve.wait_ge(nm.h, L[nm] + ia + 1)
                                vsync(ctx, ve, ve.tensor_scalar(dnc[:, 0:1], PND[:, 64:65], 1e-6,
                                                                None, ALU.add))
                                vsync(ctx, ve, ve.reciprocal(dnc[:, 1:2], dnc[:, 0:1]))
                                nv.inc(ve.tensor_scalar(aps[:, ul * 1024 + h * 64: ul * 1024 + (h + 1) * 64],
                                                        PND[:, 0:64], dnc[:, 1:2], None, ALU.mult))
                                ve.wait_ge(um.h, L[um] + ia + 1)
                                hr2 = (h % 2) * 64
                                hcc = (h // 2) * 65
                                if u_abs == 0:
                                    vsync(ctx, ve, ve.tensor_copy(kvf[hr2:hr2 + 64, hcc:hcc + 65],
                                                                  PKV[hr2:hr2 + 64, :]))
                                else:
                                    vsync(ctx, ve, ve.tensor_tensor(kvf[hr2:hr2 + 64, hcc:hcc + 65],
                                                                    kvf[hr2:hr2 + 64, hcc:hcc + 65],
                                                                    PKV[hr2:hr2 + 64, :], ALU.add))
                                uv.inc(ve.tensor_copy(kvb[hr2:hr2 + 64, hcc:hcc + 65],
                                                      kvf[hr2:hr2 + 64, hcc:hcc + 65]))
                        for ul in range(4):
                            for dvt in range(DT):
                                it = (ic * 4 + ul) * 8 + dvt
                                ve.wait_ge(tm.h, L[tm] + it + 1)
                                tv_.inc(ve.tensor_copy(atT[:, dvt * 128:(dvt + 1) * 128], PTT[:, 64:192]))
                            for dto in range(DT):
                                iw = (ic * 4 + ul) * 8 + dto
                                ve.wait_ge(wm.h, L[wm] + iw + 1)
                                lxr.wait_item(ve, iw)
                                if iw >= 2:
                                    lst.wait_item(ve, iw - 2)
                                wv_.inc(ve.tensor_tensor(ot[iw % 2][:], PW[:], xt[iw % 2][:], ALU.add))
            blk.vector(f_vector)

            def f_scalar(se):
                for b in range(2):
                    for ch in range(4):
                        ic = b * 4 + ch
                        for pr in range(2):
                            for dqt in range(DT):
                                iq = (ic * 2 + pr) * 8 + dqt
                                se.wait_ge(fv.h, L[fv] + iq + 1)
                                se.activation(t7[:], PF[:], AF.Relu)
                                fq.inc(se.activation(t6[:], t5[:], AF.Exp))
            blk.scalar(f_scalar)


def emit_mamba(ctx, dr, xn_scr, xn1_scr, x_src, x_dst, scr):
    with ExitStack() as ms:
        ctx.BCs = ms.enter_context(ctx.nc.sbuf_tensor(U("g_BCs"), [32, 4096], BF16))
        ctx.selb = ms.enter_context(ctx.nc.sbuf_tensor(U("g_selb"), [32, 4096], BF16))
        emit_mamba_m1(ctx, dr, xn_scr, scr)
        emit_mamba_m2(ctx, dr, scr)
        emit_mamba_m3(ctx, dr, scr)
        emit_mamba_m45(ctx, dr, xn1_scr, x_src, x_dst, scr)


def emit_mamba_m1(ctx, dr, xn_scr, scr):
    ctx.new_phase()
    nc = ctx.nc
    cst = ctx.cst
    m1x = Stream(ctx, "m1_x", 4)
    m1w = Stream(ctx, "m1_w", 4, kind="gp")
    m1s = Stream(ctx, "m1_s", 2, kind="gp")
    mm1 = ctx.S("m1_mm"); mv1 = ctx.S("m1_ve")
    L = {mm1: mm1.n, mv1: mv1.n}
    with ExitStack() as ps:
        xnfull = ps.enter_context(nc.sbuf_tensor(U("m1_xn"), [128, 32768], BF16))
        wt = [ps.enter_context(nc.sbuf_tensor(U(f"m1_w{i}"), [128, 1024], BF16)) for i in range(2)]
        evb = [ps.enter_context(nc.sbuf_tensor(U(f"m1_e{i}"), [128, 4096], BF16)) for i in range(2)]
        PM = ps.enter_context(nc.psum_tensor(U("m1_pm"), [128, 4096], F32))
        with nc.Block() as blk:
            def f_sync(sy):
                for dt in range(DT):
                    m1x.put(sy, xnfull[:, dt * 4096:(dt + 1) * 4096], xn_scr[dt, :, :])
            blk.sync(f_sync)

            def f_gp(gp):
                for ct in range(32):
                    if ct >= 2:
                        gp.wait_ge(mm1.h, L[mm1] + ct - 1)
                    for hh in range(2):
                        m1w.put(gp, wt[ct % 2][:, hh * 512:(hh + 1) * 512].rearrange(
                                    "p (j c) -> p j c", j=4, c=128),
                                dr["ipjt"][hh * 4:(hh + 1) * 4, :, ct * 128:(ct + 1) * 128].rearrange(
                                    "j p c -> p j c"))
                    if ct >= 1:
                        cts = ct - 1
                        gp.wait_ge(mv1.h, L[mv1] + cts + 1)
                        dst = scr["xc_scr"][cts, :, :] if cts < 16 else scr["z_scr"][cts - 16, :, :]
                        m1s.put(gp, dst, evb[cts % 2][:])
                gp.wait_ge(mv1.h, L[mv1] + 32)
                m1s.put(gp, scr["z_scr"][15, :, :], evb[31 % 2][:])
                m1s.wait_all(gp)
            blk.gpsimd(f_gp)

            def f_tensor(te):
                m1x.wait_all(te)
                for ct in range(32):
                    m1w.wait_item(te, 2 * ct)
                    m1w.wait_item(te, 2 * ct + 1)
                    if ct >= 1:
                        te.wait_ge(mv1.h, L[mv1] + ct)
                    ins = None
                    for dt in range(DT):
                        for ch in range(NCHK):
                            ins = te.matmul(PM[:, ch * 512:(ch + 1) * 512],
                                            wt[ct % 2][:, dt * 128:(dt + 1) * 128],
                                            xnfull[:, dt * 4096 + ch * 512: dt * 4096 + (ch + 1) * 512],
                                            start=(dt == 0), stop=(dt == DT - 1))
                    mm1.inc(ins)
            blk.tensor(f_tensor)

            def f_vector(ve):
                for ct in range(32):
                    ve.wait_ge(mm1.h, L[mm1] + ct + 1)
                    if ct >= 2:
                        m1s.wait_item(ve, ct - 2)
                    mv1.inc(ve.tensor_scalar(evb[ct % 2][:], PM[:], col(cst, C_IPB + ct), None, ALU.add))
            blk.vector(f_vector)


def emit_mamba_m2(ctx, dr, scr):
    ctx.new_phase()
    nc = ctx.nc
    cst = ctx.cst
    m2x = Stream(ctx, "m2_x", 2)
    m2s = Stream(ctx, "m2_s", 2, kind="gp")
    cv = ctx.S("m2_ve"); cq = ctx.S("m2_se")
    L2 = {cv: cv.n, cq: cq.n}
    with ExitStack() as ps:
        xcb = [ps.enter_context(nc.sbuf_tensor(U(f"m2_x{i}"), [128, 2048], BF16)) for i in range(2)]
        aca = ps.enter_context(nc.sbuf_tensor(U("m2_a"), [128, 2048], F32))
        acb = ps.enter_context(nc.sbuf_tensor(U("m2_b"), [128, 2048], F32))
        ub = [ps.enter_context(nc.sbuf_tensor(U(f"m2_u{i}"), [128, 2048], BF16)) for i in range(2)]
        with nc.Block() as blk:
            def f_sync(sy):
                for i in range(32):
                    b, ct = divmod(i, 16)
                    if i >= 2:
                        sy.wait_ge(cv.h, L2[cv] + i - 1)
                    m2x.put(sy, xcb[i % 2][:], scr["xc_scr"][ct, :, b * 2048:(b + 1) * 2048])
            blk.sync(f_sync)

            def f_vector(ve):
                for i in range(32):
                    b, ct = divmod(i, 16)
                    m2x.wait_item(ve, i)
                    if i >= 1:
                        ve.wait_ge(cq.h, L2[cq] + i)
                    w = [col(cst, C_CONVW + ct * 4 + j) for j in range(4)]
                    bcl = col(cst, C_CONVB + ct)
                    xc = xcb[i % 2]
                    ve.tensor_scalar(aca[:, 3:2048], xc[:, 0:2045], w[0], bcl, ALU.mult, ALU.add)
                    ve.scalar_tensor_tensor(acb[:, 3:2048], xc[:, 1:2046], w[1], aca[:, 3:2048], ALU.mult, ALU.add)
                    ve.tensor_scalar(aca[:, 3:2048], xc[:, 2:2047], w[2], None, ALU.mult)
                    ve.tensor_tensor(acb[:, 3:2048], acb[:, 3:2048], aca[:, 3:2048], ALU.add)
                    ve.scalar_tensor_tensor(acb[:, 3:2048], xc[:, 3:2048], w[3], acb[:, 3:2048], ALU.mult, ALU.add)
                    vsync(ctx, ve, ve.tensor_scalar(acb[:, 0:1], xc[:, 0:1], w[3], bcl, ALU.mult, ALU.add))
                    vsync(ctx, ve, ve.tensor_scalar(acb[:, 1:2], xc[:, 1:2], w[3], bcl, ALU.mult, ALU.add))
                    vsync(ctx, ve, ve.scalar_tensor_tensor(acb[:, 1:2], xc[:, 0:1], w[2], acb[:, 1:2], ALU.mult, ALU.add))
                    vsync(ctx, ve, ve.tensor_scalar(acb[:, 2:3], xc[:, 2:3], w[3], bcl, ALU.mult, ALU.add))
                    vsync(ctx, ve, ve.scalar_tensor_tensor(acb[:, 2:3], xc[:, 1:2], w[2], acb[:, 2:3], ALU.mult, ALU.add))
                    vsync(ctx, ve, ve.scalar_tensor_tensor(acb[:, 2:3], xc[:, 0:1], w[1], acb[:, 2:3], ALU.mult, ALU.add))
                    cv.inc(ve.engine_nop())
            blk.vector(f_vector)

            def f_scalar(se):
                for i in range(32):
                    se.wait_ge(cv.h, L2[cv] + i + 1)
                    if i >= 2:
                        m2s.wait_item(se, i - 2)
                    cq.inc(se.activation(ub[i % 2][:], acb[:], AF.Silu))
            blk.scalar(f_scalar)

            def f_gp(gp):
                for i in range(32):
                    b, ct = divmod(i, 16)
                    gp.wait_ge(cq.h, L2[cq] + i + 1)
                    m2s.put(gp, scr["u_scr"][ct, :, b * 2048:(b + 1) * 2048], ub[i % 2][:])
                m2s.wait_all(gp)
            blk.gpsimd(f_gp)


def emit_mamba_m3(ctx, dr, scr):
    ctx.new_phase()
    nc = ctx.nc
    cst = ctx.cst
    m3u = Stream(ctx, "m3_u", 4)
    m3w = Stream(ctx, "m3_w", 8, kind="gp")
    m3s = Stream(ctx, "m3_s", 2)
    m3wx = Stream(ctx, "m3_wx", 4, kind="gp")
    tmm = ctx.S("m3_mm"); tve = ctx.S("m3_ve"); xmm = ctx.S("m3_xm"); xve = ctx.S("m3_xv")
    L3 = {tmm: tmm.n, tve: tve.n, xmm: xmm.n, xve: xve.n}
    with ExitStack() as ps:
        ubf = ps.enter_context(nc.sbuf_tensor(U("m3_u"), [128, 32768], BF16))
        xpj = ps.enter_context(nc.sbuf_tensor(U("m3_xp"), [128, 512], BF16))
        dtw = [ps.enter_context(nc.sbuf_tensor(U(f"m3_w{i}"), [128, 2048], BF16)) for i in range(2)]
        ta2 = ps.enter_context(nc.sbuf_tensor(U("m3_t1"), [128, 2048], F32))
        tng = ps.enter_context(nc.sbuf_tensor(U("m3_t4"), [128, 2048], F32))
        tr = ps.enter_context(nc.sbuf_tensor(U("m3_t2"), [128, 2048], F32))
        te_ = ps.enter_context(nc.sbuf_tensor(U("m3_t3"), [128, 2048], F32))
        dlb = [ps.enter_context(nc.sbuf_tensor(U(f"m3_d{i}"), [128, 2048], BF16)) for i in range(2)]
        PX = ps.enter_context(nc.psum_tensor(U("m3_px"), [32, 2048], F32))
        PD = ps.enter_context(nc.psum_tensor(U("m3_pd"), [128, 2048], F32))
        with nc.Block() as blk:
            def f_gp(gp):
                for hh in range(4 if M3LVL >= 1 else 0):
                    m3wx.put(gp, xpj[:, hh * 128:(hh + 1) * 128].rearrange(
                                 "p (j c) -> p j c", j=4, c=32),
                             dr["xpjt"][hh * 4:(hh + 1) * 4, :, :].rearrange("j p c -> p j c"))
                for b in range(2):
                    for cot in range(M3N if M3LVL >= 2 else 0):
                        k = b * M3N + cot
                        if k >= 2:
                            gp.wait_ge(tmm.h, L3[tmm] + k - 1)
                        for hh in range(4):
                            m3w.put(gp, dtw[k % 2][:, hh * 512:(hh + 1) * 512].rearrange(
                                        "p (j c) -> p j c", j=4, c=128),
                                    dr["dtjt"][hh * 4:(hh + 1) * 4, :, cot * 128:(cot + 1) * 128].rearrange(
                                        "j p c -> p j c"))
            blk.gpsimd(f_gp)

            def f_sync(sy):
                for b in range(2):
                    if b == 1:
                        sy.wait_ge(tmm.h, L3[tmm] + M3N)
                    for ct in range(CIT):
                        m3u.put(sy, ubf[:, ct * 2048:(ct + 1) * 2048],
                                scr["u_scr"][ct, :, b * 2048:(b + 1) * 2048])
                    for cot in range(M3N if M3LVL >= 2 else 0):
                        k = b * M3N + cot
                        sy.wait_ge(tve.h, L3[tve] + k + 1)
                        m3s.put(sy, scr["dl_scr"][cot, :, b * 2048:(b + 1) * 2048], dlb[k % 2][:])
                m3s.wait_all(sy)
            blk.sync(f_sync)

            va = ctx.S("m3_va")
            sr = ctx.S("m3_sr")
            sl = ctx.S("m3_sl")
            L3b = {va: va.n, sr: sr.n, sl: sl.n}

            def f_tensor(te):
                m3wx.wait_all(te)
                for b in range(2):
                    m3u.wait_upto(te, (b + 1) * CIT - 1)
                    if b == 1 and M3LVL >= 1:
                        te.wait_ge(xve.h, L3[xve] + 1)
                    ins = None
                    for q in range(4 if M3LVL >= 1 else 0):
                        for cit in range(CIT):
                            ins = te.matmul(PX[:, q * 512:(q + 1) * 512],
                                            xpj[:, cit * 32:(cit + 1) * 32],
                                            ubf[:, cit * 2048 + q * 512: cit * 2048 + (q + 1) * 512],
                                            start=(cit == 0), stop=(cit == CIT - 1))
                    if M3LVL >= 1:
                        xmm.inc(ins)
                    for cot in range(M3N if M3LVL >= 2 else 0):
                        k = b * M3N + cot
                        for hh in range(4):
                            m3w.wait_item(te, 4 * k + hh)
                        if k >= 1:
                            te.wait_ge(va.h, L3b[va] + k)
                            te.wait_ge(sr.h, L3b[sr] + k)
                        ins = None
                        for q in range(4):
                            for cit in range(CIT):
                                ins = te.matmul(PD[:, q * 512:(q + 1) * 512],
                                                dtw[k % 2][:, cit * 128:(cit + 1) * 128],
                                                ubf[:, cit * 2048 + q * 512: cit * 2048 + (q + 1) * 512],
                                                start=(cit == 0), stop=(cit == CIT - 1))
                        tmm.inc(ins)
            blk.tensor(f_tensor)

            def f_vector(ve):
                for b in range(2):
                    if M3LVL >= 1:
                        ve.wait_ge(xmm.h, L3[xmm] + b + 1)
                        xve.inc(ve.tensor_scalar(ctx.BCs[0:32, b * 2048:(b + 1) * 2048], PX[:],
                                                 cst[0:32, C_XPB:C_XPB + 1], None, ALU.add))
                    for cot in range(M3N if M3LVL >= 2 else 0):
                        k = b * M3N + cot
                        ve.wait_ge(tmm.h, L3[tmm] + k + 1)
                        if M3LVL == 2:
                            if k >= 2:
                                m3s.wait_item(ve, k - 2)
                            va.inc(ve.engine_nop())
                            sr.inc(ve.engine_nop())
                            sl.inc(ve.engine_nop())
                            tve.inc(ve.tensor_scalar(dlb[k % 2][:], PD[:],
                                                     col(cst, C_DTB + cot), None, ALU.add))
                            continue
                        if k >= 1:
                            ve.wait_ge(sl.h, L3b[sl] + k)
                        ve.tensor_scalar(ta2[:], PD[:], col(cst, C_DTB + cot), None, ALU.add)
                        ve.tensor_scalar(tng[:], ta2[:], -1.0, None, ALU.mult)
                        va.inc(ve.tensor_tensor(ta2[:], ta2[:], tng[:], ALU.max))
                        ve.wait_ge(sl.h, L3b[sl] + k + 1)
                        if k >= 2:
                            m3s.wait_item(ve, k - 2)
                        tve.inc(ve.tensor_tensor(dlb[k % 2][:], te_[:], tr[:], ALU.add))
            blk.vector(f_vector)

            def f_scalar(se):
                for b in range(2):
                    for cot in range(M3N if M3LVL >= 3 else 0):
                        k = b * M3N + cot
                        se.wait_ge(va.h, L3b[va] + k + 1)
                        if k >= 1:
                            se.wait_ge(tve.h, L3[tve] + k)
                        se.activation(tr[:], PD[:], AF.Relu, bias=col(cst, C_DTB + cot))
                        sr.inc(se.activation(te_[:], ta2[:], AF.Exp, scale=col(cst, C_MONE)))
                        sl.inc(se.activation(te_[:], te_[:], AF.Ln, bias=col(cst, C_ONES)))
            blk.scalar(f_scalar)


def emit_mamba_m45(ctx, dr, xn1_scr, x_src, x_dst, scr):
    ctx.new_phase()
    nc = ctx.nc
    cst = ctx.cst
    selb = ctx.selb
    m4sel = Stream(ctx, "m4_sel", 1, kind="gp")
    with nc.Block() as blk:
        def f_gp(gp):
            m4sel.put(gp, selb[:], dr["sel32"][:, :])
        blk.gpsimd(f_gp)

    m4d = Stream(ctx, "m4_d", 2)
    m4u = Stream(ctx, "m4_u", 2)
    m4z = Stream(ctx, "m4_z", 2)
    m4x = Stream(ctx, "m4_x", 2)
    m4n = Stream(ctx, "m4_n", 2)
    m4st = Stream(ctx, "m4_st", 2, kind="gp")
    m4w = Stream(ctx, "m4_w", 1, kind="gp")
    bmmB = ctx.S("m4_bb")
    bmmC = ctx.S("m4_bc")
    ptb = ctx.S("m4_pt")
    sda = ctx.S("m4_da")
    sev = ctx.S("m4_ev")
    sevB = ctx.S("m4_eb")
    vsn = ctx.S("m4_vs")
    vtm = ctx.S("m4_tm")
    vdu = ctx.S("m4_du")
    vyq = ctx.S("m4_yq")
    sz1 = ctx.S("m4_sz")
    omm = ctx.S("m5_mm")
    ove = ctx.S("m5_ve")
    L4 = {s: s.n for s in (bmmB, bmmC, ptb, sda, sev, sevB, vsn, vtm, vdu, vyq, sz1, omm, ove)}
    with ExitStack() as ps:
        ystash = ps.enter_context(nc.sbuf_tensor(U("m4_y"), [128, 32768], BF16))
        with nc.Block() as blk:
            pass
        for b in range(2):
            with ExitStack() as p4:
                dlt = [p4.enter_context(nc.sbuf_tensor(U(f"m4_d{i}"), [128, 2048], BF16)) for i in range(2)]
                utt = [p4.enter_context(nc.sbuf_tensor(U(f"m4_u{i}"), [128, 2048], BF16)) for i in range(2)]
                dub2 = [p4.enter_context(nc.sbuf_tensor(U(f"m4_du{i}"), [128, 2048], BF16)) for i in range(2)]
                dAr = [p4.enter_context(nc.sbuf_tensor(U(f"m4_dA{i}"), [128, 2048], BF16)) for i in range(2)]
                tbr = [p4.enter_context(nc.sbuf_tensor(U(f"m4_t2{i}"), [128, 2048], BF16)) for i in range(2)]
                pCb = [p4.enter_context(nc.sbuf_tensor(U(f"m4_pc{i}"), [128, 2048], BF16)) for i in range(2)]
                pBb = [p4.enter_context(nc.sbuf_tensor(U(f"m4_pb{i}"), [128, 2048], BF16)) for i in range(2)]
                hb = p4.enter_context(nc.sbuf_tensor(U("m4_h"), [128, 2048], BF16))
                tm2 = p4.enter_context(nc.sbuf_tensor(U("m4_tm"), [128, 2048], BF16))
                ysb = p4.enter_context(nc.sbuf_tensor(U("m4_ys"), [128, 2048], F32))
                ztb = [p4.enter_context(nc.sbuf_tensor(U(f"m4_z{i}"), [128, 2048], BF16)) for i in range(2)]
                szb = p4.enter_context(nc.sbuf_tensor(U("m4_sz"), [128, 2048], BF16))
                tt3 = p4.enter_context(nc.sbuf_tensor(U("m4_t3"), [128, 2048], BF16))
                pB = p4.enter_context(nc.psum_tensor(U("m4_pB"), [128, 2048], F32))
                pC = p4.enter_context(nc.psum_tensor(U("m4_pC"), [128, 2048], F32))
                with nc.Block() as blk:
                    def f_sync(sy):
                        for dt in range(CIT):
                            i = b * CIT + dt
                            if dt >= 2:
                                sy.wait_ge(sda.h, L4[sda] + (i - 1) * 16)
                                sy.wait_ge(vdu.h, L4[vdu] + i - 1)
                            m4d.put(sy, dlt[dt % 2][:], scr["dl_scr"][dt, :, b * 2048:(b + 1) * 2048])
                            if dt >= 2:
                                sy.wait_ge(vyq.h, L4[vyq] + i - 1)
                            m4u.put(sy, utt[dt % 2][:], scr["u_scr"][dt, :, b * 2048:(b + 1) * 2048])
                            if dt >= 2:
                                sy.wait_ge(sz1.h, L4[sz1] + i - 1)
                            m4z.put(sy, ztb[dt % 2][:], scr["z_scr"][dt, :, b * 2048:(b + 1) * 2048])
                    blk.sync(f_sync)

                    def f_tensor(te):
                        m4sel.wait_all(te)
                        for dt in range(CIT):
                            i = b * CIT + dt
                            for s in range(16):
                                si = i * 16 + s
                                if si >= 1:
                                    te.wait_ge(sevB.h, L4[sevB] + si)
                                ins = None
                                for q in range(4):
                                    ins = te.matmul(pB[:, q * 512:(q + 1) * 512],
                                                    selb[0:32, s * 128:(s + 1) * 128],
                                                    ctx.BCs[0:32, b * 2048 + q * 512: b * 2048 + (q + 1) * 512],
                                                    start=True, stop=True)
                                bmmB.inc(ins)
                                if si >= 1:
                                    te.wait_ge(sev.h, L4[sev] + si)
                                for q in range(4):
                                    ins = te.matmul(pC[:, q * 512:(q + 1) * 512],
                                                    selb[0:32, (16 + s) * 128:(17 + s) * 128],
                                                    ctx.BCs[0:32, b * 2048 + q * 512: b * 2048 + (q + 1) * 512],
                                                    start=True, stop=True)
                                bmmC.inc(ins)
                    blk.tensor(f_tensor)

                    def f_pool(gp):
                        for dt in range(CIT):
                            i = b * CIT + dt
                            gp.wait_ge(vdu.h, L4[vdu] + i + 1)
                            for s in range(16):
                                si = i * 16 + s
                                gp.wait_ge(sevB.h, L4[sevB] + si + 1)
                                if si >= 2:
                                    gp.wait_ge(vsn.h, L4[vsn] + si - 1)
                                ptb.inc(gp.tensor_tensor(tbr[si % 2][:], dub2[i % 2][:],
                                                         pBb[si % 2][:], ALU.mult))
                    blk.gpsimd(f_pool)

                    def f_scalar(se):
                        for dt in range(CIT):
                            i = b * CIT + dt
                            m4d.wait_item(se, i)
                            for s in range(16):
                                si = i * 16 + s
                                if si >= 2:
                                    se.wait_ge(vsn.h, L4[vsn] + si - 1)
                                sda.inc(se.activation(dAr[si % 2][:], dlt[dt % 2][:], AF.Exp,
                                                      scale=col(cst, C_A + dt * 16 + s)))
                                se.wait_ge(bmmB.h, L4[bmmB] + si + 1)
                                if si >= 2:
                                    se.wait_ge(ptb.h, L4[ptb] + si - 1)
                                sevB.inc(se.activation(pBb[si % 2][:], pB[:], AF.Copy))
                                se.wait_ge(bmmC.h, L4[bmmC] + si + 1)
                                if si >= 2:
                                    se.wait_ge(vtm.h, L4[vtm] + si - 1)
                                sev.inc(se.activation(pCb[si % 2][:], pC[:], AF.Copy))
                            m4z.wait_item(se, i)
                            se.wait_ge(vyq.h, L4[vyq] + i)
                            sz1.inc(se.activation(szb[:], ztb[dt % 2][:], AF.Silu))
                    blk.scalar(f_scalar)

                    def f_vector(ve):
                        for dt in range(CIT):
                            i = b * CIT + dt
                            m4d.wait_item(ve, i)
                            m4u.wait_item(ve, i)
                            if dt >= 2:
                                ve.wait_ge(ptb.h, L4[ptb] + (i - 1) * 16)
                            vdu.inc(ve.tensor_tensor(dub2[i % 2][:], dlt[dt % 2][:],
                                                     utt[dt % 2][:], ALU.mult))
                            for s in range(16):
                                si = i * 16 + s
                                ve.wait_ge(ptb.h, L4[ptb] + si + 1)
                                ve.wait_ge(sda.h, L4[sda] + si + 1)
                                vsn.inc(ve.tensor_tensor_scan(hb[:], dAr[si % 2][:],
                                                              tbr[si % 2][:], 0.0,
                                                              ALU.mult, ALU.add))
                                ve.wait_ge(sev.h, L4[sev] + si + 1)
                                if s == 0:
                                    ins = ve.tensor_tensor(ysb[:], hb[:], pCb[si % 2][:], ALU.mult)
                                else:
                                    ve.tensor_tensor(tm2[:], hb[:], pCb[si % 2][:], ALU.mult)
                                    ins = ve.tensor_tensor(ysb[:], ysb[:], tm2[:], ALU.add)
                                vtm.inc(ins)
                            ve.wait_ge(sz1.h, L4[sz1] + i + 1)
                            ve.scalar_tensor_tensor(tt3[:], utt[dt % 2][:], col(cst, C_D + dt),
                                                    ysb[:], ALU.mult, ALU.add)
                            vyq.inc(ve.tensor_tensor(ystash[:, dt * 2048:(dt + 1) * 2048],
                                                     tt3[:], szb[:], ALU.mult))
                    blk.vector(f_vector)

            with ExitStack() as p5:
                opw = p5.enter_context(nc.sbuf_tensor(U("m5_w"), [128, 16384], BF16))
                xt5 = [p5.enter_context(nc.sbuf_tensor(U(f"m5_x{i}"), [128, 2048], F32)) for i in range(2)]
                nt5 = [p5.enter_context(nc.sbuf_tensor(U(f"m5_n{i}"), [128, 2048], BF16)) for i in range(2)]
                ov5 = [p5.enter_context(nc.sbuf_tensor(U(f"m5_o{i}"), [128, 2048], F32)) for i in range(2)]
                tv5 = p5.enter_context(nc.sbuf_tensor(U("m5_t"), [128, 2048], F32))
                PO = p5.enter_context(nc.psum_tensor(U("m5_po"), [128, 2048], F32))
                with nc.Block() as blk:
                    def f_gp(gp):
                        for h in range(4):
                            m4w.put(gp, opw[:, h * 4096:(h + 1) * 4096].rearrange(
                                        "p (j c) -> p j c", j=4, c=1024),
                                    dr["opjt"][h * 4:(h + 1) * 4, :, :].rearrange("j p c -> p j c"))
                        for dto in range(DT):
                            i5 = b * DT + dto
                            gp.wait_ge(ove.h, L4[ove] + i5 + 1)
                            m4st.put(gp, x_dst[dto, :, b * 2048:(b + 1) * 2048], ov5[dto % 2][:])
                        m4st.wait_all(gp)
                    blk.gpsimd(f_gp)

                    def f_sync(sy):
                        for dto in range(DT):
                            i5 = b * DT + dto
                            if dto >= 2:
                                sy.wait_ge(ove.h, L4[ove] + i5 - 1)
                            m4x.put(sy, xt5[dto % 2][:], x_src[dto, :, b * 2048:(b + 1) * 2048])
                            m4n.put(sy, nt5[dto % 2][:], xn1_scr[dto, :, b * 2048:(b + 1) * 2048])
                    blk.sync(f_sync)

                    def f_tensor(te):
                        m4w.wait_all(te)
                        for dto in range(DT):
                            i5 = b * DT + dto
                            if dto >= 1:
                                te.wait_ge(ove.h, L4[ove] + i5)
                            ins = None
                            for q in range(4):
                                for cit in range(CIT):
                                    ins = te.matmul(PO[:, q * 512:(q + 1) * 512],
                                                    opw[:, cit * 1024 + dto * 128: cit * 1024 + (dto + 1) * 128],
                                                    ystash[:, cit * 2048 + q * 512: cit * 2048 + (q + 1) * 512],
                                                    start=(cit == 0), stop=(cit == CIT - 1))
                            omm.inc(ins)
                    blk.tensor(f_tensor)

                    def f_vector(ve):
                        for dto in range(DT):
                            i5 = b * DT + dto
                            ve.wait_ge(omm.h, L4[omm] + i5 + 1)
                            m4x.wait_item(ve, i5)
                            m4n.wait_item(ve, i5)
                            if dto >= 2:
                                m4st.wait_item(ve, i5 - 2)
                            ve.tensor_scalar(tv5[:], PO[:], col(cst, C_OPB + dto), None, ALU.add)
                            ve.tensor_tensor(tv5[:], tv5[:], nt5[dto % 2][:], ALU.add)
                            ove.inc(ve.tensor_tensor(ov5[dto % 2][:], tv5[:], xt5[dto % 2][:], ALU.add))
                    blk.vector(f_vector)



